# version 04
# baseline (speedup 1.0000x reference)
import contextlib
import math
import numpy as np

import concourse.bass as bass
import concourse.mybir as mybir
from concourse.bass_utils import run_bass_kernel_spmd

F32 = mybir.dt.float32
BF16 = mybir.dt.bfloat16
AF = mybir.ActivationFunctionType
OP = mybir.AluOpType
AX = mybir.AxisListType

NEG = -1e30


ENGINES = ("pe", "dve", "act", "pool", "sp")
DMA_QUEUES = ("qsp", "qact", "qpool")
QUEUE_ENGINE = {"qsp": "sp", "qact": "act", "qpool": "pool"}
QUEUE_TYPE = {"qsp": "hw", "qact": "hw", "qpool": "sw"}
N_HW_DMA_SEMS = 40
N_SW_DMA_SEMS = 14
DMA_KEYS = tuple(f"dh{i}" for i in range(N_HW_DMA_SEMS)) + tuple(
    f"dsw{i}" for i in range(N_SW_DMA_SEMS))


class Buf:
    __slots__ = ("name", "ap_fn", "last_w", "readers", "dma_key", "dma_qt")

    def __init__(self, name, ap_fn):
        self.name = name
        self.ap_fn = ap_fn
        self.last_w = None
        self.readers = []
        self.dma_key = None
        self.dma_qt = None

    def ap(self):
        return self.ap_fn()


class Ctx:
    def __init__(self, nc):
        self.nc = nc
        self.streams = {e: [] for e in ENGINES}
        self.cnt = {}
        self.sems = {}
        self.waited = {e: {} for e in ENGINES}
        self._exit = contextlib.ExitStack()
        for e in ENGINES:
            self.cnt[e] = 0
        for q in DMA_QUEUES:
            self.cnt[q] = 0
        self._sem_names = list(ENGINES) + list(DMA_KEYS)
        for k in DMA_KEYS:
            self.cnt[k] = 0
        self._free_dma = {"hw": [f"dh{i}" for i in range(N_HW_DMA_SEMS)],
                          "sw": [f"dsw{i}" for i in range(N_SW_DMA_SEMS)]}
        self.bufs = []
        self._scopes = [self._exit]
        self._scope_bufs = [[]]

    def alloc(self):
        for s in self._sem_names:
            self.sems[s] = self._exit.enter_context(self.nc.semaphore("sem_" + s))

    @contextlib.contextmanager
    def scope(self):
        st = contextlib.ExitStack()
        self._scopes.append(st)
        self._scope_bufs.append([])
        try:
            yield
        finally:
            self.barrier_all()
            self._scopes.pop()
            for b in self._scope_bufs.pop():
                if b.dma_key is not None:
                    self._free_dma[b.dma_qt].append(b.dma_key)
                    b.dma_key = None
            st.close()

    def sbuf_root(self, name, shape, dtype):
        self._uid = getattr(self, "_uid", 0) + 1
        name = f"{name}_u{self._uid}"
        t = self._scopes[0].enter_context(self.nc.sbuf_tensor(name, shape, dtype))
        b = Buf(name, lambda t=t: t)
        self.bufs.append(b)
        return b

    def sbuf(self, name, shape, dtype):
        self._uid = getattr(self, "_uid", 0) + 1
        name = f"{name}_u{self._uid}"
        t = self._scopes[-1].enter_context(self.nc.sbuf_tensor(name, shape, dtype))
        b = Buf(name, lambda t=t: t)
        self.bufs.append(b)
        self._scope_bufs[-1].append(b)
        return b

    def psum(self, name, shape, dtype=F32):
        t = self._exit.enter_context(self.nc.psum_tensor(name, shape, dtype))
        b = Buf(name, lambda t=t: t)
        self.bufs.append(b)
        return b

    def dram(self, name, shape, dtype, kind="Internal"):
        t = self.nc.dram_tensor(name, shape, dtype, kind=kind)
        b = Buf(name, lambda t=t: t.ap())
        self.bufs.append(b)
        return b

    def _waits_for(self, reads, writes):
        toks = []
        for b in reads:
            if b.last_w is not None:
                toks.append(b.last_w)
        for b in writes:
            if b.last_w is not None:
                toks.append(b.last_w)
            toks.extend(b.readers)
        best = {}
        for (k, c) in toks:
            if best.get(k, -1) < c:
                best[k] = c
        return best

    def _commit(self, tok, reads, writes):
        for b in reads:
            b.readers.append(tok)
        for b in writes:
            b.last_w = tok
            b.readers = []

    def op(self, engine, fn, reads=(), writes=(), inc=True, commit=True):
        best = self._waits_for(reads, writes)
        waited = self.waited[engine]
        needed = [(k, c) for (k, c) in best.items() if waited.get(k, 0) < c]
        for (k, c) in needed:
            waited[k] = c
        if inc:
            self.cnt[engine] += 1
        tok = (engine, self.cnt[engine])
        sems = self.sems

        def emit(eng, needed=tuple(needed), fn=fn, inc=inc, engine=engine):
            for (k, c) in needed:
                eng.wait_ge(sems[k], c * 16 if k in DMA_KEYS else c)
            instr = fn(eng)
            if inc:
                instr.then_inc(sems[engine], 1)
        self.streams[engine].append(emit)
        if commit:
            assert inc, "op(commit=True) requires inc=True"
            self._commit(tok, reads, writes)
        return tok

    def commit(self, tok, reads=(), writes=()):
        self._commit(tok, reads, writes)

    def _dma_key_for(self, buf, qt):
        if buf.dma_key is None:
            assert self._free_dma[qt], f"out of {qt} dma sems"
            buf.dma_key = self._free_dma[qt].pop()
            buf.dma_qt = qt
        assert buf.dma_qt == qt, f"{buf.name} used from both queue types"
        return buf.dma_key

    def dma(self, queue, out_ap, in_ap, reads=(), writes=(), **kw):
        engine = QUEUE_ENGINE[queue]
        key = self._dma_key_for(writes[0], QUEUE_TYPE[queue])
        best = self._waits_for(reads, writes)
        waited = self.waited[engine]
        needed = [(k, c) for (k, c) in best.items() if waited.get(k, 0) < c]
        for (k, c) in needed:
            waited[k] = c
        self.cnt[key] += 1
        tok = (key, self.cnt[key])
        sems = self.sems

        def emit(eng, needed=tuple(needed), key=key):
            for (k, c) in needed:
                eng.wait_ge(sems[k], c * 16 if k in DMA_KEYS else c)
            eng.dma_start(out=out_ap, in_=in_ap, **kw).then_inc(sems[key], 16)
        self.streams[engine].append(emit)
        self._commit(tok, reads, writes)
        return tok

    def raw(self, engine, fn):
        self.streams[engine].append(fn)

    def barrier_all(self, engines=ENGINES):
        snapshot = dict(self.cnt)
        sems = self.sems
        for e in engines:
            waited = self.waited[e]
            needed = []
            for k, c in snapshot.items():
                if c > 0 and waited.get(k, 0) < c:
                    needed.append((k, c))
                    waited[k] = c

            def emit(eng, needed=tuple(needed)):
                for (k, c) in needed:
                    eng.wait_ge(sems[k], c * 16 if k in DMA_KEYS else c)
            self.streams[e].append(emit)

    def finalize(self):
        nc = self.nc
        with nc.Block() as block:
            for key, attr in (("sp", "sync"), ("pe", "tensor"), ("dve", "vector"),
                              ("act", "scalar"), ("pool", "gpsimd")):
                stream = self.streams[key]
                if not stream:
                    continue

                def body(eng, stream=stream):
                    for fn in stream:
                        fn(eng)
                getattr(block, attr)(body)
        self._exit.close()


T = 2048
TT = T // 128
D = 2048
DC = D // 128
INTER = 4096
GN = 128
NH, HD = 64, 64
L0C = 128
NCH = T // L0C
CONV_DIM = INTER + 2 * GN
DIN_PAD = 8576
XBC0, XBC1 = INTER, INTER + CONV_DIM
DT0 = INTER + CONV_DIM
F1 = 5632
FC1 = F1 // 128
HK, HV, DK, DV = 16, 32, 128, 128
QKVD = 8192
L1C = 128
NE, FMOE, FSH = 16, 1024, 2048


def ts(i, n):
    return slice(i * n, (i + 1) * n)


def ap_append0(ap, n):
    return bass.AP(ap.tensor, ap.offset, list(ap.ap) + [[0, n]])


def ap_mid0(ap, n):
    a = list(ap.ap)
    return bass.AP(ap.tensor, ap.offset, [a[0], [0, n]] + a[1:])


def tile_fm(w, cols_per):
    Dd, C = w.shape
    nc_ = C // cols_per
    t = w.reshape(Dd // 128, 128, nc_, cols_per).transpose(2, 1, 0, 3)
    return np.ascontiguousarray(t).reshape(nc_ * 128, (Dd // 128) * cols_per)




class MK:

    def __init__(self, nc, dbg=()):
        self.nc = nc
        self.ctx = Ctx(nc)
        self.inp = {}
        self.out = {}
        self.scr = {}
        self.dbg = set(dbg)

    def din(self, name, shape, dtype=F32):
        b = self.ctx.dram(name, shape, dtype, kind="ExternalInput")
        self.inp[name] = b
        return b

    def dout(self, name, shape, dtype=F32):
        b = self.ctx.dram(name, shape, dtype, kind="ExternalOutput")
        self.out[name] = b
        return b

    def dscr(self, name, shape, dtype):
        b = self.ctx.dram(name, shape, dtype, kind="Internal")
        self.scr[name] = b
        return b

    def dbg_dump(self, name, src_buf, ap, shape, dtype=F32):
        if name in self.dbg:
            o = self.dout("dbg_" + name, shape, dtype)
            self.ctx.dma("qact", o.ap(), ap, reads=(src_buf,), writes=(o,))

    def cb(self, val):
        cache = getattr(self, "_cbcache", None)
        if cache is None:
            cache = self._cbcache = {}
        assert val in cache, f"const {val} not preallocated"
        return cache[val]

    def prealloc_consts(self, vals):
        cache = getattr(self, "_cbcache", None)
        if cache is None:
            cache = self._cbcache = {}
        for val in vals:
            if val in cache:
                continue
            b = self.ctx.sbuf("cbv", [128, 1], F32)
            self.dve(lambda e, b=b, val=val: e.memset(b.ap()[:, :], float(val)),
                     writes=(b,))
            cache[val] = b

    def dve(self, fn, reads=(), writes=()):
        return self.ctx.op("dve", fn, reads=reads, writes=writes)

    def act(self, fn, reads=(), writes=()):
        return self.ctx.op("act", fn, reads=reads, writes=writes)

    def pool(self, fn, reads=(), writes=()):
        return self.ctx.op("pool", fn, reads=reads, writes=writes)

    def pe(self, fn, reads=(), writes=()):
        return self.ctx.op("pe", fn, reads=reads, writes=writes)

    def mm_acc(self, psum_buf, out_ap, terms):
        n = len(terms)
        all_reads = []
        for i, (l, r, rb) in enumerate(terms):
            for b in rb:
                if b not in all_reads:
                    all_reads.append(b)
            tok = self.ctx.op(
                "pe",
                lambda eng, l=l, r=r, i=i, n=n: eng.matmul(
                    out_ap, l, r, start=(i == 0), stop=(i == n - 1)),
                reads=rb, writes=(psum_buf,) if i == 0 else (),
                inc=(i == n - 1), commit=False)
        self.ctx.commit(tok, reads=all_reads, writes=(psum_buf,))
        return tok

    def transpose(self, psum_buf, out_ap, in_ap, ident_ap, reads):
        return self.ctx.op(
            "pe", lambda eng: eng.transpose(out_ap, in_ap, ident_ap),
            reads=reads, writes=(psum_buf,))

    def declare_io(self):
        self.din("x_own", [T, D])
        self.din("x_prev", [T, D])
        self.din("maskcol", [128, 1])
        self.din("ident_bf", [128, 128], BF16)
        self.din("ident_f32", [128, 128])
        self.din("masks3", [128, 512])
        self.din("ones_row", [1, 2048])
        self.din("w_in", [67 * 128, D])
        self.din("conv_w0", [CONV_DIM, 4])
        self.din("conv_b0", [CONV_DIM, 1])
        self.din("dt_bias0", [NH, 1])
        self.din("a0", [NH, 1])
        self.din("d0_row", [1, NH])
        self.din("nw0", [128, 32])
        self.din("w_out", [INTER, D])
        self.din("ffn_w1t", [F1, D])
        self.din("ffn_w2t", [F1, D])
        self.din("ffn_w3t", [F1, D])
        self.dout("y_out", [T, D])
        self.x_scr = self.dscr("x_scr", [INTER, T], BF16)
        self.gate_scr = self.dscr("gate_scr", [INTER, T], BF16)
        self.cA_scr = self.dscr("cA_scr", [NH, T], F32)
        self.rr_scr = self.dscr("rr_scr", [NH, T], F32)
        self.zn_scr = self.dscr("zn_scr", [128, NCH * 32 * 128], BF16)
        self.x1a_scr = self.dscr("x1a_scr", [T, D], F32)
        self.xn_scr = self.dscr("xn_scr", [128, DC * T], BF16)
        self.h_scr = self.dscr("h_scr", [F1, T], BF16)
        self.q3_scr = self.dscr("q3_scr", [F1, D], BF16)

    def alloc_persist(self):
        c = self.ctx
        self.consts = c.sbuf("consts", [128, 512], F32)
        self.identb = c.sbuf("identb", [128, 128], BF16)
        self.identf = c.sbuf("identf", [128, 128], F32)
        self.onesr = c.sbuf("onesr", [1, 512], F32)
        self.maskc = c.sbuf("maskc", [128, 1], F32)
        self.stats = c.sbuf("stats", [128, 16], F32)
        self.gthr = c.sbuf("gthr", [128, 8], F32)
        self.prealloc_consts((1e-5, 1.0, 1e-6, 1e-20, 0.5))
        self.pb = [c.psum(f"pb{i}", [128, 512], F32) for i in range(6)]
        self.pt = [c.psum(f"pt{i}", [128, 1024], BF16) for i in range(2)]

    def load_consts(self):
        c = self.ctx
        for dst, srcn in ((self.consts, "masks3"), (self.identb, "ident_bf"),
                          (self.identf, "ident_f32"), (self.maskc, "maskcol")):
            c.dma("qsp", dst.ap()[:, :], self.inp[srcn].ap(), reads=(self.inp[srcn],),
                  writes=(dst,))
        c.dma("qsp", self.onesr.ap()[:, :], self.inp["ones_row"].ap()[:, 0:512],
              reads=(self.inp["ones_row"],), writes=(self.onesr,))

    def m_low0(self, rep=1):
        a = self.consts.ap()[:, 0:128]
        return ap_mid0(a, rep) if rep > 1 else a

    def mask(self, which, rep=1):
        i = {"low0": 0, "up0": 1, "upS": 2, "lowS": 3}[which]
        a = self.consts.ap()[:, i * 128:(i + 1) * 128]
        return ap_mid0(a, rep) if rep > 1 else a

    def kone(self, row_ap, out_buf, out_ap, reads):
        self.mm_acc(out_buf, out_ap,
                    [(self.onesr.ap()[0:1, 0:128], row_ap, tuple(reads) + (self.onesr,))])

    def emit_ln(self, src_buf, xn_fm, xld, xnstage, dump):
        c = self.ctx
        st = self.stats
        xn3 = xn_fm.ap()[:, :].rearrange("p (dc t) -> p dc t", dc=DC)
        for i in range(TT):
            xb = xld
            c.dma("qsp", xb.ap()[:, :], src_buf.ap()[ts(i, 128), :],
                  reads=(src_buf,), writes=(xb,))
            self.act(lambda e, xb=xb: e.activation(dump.ap()[:, 0:D], xb.ap()[:, :],
                                                   AF.Square, accum_out=st.ap()[:, 0:1]),
                     reads=(xb,), writes=(dump, st))
            self.dve(lambda e, xb=xb: e.reduce_sum(st.ap()[:, 1:2], xb.ap()[:, :], axis=AX.X),
                     reads=(xb,), writes=(st,))
            self.dve(lambda e: e.tensor_scalar_mul(st.ap()[:, 2:3], st.ap()[:, 1:2], -1.0 / D),
                     reads=(st,), writes=(st,))
            self.dve(lambda e: e.tensor_mul(st.ap()[:, 3:4], st.ap()[:, 2:3], st.ap()[:, 2:3]),
                     reads=(st,), writes=(st,))
            self.dve(lambda e: e.scalar_tensor_tensor(st.ap()[:, 4:5], st.ap()[:, 0:1],
                                                      1.0 / D, st.ap()[:, 3:4],
                                                      op0=OP.mult, op1=OP.subtract),
                     reads=(st,), writes=(st,))
            self.act(lambda e: e.activation(st.ap()[:, 5:6], st.ap()[:, 4:5], AF.Ln, bias=self.cb(1e-5).ap()[:, :]),
                     reads=(st, self.cb(1e-5)), writes=(st,))
            self.act(lambda e: e.activation(st.ap()[:, 6:7], st.ap()[:, 5:6], AF.Exp, scale=-0.5),
                     reads=(st,), writes=(st,))
            self.dve(lambda e, xb=xb: e.tensor_scalar(xnstage.ap()[:, 0:D], xb.ap()[:, :],
                                                      st.ap()[:, 2:3], st.ap()[:, 6:7],
                                                      op0=OP.add, op1=OP.mult),
                     reads=(xb, st), writes=(xnstage,))
            for g in range(4):
                ptb = self.pt[g % 2]
                for j in range(4):
                    dc = g * 4 + j
                    self.transpose(ptb, ptb.ap()[:, ts(j, 128)],
                                   xnstage.ap()[:, ts(dc, 128)], self.identb.ap()[:, :],
                                   reads=(xnstage, self.identb))
                self.dve(lambda e, g=g, i=i, ptb=ptb:
                         e.tensor_copy(xn3[:, g * 4:(g + 1) * 4, ts(i, 128)],
                                       ptb.ap()[:, 0:512].rearrange("p (a b) -> p a b", a=4)),
                         reads=(ptb,), writes=(xn_fm,))

    def emit_front(self, is_prev, xn_fm, sc):
        c = self.ctx
        w_ap = self.inp["w_in"].ap()
        w_buf = self.inp["w_in"]
        xn3 = xn_fm.ap()[:, :].rearrange("p (dc t) -> p dc t", dc=DC)
        for idx, cc in enumerate(range(32, 67)):
            wc = sc["wcol"][idx % 2]
            c.dma("qpool", wc.ap()[:, :], w_ap[ts(cc, 128), :],
                  reads=(w_buf,), writes=(wc,))
            self.mm_kouter(
                [(self.pb[tg], self.pb[tg].ap()[:, :]) for tg in range(4)],
                [(wc.ap()[:, ts(dc, 128)],
                  [xn3[:, dc, ts(tg, 512)] for tg in range(4)], (wc, xn_fm))
                 for dc in range(DC)])
            for tg in range(4):
                pa = self.pb[tg]
                if cc < 66:
                    self.dve(lambda e, tg=tg, pa=pa: e.tensor_copy(
                        sc["conv_in"].ap()[:, 3 + tg * 512:3 + (tg + 1) * 512],
                        pa.ap()[:, :]),
                        reads=(pa,), writes=(sc["conv_in"],))
                else:
                    self.dve(lambda e, tg=tg, pa=pa: e.tensor_copy(
                        sc["dtraw"].ap()[:, ts(tg, 512)], pa.ap()[0:64, :]),
                        reads=(pa,), writes=(sc["dtraw"],))
            if cc < 66:
                self.emit_conv_chunk(cc - 32, is_prev, sc)
        self.emit_dt_pipeline(is_prev, sc)

    def emit_conv_chunk(self, rel, is_prev, sc):
        c = self.ctx
        ci, cw, cb = sc["conv_in"], sc["cw_sb"], sc["cb_sb"]
        c.dma("qsp", cw.ap()[:, :], self.inp["conv_w0"].ap()[ts(rel, 128), :],
              reads=(self.inp["conv_w0"],), writes=(cw,))
        c.dma("qsp", cb.ap()[:, :], self.inp["conv_b0"].ap()[ts(rel, 128), :],
              reads=(self.inp["conv_b0"],), writes=(cb,))
        if is_prev:
            self.dve(lambda e: e.memset(ci.ap()[:, 0:3], 0.0), writes=(ci,))
        else:
            if rel < 34:
                self.dve(lambda e, rel=rel: e.tensor_copy(
                    ci.ap()[:, 0:3], self.halobuf.ap()[:, rel * 3:rel * 3 + 3]),
                    reads=(self.halobuf,), writes=(ci,))
        acc = sc["acc"]
        self.dve(lambda e: e.tensor_scalar(acc.ap()[:, 0:T], ci.ap()[:, 0:T],
                                           cw.ap()[:, 0:1], cb.ap()[:, 0:1],
                                           op0=OP.mult, op1=OP.add),
                 reads=(ci, cw, cb), writes=(acc,))
        for k in range(1, 4):
            self.dve(lambda e, k=k: e.scalar_tensor_tensor(
                acc.ap()[:, 0:T], ci.ap()[:, k:k + T], cw.ap()[:, k:k + 1],
                acc.ap()[:, 0:T], op0=OP.mult, op1=OP.add),
                reads=(ci, cw, acc), writes=(acc,))
        if is_prev and rel < 34:
            self.dve(lambda e, rel=rel: e.tensor_scalar_mul(
                self.halobuf.ap()[:, rel * 3:rel * 3 + 3], ci.ap()[:, T:T + 3],
                self.maskc.ap()[:, 0:1]),
                reads=(ci, self.maskc), writes=(self.halobuf,))
        if rel < 32:
            self.act(lambda e: e.activation(sc["xbcs"].ap()[:, :], acc.ap()[:, 0:T], AF.Silu),
                     reads=(acc,), writes=(sc["xbcs"],))
            c.dma("qact", self.x_scr.ap()[ts(rel, 128), :], sc["xbcs"].ap()[:, :],
                  reads=(sc["xbcs"],), writes=(self.x_scr,))
        elif rel == 32:
            self.act(lambda e: e.activation(self.Bfm.ap()[:, :], acc.ap()[:, 0:T], AF.Silu),
                     reads=(acc,), writes=(self.Bfm,))
        else:
            self.act(lambda e: e.activation(self.Cfm.ap()[:, :], acc.ap()[:, 0:T], AF.Silu),
                     reads=(acc,), writes=(self.Cfm,))

    def emit_dt_pipeline(self, is_prev, sc):
        c = self.ctx
        dtr, dtb = sc["dtraw"], sc["dtb_sb"]
        w0 = sc["dtw0"]; w1 = sc["dtw1"]
        self.dve(lambda e: e.tensor_scalar_add(w0.ap(), dtr.ap()[:, :], dtb.ap()[:, 0:1]),
                 reads=(dtr, dtb), writes=(w0,))
        self.act(lambda e: e.activation(w1.ap(), w0.ap(), AF.Abs),
                 reads=(w0,), writes=(w1,))
        self.act(lambda e: e.activation(w1.ap(), w1.ap(), AF.Exp, scale=-1.0),
                 reads=(w1,), writes=(w1,))
        self.act(lambda e: e.activation(w1.ap(), w1.ap(), AF.Ln,
                                        bias=self.cb(1.0).ap()[0:64, :]),
                 reads=(w1, self.cb(1.0)), writes=(w1,))
        self.dve(lambda e: e.tensor_scalar_max(w0.ap(), w0.ap(), 0.0),
                 reads=(w0,), writes=(w0,))
        self.dve(lambda e: e.tensor_add(self.dt_sb.ap()[:, :], w0.ap(), w1.ap()),
                 reads=(w0, w1), writes=(self.dt_sb,))
        self.act(lambda e: e.activation(w1.ap(), self.dt_sb.ap()[:, :], AF.Ln),
                 reads=(self.dt_sb,), writes=(w1,))
        self.dve(lambda e: e.tensor_scalar_mul(w0.ap(), self.dt_sb.ap()[:, :],
                                               sc["a0_sb"].ap()[:, 0:1]),
                 reads=(self.dt_sb, sc["a0_sb"]), writes=(w0,))
        for cch in range(NCH):
            self.dve(lambda e, cch=cch: e.tensor_tensor_scan(
                self.cA.ap()[:, ts(cch, 128)], w0.ap()[:, ts(cch, 128)],
                self.zeros64.ap()[:, :], 0.0, op0=OP.add, op1=OP.add),
                reads=(w0, self.zeros64), writes=(self.cA,))
        self.dve(lambda e: e.tensor_sub(w1.ap(), w1.ap(), self.cA.ap()[:, :]),
                 reads=(w1, self.cA), writes=(w1,))
        c.dma("qact", self.cA_scr.ap()[:, :], self.cA.ap()[:, :],
              reads=(self.cA,), writes=(self.cA_scr,))
        c.dma("qact", self.rr_scr.ap()[:, :], w1.ap(),
              reads=(w1,), writes=(self.rr_scr,))

    def flat_row(self, scr_buf, cch, part):
        t = scr_buf.ap().tensor
        return bass.AP(t, part * 16 * T + cch * 128, [[1, 1], [T, 16], [1, 128]])

    def emit_chunk_smalls(self, cch, need_decay, sb):
        w0 = sb["csc"]
        cAc = self.cA.ap()[:, ts(cch, 128)]
        last = self.cA.ap()[:, cch * 128 + 127:cch * 128 + 128]
        pa = self.pb[4]
        self.act(lambda e: e.activation(w0.ap()[:, 0:128], cAc, AF.Exp, scale=-1.0,
                                        bias=last),
                 reads=(self.cA,), writes=(w0,))
        self.dve(lambda e: e.tensor_mul(w0.ap()[:, 128:256], w0.ap()[:, 0:128],
                                        self.dt_sb.ap()[:, ts(cch, 128)]),
                 reads=(w0, self.dt_sb), writes=(w0,))
        self.transpose(pa, pa.ap()[:, 0:64], w0.ap()[:, 128:256],
                       self.identf.ap()[0:64, 0:64], reads=(w0, self.identf))
        self.dve(lambda e, pa=pa: e.tensor_copy(sb["w_tm"].ap()[:, :], pa.ap()[:, 0:64]),
                 reads=(pa,), writes=(sb["w_tm"],))
        if need_decay:
            self.act(lambda e: e.activation(w0.ap()[:, 256:384], cAc, AF.Exp),
                     reads=(self.cA,), writes=(w0,))
            self.transpose(pa, pa.ap()[:, 64:128], w0.ap()[:, 256:384],
                           self.identf.ap()[0:64, 0:64], reads=(w0, self.identf))
            self.dve(lambda e, pa=pa: e.tensor_copy(sb["expCA_tm"].ap()[:, :],
                                                    pa.ap()[:, 64:128]),
                     reads=(pa,), writes=(sb["expCA_tm"],))
        self.act(lambda e: e.activation(w0.ap()[:, 384:385], last, AF.Exp),
                 reads=(self.cA,), writes=(w0,))
        self.transpose(pa, pa.ap()[0:1, 128:192], w0.ap()[:, 384:385],
                       self.identf.ap()[0:64, 0:64], reads=(w0, self.identf))
        self.dve(lambda e, pa=pa: e.tensor_copy(sb["cdrow"].ap()[:, :], pa.ap()[0:1, 128:192]),
                 reads=(pa,), writes=(sb["cdrow"],))
        self.kone(sb["cdrow"].ap()[0:1, :], pa, pa.ap()[:, 192:256], (sb["cdrow"],))
        self.dve(lambda e, pa=pa: e.tensor_copy(sb["cdcols"].ap()[:, :], pa.ap()[:, 192:256]),
                 reads=(pa,), writes=(sb["cdcols"],))

    def emit_ssd_chunk(self, cch, full, sb):
        c = self.ctx
        x_tm = sb["x_tm"][cch % 2]
        c.dma("qsp", x_tm.ap()[:, :], self.x_scr.ap()[:, ts(cch, 128)],
              reads=(self.x_scr,), writes=(x_tm,), transpose=True)
        self.emit_chunk_smalls(cch, full, sb)
        ptb = self.pt[0]
        self.transpose(ptb, ptb.ap()[:, 0:128], self.Bfm.ap()[:, ts(cch, 128)],
                       self.identb.ap()[:, :], reads=(self.Bfm, self.identb))
        self.act(lambda e, ptb=ptb: e.activation(sb["B_tm"].ap()[:, :], ptb.ap()[:, 0:128],
                                                 AF.Copy),
                 reads=(ptb,), writes=(sb["B_tm"],))
        if full:
            pcb = self.pb[5]
            self.mm_acc(pcb, pcb.ap()[:, 0:128], [
                (self.Cfm.ap()[:, ts(cch, 128)], self.Bfm.ap()[:, ts(cch, 128)],
                 (self.Cfm, self.Bfm))])
            self.act(lambda e, pcb=pcb: e.activation(sb["CBt"].ap()[:, :],
                                                     pcb.ap()[:, 0:128], AF.Copy),
                     reads=(pcb,), writes=(sb["CBt"],))
            for part in range(2):
                self._load_rows(cch, part, sb)
        for g4 in range(16):
            if full and g4 in (4, 8):
                self._load_rows(cch, g4 // 4 + 1, sb)
            self._ssd_group(cch, g4, full, sb)

    def _load_rows(self, cch, part, sb):
        c = self.ctx
        l2 = sb["L2"][part % 2]
        r2 = sb["R2"][part % 2]
        c.dma("qsp", l2.ap()[0:1, :], self.flat_row(self.cA_scr, cch, part),
              reads=(self.cA_scr,), writes=(l2,))
        c.dma("qsp", r2.ap()[1:2, :], self.flat_row(self.rr_scr, cch, part),
              reads=(self.rr_scr,), writes=(r2,))

    def _ssd_group(self, cch, g4, full, sb):
        h0 = 4 * g4
        x_tm = sb["x_tm"][cch % 2]
        carry = self.carry
        xsl = x_tm.ap()[:, h0 * 64:(h0 + 4) * 64]
        csl = carry.ap()[:, h0 * 64:(h0 + 4) * 64]
        if full:
            l2 = sb["L2"][(h0 // 16) % 2]
            r2 = sb["R2"][(h0 // 16) % 2]
            pdiff = self.pb[g4 % 2]
            for j in range(4):
                hh = (h0 + j) % 16
                self.mm_acc(pdiff, pdiff.ap()[:, ts(j, 128)], [
                    (l2.ap()[0:2, ts(hh, 128)], r2.ap()[0:2, ts(hh, 128)], (l2, r2))])
            self.dve(lambda e, pdiff=pdiff: e.tensor_add(
                sb["D4"].ap()[:, :], pdiff.ap()[:, :], self.m_low0(rep=4)),
                reads=(pdiff, self.consts), writes=(sb["D4"],))
            self.act(lambda e: e.activation(sb["E4"].ap()[:, :], sb["D4"].ap()[:, :], AF.Exp),
                     reads=(sb["D4"],), writes=(sb["E4"],))
            self.dve(lambda e: e.tensor_mul(sb["M4"].ap()[:, :], sb["E4"].ap()[:, :],
                                            ap_mid0(sb["CBt"].ap()[:, :], 4)),
                     reads=(sb["E4"], sb["CBt"]), writes=(sb["M4"],))
        self.pool(lambda e: e.tensor_mul(
            sb["xw4"].ap()[:, :], xsl, ap_append0(sb["w_tm"].ap()[:, h0:h0 + 4], 64)),
            reads=(x_tm, sb["w_tm"]), writes=(sb["xw4"],))
        pstat = self.pb[3]
        self.mm_acc(pstat, pstat.ap()[:, 0:256], [
            (sb["B_tm"].ap()[:, :], sb["xw4"].ap()[:, 0:256],
             (sb["B_tm"], sb["xw4"]))])
        if full:
            pY = self.pb[2]
            for j in range(4):
                self.mm_acc(pY, pY.ap()[:, ts(j, 64)], [
                    (sb["M4"].ap()[:, ts(j, 128)],
                     x_tm.ap()[:, (h0 + j) * 64:(h0 + j + 1) * 64],
                     (sb["M4"], x_tm))])
            self.pool(lambda e: e.tensor_copy(sb["carry_bf"].ap()[:, :], csl),
                      reads=(carry,), writes=(sb["carry_bf"],))
            pI = self.pb[5]
            self.mm_acc(pI, pI.ap()[:, 0:256], [
                (self.Cfm.ap()[:, ts(cch, 128)],
                 sb["carry_bf"].ap()[:, 0:256],
                 (self.Cfm, sb["carry_bf"]))])
            self.dve(lambda e, pI=pI: e.tensor_mul(
                sb["t1"].ap()[:, :], pI.ap()[:, 0:256],
                ap_append0(sb["expCA_tm"].ap()[:, h0:h0 + 4], 64)),
                reads=(pI, sb["expCA_tm"]), writes=(sb["t1"],))
            self.dve(lambda e, pY=pY: e.tensor_add(sb["t2"].ap()[:, :], sb["t1"].ap()[:, :],
                                                   pY.ap()[:, 0:256]),
                     reads=(sb["t1"], pY), writes=(sb["t2"],))
            self.pool(lambda e: e.tensor_mul(
                sb["xD4"].ap()[:, :], xsl, ap_append0(sb["dcols"].ap()[:, h0:h0 + 4], 64)),
                reads=(x_tm, sb["dcols"]), writes=(sb["xD4"],))
            self.dve(lambda e: e.tensor_add(
                sb["Y_tm"].ap()[:, h0 * 64:(h0 + 4) * 64], sb["t2"].ap()[:, :],
                sb["xD4"].ap()[:, :]),
                reads=(sb["t2"], sb["xD4"]), writes=(sb["Y_tm"],))
        self.pool(lambda e: e.tensor_mul(
            sb["tscan"].ap()[:, :], csl, ap_append0(sb["cdcols"].ap()[:, h0:h0 + 4], 64)),
            reads=(carry, sb["cdcols"]), writes=(sb["tscan"],))
        self.dve(lambda e, pstat=pstat: e.tensor_add(csl, sb["tscan"].ap()[:, :],
                                                     pstat.ap()[:, 0:256]),
                 reads=(sb["tscan"], pstat), writes=(carry,))

    def emit_z_chunk(self, cch, sb):
        c = self.ctx
        gt = sb["gate_tile"][0]
        c.dma("qsp", gt.ap()[:, :],
              self.gate_scr.ap()[:, ts(cch, 128)], reads=(self.gate_scr,), writes=(gt,),
              transpose=True)
        self.act(lambda e, gt=gt: e.activation(gt.ap()[:, :], gt.ap()[:, :], AF.Silu),
                 reads=(gt,), writes=(gt,))
        self.pool(lambda e, gt=gt: e.tensor_mul(sb["Y_tm"].ap()[:, :], sb["Y_tm"].ap()[:, :],
                                                gt.ap()[:, :]),
                  reads=(sb["Y_tm"], gt), writes=(sb["Y_tm"],))
        st = self.stats
        self.act(lambda e: e.activation(sb["dump"].ap()[:, :], sb["Y_tm"].ap()[:, :],
                                        AF.Square, accum_out=st.ap()[:, 8:9]),
                 reads=(sb["Y_tm"],), writes=(sb["dump"], st))
        self.act(lambda e: e.activation(st.ap()[:, 9:10], st.ap()[:, 8:9], AF.Ln,
                                        scale=1.0 / INTER, bias=self.cb(1e-5).ap()[:, :]),
                 reads=(st, self.cb(1e-5)), writes=(st,))
        self.act(lambda e: e.activation(st.ap()[:, 10:11], st.ap()[:, 9:10], AF.Exp,
                                        scale=-0.5),
                 reads=(st,), writes=(st,))
        self.pool(lambda e: e.tensor_scalar_mul(sb["dump"].ap()[:, :], sb["Y_tm"].ap()[:, :],
                                                st.ap()[:, 10:11]),
                  reads=(sb["Y_tm"], st), writes=(sb["dump"],))
        zf3 = sb["zn_fm"].ap()[:, :].rearrange("p (ic t) -> p ic t", ic=32)
        for g in range(8):
            ptb = self.pt[g % 2]
            for j in range(4):
                ic = g * 4 + j
                self.transpose(ptb, ptb.ap()[:, ts(j, 128)], sb["dump"].ap()[:, ts(ic, 128)],
                               self.identb.ap()[:, :], reads=(sb["dump"], self.identb))
            if g % 2 == 0:
                self.dve(lambda e, g=g, ptb=ptb: e.tensor_copy(
                    zf3[:, g * 4:(g + 1) * 4, :], ptb.ap()[:, 0:512].rearrange(
                        "p (a b) -> p a b", a=4)),
                    reads=(ptb,), writes=(sb["zn_fm"],))
            else:
                self.act(lambda e, g=g, ptb=ptb: e.activation(
                    zf3[:, g * 4:(g + 1) * 4, :], ptb.ap()[:, 0:512].rearrange(
                        "p (a b) -> p a b", a=4), AF.Copy),
                    reads=(ptb,), writes=(sb["zn_fm"],))
        c.dma("qact", self.zn_scr.ap()[:, ts(cch, 4096)], sb["zn_fm"].ap()[:, :],
              reads=(sb["zn_fm"],), writes=(self.zn_scr,))

    def emit_gate(self, xn_fm, sc):
        c = self.ctx
        w_ap = self.inp["w_in"].ap()
        xn3 = xn_fm.ap()[:, :].rearrange("p (dc t) -> p dc t", dc=DC)
        for cc in range(32):
            wc = sc["wcol"][cc % 2]
            c.dma("qpool", wc.ap()[:, :], w_ap[ts(cc, 128), :],
                  reads=(self.inp["w_in"],), writes=(wc,))
            self.mm_kouter(
                [(self.pb[tg], self.pb[tg].ap()[:, :]) for tg in range(4)],
                [(wc.ap()[:, ts(dc, 128)],
                  [xn3[:, dc, ts(tg, 512)] for tg in range(4)], (wc, xn_fm))
                 for dc in range(DC)])
            for tg in range(4):
                pa = self.pb[tg]
                self.dve(lambda e, pa=pa, tg=tg: e.tensor_copy(
                    sc["gstage"].ap()[:, ts(tg, 512)], pa.ap()[:, :]),
                    reads=(pa,), writes=(sc["gstage"],))
            c.dma("qact", self.gate_scr.ap()[ts(cc, 128), :],
                  sc["gstage"].ap()[:, :], reads=(sc["gstage"],),
                  writes=(self.gate_scr,))

    def emit_wout(self, zsrc=None, w_key="w_out", nw_key="nw0", res_buf=None,
                  dest=None):
        c = self.ctx
        if zsrc is None:
            zsrc = self.zn_scr
        if res_buf is None:
            res_buf = self.inp["x_own"]
        if dest is None:
            dest = self.x1a_scr
        with c.scope():
            nw_sb = c.sbuf("nw_sb", [128, 32], F32)
            c.dma("qsp", nw_sb.ap()[:, :], self.inp[nw_key].ap(),
                  reads=(self.inp[nw_key],), writes=(nw_sb,))
            wof = [c.sbuf(f"wof{i}", [128, 512], F32) for i in range(2)]
            wos = [c.sbuf(f"wos{i}", [128, 32 * 512], BF16) for i in range(2)]
            zld = [c.sbuf(f"zld{i}", [128, 4096], BF16) for i in range(2)]
            xres = [c.sbuf(f"xres{i}", [128, 512], F32) for i in range(2)]
            x1t = [c.sbuf(f"x1t{i}", [128, 512], F32) for i in range(2)]
            for ds in range(4):
                wo = wos[ds % 2]
                for ic in range(32):
                    wf = wof[ic % 2]
                    c.dma("qsp", wf.ap()[:, :],
                          self.inp[w_key].ap()[ts(ic, 128), ts(ds, 512)],
                          reads=(self.inp[w_key],), writes=(wf,))
                    self.dve(lambda e, wf=wf, ic=ic, wo=wo: e.tensor_scalar_mul(
                        wo.ap()[:, ts(ic, 512)], wf.ap()[:, :], nw_sb.ap()[:, ic:ic + 1]),
                        reads=(wf, nw_sb), writes=(wo,))
                for tt in range(TT):
                    zl = zld[tt % 2]
                    c.dma("qsp", zl.ap()[:, :], zsrc.ap()[:, ts(tt, 4096)],
                          reads=(zsrc,), writes=(zl,))
                    xr = xres[tt % 2]
                    c.dma("qsp", xr.ap()[:, :],
                          res_buf.ap()[ts(tt, 128), ts(ds, 512)],
                          reads=(res_buf,), writes=(xr,))
                    pa = self.pb[tt % 2]
                    self.mm_acc(pa, pa.ap()[:, :], [
                        (zl.ap()[:, ts(ic, 128)], wo.ap()[:, ts(ic, 512)], (zl, wo))
                        for ic in range(32)])
                    xt = x1t[tt % 2]
                    self.dve(lambda e, pa=pa, xr=xr, xt=xt: e.tensor_add(
                        xt.ap()[:, :], pa.ap()[:, :], xr.ap()[:, :]),
                        reads=(pa, xr), writes=(xt,))
                    c.dma("qact", dest.ap()[ts(tt, 128), ts(ds, 512)],
                          xt.ap()[:, :], reads=(xt,), writes=(dest,))

    def alloc_l0_wide(self):
        c = self.ctx
        self.carry = c.sbuf("carry", [128, NH * HD], F32)
        self.halobuf = c.sbuf("halobuf", [128, 3 * 34], F32)
        self.Bfm = c.sbuf("Bfm", [128, T], BF16)
        self.Cfm = c.sbuf("Cfm", [128, T], BF16)
        self.dt_sb = c.sbuf("dt_sb", [64, T], F32)
        self.cA = c.sbuf("cA", [64, T], F32)
        self.zeros64 = c.sbuf("zeros64", [64, 128], F32)
        self.dve(lambda e: e.memset(self.zeros64.ap()[:, :], 0.0), writes=(self.zeros64,))

    def emit_l0_attn(self):
        c = self.ctx
        for phase in ("prev", "own"):
            is_prev = phase == "prev"
            with c.scope():
                sc = {}
                xn_fm = c.sbuf("xn_fm", [128, DC * T], BF16)
                sc["wcol"] = [c.sbuf(f"wcol{i}", [128, D], BF16) for i in range(2)]
                sc["conv_in"] = c.sbuf("conv_in", [128, T + 3], F32)
                sc["xbcs"] = c.sbuf("xbcs", [128, T], BF16)
                sc["acc"] = c.sbuf("acc", [128, T], F32)
                sc["dtraw"] = c.sbuf("dtraw", [64, T], F32)
                sc["dtw0"] = DTView(sc["conv_in"], 0)
                sc["dtw1"] = DTView(sc["acc"], 0)
                sc["dump"] = c.sbuf("dumpA", [128, D], BF16)
                sc["cw_sb"] = c.sbuf("cw_sb", [128, 4], F32)
                sc["cb_sb"] = c.sbuf("cb_sb", [128, 1], F32)
                sc["dtb_sb"] = c.sbuf("dtb_sb", [64, 1], F32)
                sc["a0_sb"] = c.sbuf("a0_sb", [64, 1], F32)
                xld = c.sbuf("xld", [128, D], F32)
                xnstage = c.sbuf("xnstage", [128, D], BF16)
                c.dma("qsp", sc["dtb_sb"].ap()[:, :], self.inp["dt_bias0"].ap(),
                      reads=(self.inp["dt_bias0"],), writes=(sc["dtb_sb"],))
                c.dma("qsp", sc["a0_sb"].ap()[:, :], self.inp["a0"].ap(),
                      reads=(self.inp["a0"],), writes=(sc["a0_sb"],))
                src = self.inp["x_prev" if is_prev else "x_own"]
                self.emit_ln(src, Buf("xn_fm_view", xn_fm.ap_fn) if False else xn_fm,
                             xld, xnstage, sc["dump"])
                if not is_prev:
                    self.dbg_dump("xn_fm", xn_fm, xn_fm.ap()[:, :], [128, DC * T], BF16)
                self.emit_front(is_prev, xn_fm, sc)
                if not is_prev:
                    sc["gstage"] = c.sbuf("gstage", [128, 2048], BF16)
                    self.emit_gate(xn_fm, sc)
                    c.dma("qact", self.xn_scr.ap(), xn_fm.ap()[:, :],
                          reads=(xn_fm,), writes=(self.xn_scr,))
            with c.scope():
                sb = {}
                sb["x_tm"] = [c.sbuf(f"x_tm{i}", [128, INTER], BF16) for i in range(2)]
                sb["B_tm"] = c.sbuf("B_tm", [128, 128], BF16)
                sb["w_tm"] = c.sbuf("w_tm", [128, NH], F32)
                sb["expCA_tm"] = c.sbuf("expCA_tm", [128, NH], F32)
                sb["cdrow"] = c.sbuf("cdrow", [1, NH], F32)
                sb["cdcols"] = c.sbuf("cdcols", [128, NH], F32)
                sb["csc"] = c.sbuf("csc", [64, 512], F32)
                sb["L2"] = [c.sbuf(f"L2_{i}", [2, 2048], F32) for i in range(2)]
                sb["R2"] = [c.sbuf(f"R2_{i}", [2, 2048], F32) for i in range(2)]
                sb["CBt"] = c.sbuf("CBt", [128, 128], F32)
                sb["D4"] = c.sbuf("D4", [128, 512], F32)
                sb["E4"] = c.sbuf("E4", [128, 512], F32)
                sb["M4"] = c.sbuf("M4", [128, 512], BF16)
                sb["xw4"] = c.sbuf("xw4", [128, 256], BF16)
                sb["xD4"] = c.sbuf("xD4", [128, 256], F32)
                sb["t1"] = c.sbuf("t1", [128, 256], F32)
                sb["t2"] = c.sbuf("t2", [128, 256], F32)
                sb["tscan"] = c.sbuf("tscan", [128, 256], F32)
                for i in range(2):
                    c.dma("qsp", sb["L2"][i].ap()[1:2, :], self.inp["ones_row"].ap(),
                          reads=(self.inp["ones_row"],), writes=(sb["L2"][i],))
                    self.dve(lambda e, i=i: e.memset(sb["R2"][i].ap()[0:1, :], 1.0),
                             writes=(sb["R2"][i],))
                if is_prev:
                    self.dve(lambda e: e.memset(self.carry.ap()[:, :], 0.0),
                             writes=(self.carry,))
                else:
                    sb["Y_tm"] = c.sbuf("Y_tm", [128, INTER], F32)
                    sb["carry_bf"] = c.sbuf("carry_bf", [128, 256], BF16)
                    sb["gate_tile"] = [c.sbuf("gt0", [128, INTER], BF16)]
                    sb["dump"] = c.sbuf("dumpB", [128, INTER], BF16)
                    sb["zn_fm"] = c.sbuf("zn_fm", [128, INTER], BF16)
                    sb["drow"] = c.sbuf("drow", [1, NH], F32)
                    sb["dcols"] = c.sbuf("dcols", [128, NH], F32)
                    c.dma("qsp", sb["drow"].ap()[:, :], self.inp["d0_row"].ap(),
                          reads=(self.inp["d0_row"],), writes=(sb["drow"],))
                    pa = self.pb[4]
                    self.kone(sb["drow"].ap()[0:1, :], pa, pa.ap()[:, 0:64], (sb["drow"],))
                    self.dve(lambda e, pa=pa: e.tensor_copy(sb["dcols"].ap()[:, :],
                                                            pa.ap()[:, 0:64]),
                             reads=(pa,), writes=(sb["dcols"],))
                    self.dve(lambda e: e.tensor_scalar_mul(self.carry.ap()[:, :],
                                                           self.carry.ap()[:, :],
                                                           self.maskc.ap()[:, 0:1]),
                             reads=(self.carry, self.maskc), writes=(self.carry,))
                for cch in range(NCH):
                    self.emit_ssd_chunk(cch, not is_prev, sb)
                    if not is_prev:
                        self.emit_z_chunk(cch, sb)
                if not is_prev:
                    self.dbg_dump("Y_l0", sb["Y_tm"], sb["Y_tm"].ap()[:, :],
                                  [128, INTER], F32)
        self.emit_wout()


class DTView:
    def __init__(self, buf, p0):
        self.buf = buf
        self.p0 = p0
        self.name = buf.name + f"@{p0}"

    def ap(self):
        return self.buf.ap()[self.p0:self.p0 + 64, 0:T]

    @property
    def last_w(self):
        return self.buf.last_w

    @last_w.setter
    def last_w(self, v):
        self.buf.last_w = v

    @property
    def readers(self):
        return self.buf.readers

    @readers.setter
    def readers(self, v):
        self.buf.readers = v


class MKFFN:
    pass


def _ffn_methods():
    def emit_gamma(self):
        c = self.ctx
        with c.scope():
            gld = [c.sbuf(f"gld{i}", [128, 2048], F32) for i in range(2)]
            gs = c.sbuf("gs", [128, 8], F32)
            gtmp = c.sbuf("gtmp", [128, 2048], BF16)
            onescol = c.sbuf("onescol", [128, 1], F32)
            grow = c.sbuf("grow", [1, 8], F32)
            self.dve(lambda e: e.memset(onescol.ap()[:, :], 1.0), writes=(onescol,))
            self.dve(lambda e: e.memset(gs.ap()[:, :], 0.0), writes=(gs,))
            for wi, wname in enumerate(("ffn_w1t", "ffn_w2t", "ffn_w3t")):
                w = self.inp[wname]
                rows, cols = (F1, D)
                nch = cols // 2048 + (1 if cols % 2048 else 0)
                for rs in range(rows // 128):
                    for ch in range(nch):
                        c0 = ch * 2048
                        c1 = min(c0 + 2048, cols)
                        g = gld[(rs * nch + ch) % 2]
                        c.dma("qact", g.ap()[:, 0:c1 - c0], w.ap()[ts(rs, 128), c0:c1],
                              reads=(w,), writes=(g,))
                        self.act(lambda e, g=g, c0=c0, c1=c1, wi=wi: e.activation(
                            gtmp.ap()[:, 0:c1 - c0], g.ap()[:, 0:c1 - c0], AF.Abs,
                            accum_out=gs.ap()[:, 4:5]),
                            reads=(g,), writes=(gtmp, gs))
                        self.dve(lambda e, wi=wi: e.tensor_add(
                            gs.ap()[:, wi:wi + 1], gs.ap()[:, wi:wi + 1],
                            gs.ap()[:, 4:5]),
                            reads=(gs,), writes=(gs,))
            pa = self.pb[4]
            self.mm_acc(pa, pa.ap()[0:1, 0:3], [
                (onescol.ap()[:, :], gs.ap()[:, 0:3], (onescol, gs))])
            self.dve(lambda e, pa=pa: e.tensor_copy(grow.ap()[0:1, 0:3], pa.ap()[0:1, 0:3]),
                     reads=(pa,), writes=(grow,))
            self.kone(grow.ap()[0:1, 0:3], pa, pa.ap()[:, 8:11], (grow,))
            n1 = float(D * F1)
            for wi, n in enumerate((n1, n1, n1)):
                self.dve(lambda e, wi=wi, n=n, pa=pa: e.tensor_scalar(
                    self.gthr.ap()[:, wi:wi + 1], pa.ap()[:, 8 + wi:9 + wi],
                    0.5 / n, 0.5e-8, op0=OP.mult, op1=OP.add),
                    reads=(pa,), writes=(self.gthr,))
                self.dve(lambda e, wi=wi: e.tensor_scalar_mul(
                    self.gthr.ap()[:, 3 + wi:4 + wi], self.gthr.ap()[:, wi:wi + 1], -1.0),
                    reads=(self.gthr,), writes=(self.gthr,))

    def _quant(self, dst_buf, src_buf, wi, ta, tb):
        self.pool(lambda e: e.tensor_single_scalar(ta.ap()[:, :], src_buf.ap()[:, :],
                                                   self.gthr.ap()[:, wi:wi + 1], op=OP.is_gt),
                  reads=(src_buf, self.gthr), writes=(ta,))
        self.pool(lambda e: e.tensor_single_scalar(tb.ap()[:, :], src_buf.ap()[:, :],
                                                   self.gthr.ap()[:, 3 + wi:4 + wi], op=OP.is_lt),
                  reads=(src_buf, self.gthr), writes=(tb,))
        self.pool(lambda e: e.tensor_sub(dst_buf.ap()[:, :], ta.ap()[:, :], tb.ap()[:, :]),
                  reads=(ta, tb), writes=(dst_buf,))

    def emit_ffn(self, dest_scr):
        c = self.ctx
        with c.scope():
            xn_fm = c.sbuf("xn_fm1", [128, DC * T], BF16)
            c.dma("qsp", xn_fm.ap()[:, :], self.xn_scr.ap(), reads=(self.xn_scr,),
                  writes=(xn_fm,))
            xn3 = xn_fm.ap()[:, :].rearrange("p (dc t) -> p dc t", dc=DC)
            wld = [c.sbuf(f"wld{i}", [128, 2048], F32) for i in range(2)]
            qt = c.sbuf("qt", [128, 2048], BF16)
            qt2 = c.sbuf("qt2", [128, 2048], BF16)
            q1 = c.sbuf("q1", [128, 2048], BF16)
            q2 = c.sbuf("q2", [128, 2048], BF16)
            sgm = c.sbuf("sgm", [128, 512], F32)
            hst = c.sbuf("hst", [128, 512], BF16)
            for fc in range(FC1):
                for wi, (wn, qdst) in enumerate((("ffn_w1t", q1), ("ffn_w2t", q2))):
                    wl = wld[wi]
                    c.dma("qsp", wl.ap()[:, :], self.inp[wn].ap()[ts(fc, 128), :],
                          reads=(self.inp[wn],), writes=(wl,))
                    self._quant(qdst, wl, wi, qt, qt2)
                kt = []
                for dc in range(DC):
                    kt.append((q1.ap()[:, ts(dc, 128)],
                               [xn3[:, dc, ts(tg, 512)] for tg in range(4)], (q1, xn_fm)))
                banks1 = [(self.pb[tg], self.pb[tg].ap()[:, :]) for tg in range(4)]
                self.mm_kouter(banks1, kt)
                kt2 = []
                for dc in range(DC):
                    kt2.append((q2.ap()[:, ts(dc, 128)],
                                [xn3[:, dc, ts(tg, 512)] for tg in range(2)], (q2, xn_fm)))
                banks2 = [(self.pb[4 + tg], self.pb[4 + tg].ap()[:, :]) for tg in range(2)]
                self.mm_kouter(banks2, kt2)
                kt3 = []
                for dc in range(DC):
                    kt3.append((q2.ap()[:, ts(dc, 128)],
                                [xn3[:, dc, ts(2 + tg, 512)] for tg in range(2)], (q2, xn_fm)))
                for tg in range(4):
                    p1 = self.pb[tg]
                    if tg == 2:
                        self.mm_kouter(banks2, kt3)
                    p2 = self.pb[4 + tg % 2]
                    self.act(lambda e, p1=p1: e.activation(sgm.ap()[:, :], p1.ap()[:, :],
                                                           AF.Sigmoid),
                             reads=(p1,), writes=(sgm,))
                    self.dve(lambda e, p2=p2: e.tensor_mul(hst.ap()[:, :], sgm.ap()[:, :],
                                                           p2.ap()[:, :]),
                             reads=(sgm, p2), writes=(hst,))
                    c.dma("qact", self.h_scr.ap()[ts(fc, 128), ts(tg, 512)],
                          hst.ap()[:, :], reads=(hst,), writes=(self.h_scr,))
        with c.scope():
            wld = [c.sbuf(f"w3ld{i}", [128, 2048], F32) for i in range(2)]
            qt = c.sbuf("qt3", [128, 2048], BF16)
            qt2 = c.sbuf("qt32", [128, 2048], BF16)
            q3 = c.sbuf("q3s", [128, 2048], BF16)
            for fcc in range(FC1):
                wl = wld[fcc % 2]
                c.dma("qsp", wl.ap()[:, :], self.inp["ffn_w3t"].ap()[ts(fcc, 128), :],
                      reads=(self.inp["ffn_w3t"],), writes=(wl,))
                self._quant(q3, wl, 2, qt, qt2)
                c.dma("qact", self.q3_scr.ap()[ts(fcc, 128), :], q3.ap()[:, :],
                      reads=(q3,), writes=(self.q3_scr,))
        with c.scope():
            q3sb = c.sbuf("q3sb", [128, FC1 * 512], BF16)
            hblk = [c.sbuf(f"hblk{i}", [128, FC1 * 128], BF16) for i in range(2)]
            xres = [c.sbuf(f"xres{i}", [128, 512], F32) for i in range(2)]
            outt = c.sbuf("outt", [128, 512], F32)
            for dsl in range(4):
                for fcc in range(FC1):
                    c.dma("qpool", q3sb.ap()[:, ts(fcc, 512)],
                          self.q3_scr.ap()[ts(fcc, 128), ts(dsl, 512)],
                          reads=(self.q3_scr,), writes=(q3sb,))
                for tt in range(TT):
                    hb = hblk[tt % 2]
                    for qq in range(4):
                        src = self.h_scr.ap()[ts(qq, 1408), ts(tt, 128)].rearrange(
                            "(a p) t -> p a t", p=128)
                        c.dma("qsp", hb.ap()[:, :].rearrange(
                            "p (a t) -> p a t", a=FC1)[:, qq * 11:(qq + 1) * 11, :],
                            src, reads=(self.h_scr,), writes=(hb,))
                    xr = xres[tt % 2]
                    c.dma("qsp", xr.ap()[:, :],
                          self.x1a_scr.ap()[ts(tt, 128), ts(dsl, 512)],
                          reads=(self.x1a_scr,), writes=(xr,))
                    pa = self.pb[tt % 2]
                    self.mm_acc(pa, pa.ap()[:, :], [
                        (hb.ap()[:, ts(fcc, 128)], q3sb.ap()[:, ts(fcc, 512)], (hb, q3sb))
                        for fcc in range(FC1)])
                    self.dve(lambda e, pa=pa, xr=xr: e.tensor_add(
                        outt.ap()[:, :], pa.ap()[:, :], xr.ap()[:, :]),
                        reads=(pa, xr), writes=(outt,))
                    c.dma("qact", dest_scr.ap()[ts(tt, 128), ts(dsl, 512)],
                          outt.ap()[:, :], reads=(outt,), writes=(dest_scr,))

    return emit_gamma, _quant, emit_ffn


MK.emit_gamma, MK._quant, MK.emit_ffn = _ffn_methods()



def build_l0_program(dbg=()):
    nc = bass.Bass("TRN2", target_bir_lowering=False)
    mk = MK(nc, dbg=dbg)
    mk.declare_io()
    mk.ctx.alloc()
    mk.alloc_persist()
    mk.load_consts()
    mk.emit_gamma()
    with mk.ctx.scope():
        mk.alloc_l0_wide()
        mk.emit_l0_attn()
    mk.emit_ffn(mk.out["y_out"])
    mk.ctx.barrier_all()
    mk.ctx.finalize()
    return nc, mk


def host_l0_inputs(IN):
    f = np.float32
    import ml_dtypes
    bf = ml_dtypes.bfloat16
    r = np.arange(128)
    low0 = np.where(r[:, None] >= r[None, :], 0.0, NEG).astype(f)
    up0 = np.where(r[None, :] >= r[:, None], 0.0, NEG).astype(f)
    upS = np.where(r[None, :] > r[:, None], 0.0, NEG).astype(f)
    lowS = np.where(r[:, None] > r[None, :], 0.0, NEG).astype(f)
    shared = {
        "ident_bf": np.eye(128).astype(bf),
        "ident_f32": np.eye(128).astype(f),
        "masks3": np.concatenate([low0, up0, upS, lowS], axis=1),
        "ones_row": np.ones((1, 2048), f),
        "w_in": tile_fm(np.pad(np.asarray(IN["L0_attn_W_in"], f), ((0, 0), (0, 64))), 128),
        "conv_w0": np.ascontiguousarray(IN["L0_attn_conv_w"]).astype(f),
        "conv_b0": np.ascontiguousarray(IN["L0_attn_conv_b"]).reshape(-1, 1).astype(f),
        "dt_bias0": IN["L0_attn_dt_bias"].reshape(-1, 1).astype(f),
        "a0": IN["L0_attn_A"].reshape(-1, 1).astype(f),
        "d0_row": IN["L0_attn_D"].reshape(1, -1).astype(f),
        "nw0": np.ascontiguousarray(IN["L0_attn_norm_weight"].reshape(32, 128).T).astype(f),
        "w_out": np.ascontiguousarray(IN["L0_attn_W_out"]).astype(f),
        "ffn_w1t": tile_fm(np.asarray(IN["L0_ffn_w1"], f).T, 128),
        "ffn_w2t": tile_fm(np.asarray(IN["L0_ffn_w2"], f).T, 128),
        "ffn_w3t": np.ascontiguousarray(IN["L0_ffn_w3"].T).astype(f),
    }
    return shared


def core_l0_inputs(IN, shared, core):
    b, h = core // 2, core % 2
    x = IN["L0_attn_hidden_states"]
    d = dict(shared)
    d["x_own"] = np.ascontiguousarray(x[b, h * T:(h + 1) * T]).astype(np.float32)
    d["x_prev"] = np.ascontiguousarray(x[b, 0:T]).astype(np.float32)
    d["maskcol"] = np.full((128, 1), 1.0 if h == 1 else 0.0, np.float32)
    return d


def _l1_methods():
    def declare_l1_io(self):
        self.din("wqkv", [QKVD, D])
        self.din("wz", [4096, D])
        self.din("wab", [D, 64])
        self.din("conv_w1", [QKVD, 4])
        self.din("alog", [HV, 1])
        self.din("dtb1", [HV, 1])
        self.din("nw1", [128, 32])
        self.din("wout1", [4096, D])
        self.din("gwT", [D, 16])
        self.din("ebias", [1, 16])
        self.din("gate_wT", [NE * 2 * 128, DC * 512])
        self.din("up_wT", [NE * 2 * 128, DC * 512])
        self.din("down_wT", [NE * 4 * 128, 8 * 512])
        self.din("shgT", [4 * 128, DC * 512])
        self.din("shuT", [4 * 128, DC * 512])
        self.din("shdT", [4 * 128, DC * 512])
        self.qkv_scr = self.dscr("qkv_scr", [QKVD, T], BF16)
        self.z_scr = self.dscr("z_scr", [4096, T], BF16)
        self.yn_scr = self.dscr("yn_scr", [128, NCH * 32 * 128], BF16)
        self.lr_scr = self.dscr("lr_scr", [3 * HV, T], F32)
        self.x2a_scr = self.dscr("x2a_scr", [T, D], F32)
        self.xn1_scr = self.dscr("xn1_scr", [128, DC * T], BF16)

    def alloc_l1_persist(self):
        c = self.ctx
        self.cA1 = c.sbuf("cA1", [32, T], F32)
        self.beta_hm = c.sbuf("beta_hm", [32, T], F32)
        self.La_hm = c.sbuf("La_hm", [32, T], F32)
        self.Ra_hm = c.sbuf("Ra_hm", [32, T], F32)
        self.Rq_hm = c.sbuf("Rq_hm", [32, T], F32)
        self.g_hm = c.sbuf("g_hm", [32, T], F32)
        self.lnb_hm = c.sbuf("lnb_hm", [32, T], F32)
        self.lnik_hm = c.sbuf("lnik_hm", [16, T], F32)
        self.lniq_hm = c.sbuf("lniq_hm", [16, T], F32)
        self.zeros32 = c.sbuf("zeros32", [32, 128], F32)

    def emit_l1_front(self, x1_buf):
        c = self.ctx
        self.dve(lambda e: e.memset(self.zeros32.ap()[:, :], 0.0), writes=(self.zeros32,))
        with c.scope():
            sc = {}
            xn_fm = c.sbuf("xn1_fm", [128, DC * T], BF16)
            with c.scope():
                xld = c.sbuf("xld1", [128, D], F32)
                xnstage = c.sbuf("xnstage1", [128, D], BF16)
                dump = c.sbuf("dumpL1", [128, D], BF16)
                self.emit_ln(x1_buf, xn_fm, xld, xnstage, dump)
            c.dma("qact", self.xn1_scr.ap(), xn_fm.ap()[:, :],
                  reads=(xn_fm,), writes=(self.xn1_scr,))
            xn3 = xn_fm.ap()[:, :].rearrange("p (dc t) -> p dc t", dc=DC)
            self._l1_qkv_conv(xn3, xn_fm)
            self._l1_zab(xn3, xn_fm)

    def _l1_qkv_conv(self, xn3, xn_fm):
        c = self.ctx
        with c.scope():
            wcol = [c.sbuf(f"wcol1{i}", [128, D], BF16) for i in range(2)]
            conv_in = c.sbuf("conv_in1", [128, T + 3], F32)
            acc = c.sbuf("acc1", [128, T], F32)
            xbcs = c.sbuf("xbcs1", [128, T], BF16)
            cw = c.sbuf("cw1", [128, 4], F32)
            for cc in range(64):
                wc = wcol[cc % 2]
                c.dma("qpool", wc.ap()[:, :], self.inp["wqkv"].ap()[ts(cc, 128), :],
                      reads=(self.inp["wqkv"],), writes=(wc,))
                self.mm_kouter(
                    [(self.pb[tg], self.pb[tg].ap()[:, :]) for tg in range(4)],
                    [(wc.ap()[:, ts(dc, 128)],
                      [xn3[:, dc, ts(tg, 512)] for tg in range(4)], (wc, xn_fm))
                     for dc in range(DC)])
                for tg in range(4):
                    pa = self.pb[tg]
                    self.dve(lambda e, tg=tg, pa=pa: e.tensor_copy(
                        conv_in.ap()[:, 3 + tg * 512:3 + (tg + 1) * 512], pa.ap()[:, :]),
                        reads=(pa,), writes=(conv_in,))
                c.dma("qsp", cw.ap()[:, :], self.inp["conv_w1"].ap()[ts(cc, 128), :],
                      reads=(self.inp["conv_w1"],), writes=(cw,))
                self.dve(lambda e: e.memset(conv_in.ap()[:, 0:3], 0.0), writes=(conv_in,))
                self.dve(lambda e: e.tensor_scalar(acc.ap()[:, 0:T], conv_in.ap()[:, 0:T],
                                                   cw.ap()[:, 0:1], None, op0=OP.mult),
                         reads=(conv_in, cw), writes=(acc,))
                for k in range(1, 4):
                    self.dve(lambda e, k=k: e.scalar_tensor_tensor(
                        acc.ap()[:, 0:T], conv_in.ap()[:, k:k + T], cw.ap()[:, k:k + 1],
                        acc.ap()[:, 0:T], op0=OP.mult, op1=OP.add),
                        reads=(conv_in, cw, acc), writes=(acc,))
                self.act(lambda e: e.activation(xbcs.ap()[:, :], acc.ap()[:, 0:T], AF.Silu),
                         reads=(acc,), writes=(xbcs,))
                c.dma("qact", self.qkv_scr.ap()[ts(cc, 128), :], xbcs.ap()[:, :],
                      reads=(xbcs,), writes=(self.qkv_scr,))

    def _l1_zab(self, xn3, xn_fm):
        c = self.ctx
        with c.scope():
            wcol = [c.sbuf(f"wzc{i}", [128, D], BF16) for i in range(2)]
            gstage = c.sbuf("gstage1", [128, 2048], BF16)
            for cc in range(32):
                wc = wcol[cc % 2]
                c.dma("qpool", wc.ap()[:, :], self.inp["wz"].ap()[ts(cc, 128), :],
                      reads=(self.inp["wz"],), writes=(wc,))
                self.mm_kouter(
                    [(self.pb[tg], self.pb[tg].ap()[:, :]) for tg in range(4)],
                    [(wc.ap()[:, ts(dc, 128)],
                      [xn3[:, dc, ts(tg, 512)] for tg in range(4)], (wc, xn_fm))
                     for dc in range(DC)])
                for tg in range(4):
                    pa = self.pb[tg]
                    self.dve(lambda e, pa=pa, tg=tg: e.tensor_copy(
                        gstage.ap()[:, ts(tg, 512)], pa.ap()[:, :]),
                        reads=(pa,), writes=(gstage,))
                c.dma("qact", self.z_scr.ap()[ts(cc, 128), :],
                      gstage.ap()[:, :], reads=(gstage,), writes=(self.z_scr,))
            wabs = c.sbuf("wabs", [128, DC * 64], BF16)
            wab3 = wabs.ap()[:, :].rearrange("p (dc g) -> p dc g", dc=DC)
            ab_hm = c.sbuf("ab_hm", [64, T], F32)
            abst = c.sbuf("abst", [128, 64], F32)
            for q in range(4):
                src = self.inp["wab"].ap()[ts(q, 512), :].rearrange(
                    "(a p) c -> p a c", p=128)
                c.dma("qpool", wab3[:, q * 4:(q + 1) * 4, :], src,
                      reads=(self.inp["wab"],), writes=(wabs,))
            for tt in range(TT):
                pa = self.pb[tt % 2]
                self.mm_acc(pa, pa.ap()[:, 0:64], [
                    (xn3[:, dc, ts(tt, 128)], wab3[:, dc, :], (xn_fm, wabs))
                    for dc in range(DC)])
                self.dve(lambda e, pa=pa: e.tensor_copy(abst.ap()[:, :], pa.ap()[:, 0:64]),
                         reads=(pa,), writes=(abst,))
                pt = self.pb[2 + tt % 2]
                self.transpose(pt, pt.ap()[0:64, 0:128], abst.ap()[:, :],
                               self.identf.ap()[:, :], reads=(abst, self.identf))
                self.dve(lambda e, pt=pt, tt=tt: e.tensor_copy(
                    ab_hm.ap()[:, ts(tt, 128)], pt.ap()[0:64, 0:128]),
                    reads=(pt,), writes=(ab_hm,))
            al = c.sbuf("al", [32, 2], F32)
            c.dma("qsp", al.ap()[:, 0:1], self.inp["alog"].ap(),
                  reads=(self.inp["alog"],), writes=(al,))
            c.dma("qsp", al.ap()[1:2, 1:2] if False else al.ap()[:, 1:2],
                  self.inp["dtb1"].ap(), reads=(self.inp["dtb1"],), writes=(al,))
            self.act(lambda e: e.activation(al.ap()[:, 0:1], al.ap()[:, 0:1], AF.Exp),
                     reads=(al,), writes=(al,))
            w0 = c.sbuf("spw0", [32, T], F32)
            w1 = c.sbuf("spw1", [32, T], F32)
            self.dve(lambda e: e.tensor_scalar_add(w0.ap()[:, :], ab_hm.ap()[0:32, :],
                                                   al.ap()[:, 1:2]),
                     reads=(ab_hm, al), writes=(w0,))
            self.act(lambda e: e.activation(w1.ap()[:, :], w0.ap()[:, :], AF.Abs),
                     reads=(w0,), writes=(w1,))
            self.act(lambda e: e.activation(w1.ap()[:, :], w1.ap()[:, :], AF.Exp, scale=-1.0),
                     reads=(w1,), writes=(w1,))
            self.act(lambda e: e.activation(w1.ap()[:, :], w1.ap()[:, :], AF.Ln,
                                            bias=self.cb(1.0).ap()[0:32, :]),
                     reads=(w1, self.cb(1.0)), writes=(w1,))
            self.dve(lambda e: e.tensor_scalar_max(w0.ap()[:, :], w0.ap()[:, :], 0.0),
                     reads=(w0,), writes=(w0,))
            self.dve(lambda e: e.tensor_add(w0.ap()[:, :], w0.ap()[:, :], w1.ap()[:, :]),
                     reads=(w0, w1), writes=(w0,))
            self.dve(lambda e: e.tensor_scalar(self.g_hm.ap()[:, :], w0.ap()[:, :],
                                               al.ap()[:, 0:1], -1.0,
                                               op0=OP.mult, op1=OP.mult),
                     reads=(w0, al), writes=(self.g_hm,))
            for cch in range(NCH):
                self.dve(lambda e, cch=cch: e.tensor_tensor_scan(
                    self.cA1.ap()[:, ts(cch, 128)], self.g_hm.ap()[:, ts(cch, 128)],
                    self.zeros32.ap()[:, :], 0.0, op0=OP.add, op1=OP.add),
                    reads=(self.g_hm, self.zeros32), writes=(self.cA1,))
            self.dve(lambda e: e.tensor_copy(w0.ap()[:, :], ab_hm.ap()[32:64, :]),
                     reads=(ab_hm,), writes=(w0,))
            self.act(lambda e: e.activation(self.beta_hm.ap()[:, :], w0.ap()[:, :],
                                            AF.Sigmoid),
                     reads=(w0,), writes=(self.beta_hm,))
            self.act(lambda e: e.activation(w1.ap()[:, :], w0.ap()[:, :], AF.Abs),
                     reads=(w0,), writes=(w1,))
            self.act(lambda e: e.activation(w1.ap()[:, :], w1.ap()[:, :], AF.Exp, scale=-1.0),
                     reads=(w1,), writes=(w1,))
            self.act(lambda e: e.activation(w1.ap()[:, :], w1.ap()[:, :], AF.Ln,
                                            bias=self.cb(1.0).ap()[0:32, :]),
                     reads=(w1, self.cb(1.0)), writes=(w1,))
            self.dve(lambda e: e.tensor_scalar(w0.ap()[:, :], w0.ap()[:, :],
                                               -1.0, 0.0, op0=OP.mult, op1=OP.max),
                     reads=(w0,), writes=(w0,))
            self.dve(lambda e: e.scalar_tensor_tensor(self.lnb_hm.ap()[:, :],
                                                      w0.ap()[:, :], -1.0, w1.ap()[:, :],
                                                      op0=OP.bypass, op1=OP.add),
                     reads=(w0, w1), writes=(self.lnb_hm,))
            self.dve(lambda e: e.tensor_scalar_mul(self.lnb_hm.ap()[:, :],
                                                   self.lnb_hm.ap()[:, :], -1.0),
                     reads=(self.lnb_hm,), writes=(self.lnb_hm,))
    return declare_l1_io, alloc_l1_persist, emit_l1_front, _l1_qkv_conv, _l1_zab


(MK.declare_l1_io, MK.alloc_l1_persist, MK.emit_l1_front, MK._l1_qkv_conv,
 MK._l1_zab) = _l1_methods()


def _l1_chunk_methods():
    def emit_l1_prepass(self):
        c = self.ctx
        with c.scope():
            qk_tm = [c.sbuf(f"qkt{i}", [128, 2048], BF16) for i in range(2)]
            sqf = c.sbuf("sqf", [128, 2048], F32)
            ss = c.sbuf("ssn", [128, 16], F32)
            for cch in range(NCH):
                for qi, dst in ((0, self.lniq_hm), (1, self.lnik_hm)):
                    qt = qk_tm[qi]
                    c.dma("qsp", qt.ap()[:, :],
                          self.qkv_scr.ap()[qi * 2048:(qi + 1) * 2048, ts(cch, 128)],
                          reads=(self.qkv_scr,), writes=(qt,), transpose=True)
                    self.act(lambda e, qt=qt: e.activation(sqf.ap()[:, :], qt.ap()[:, :],
                                                           AF.Square),
                             reads=(qt,), writes=(sqf,))
                    self.dve(lambda e: e.tensor_reduce(
                        ss.ap()[:, :].rearrange("p (g o) -> p g o", g=16),
                        sqf.ap()[:, :].rearrange("p (g n) -> p g n", g=16),
                        axis=AX.X, op=OP.add),
                        reads=(sqf,), writes=(ss,))
                    self.act(lambda e: e.activation(ss.ap()[:, :], ss.ap()[:, :], AF.Ln,
                                                    bias=self.cb(1e-6).ap()[:, :]),
                             reads=(ss, self.cb(1e-6)), writes=(ss,))
                    if qi == 0:
                        self.dve(lambda e: e.tensor_scalar(
                            ss.ap()[:, :], ss.ap()[:, :], float(np.log(128.0)), -0.5,
                            op0=OP.add, op1=OP.mult),
                            reads=(ss,), writes=(ss,))
                    else:
                        self.dve(lambda e: e.tensor_scalar_mul(ss.ap()[:, :],
                                                               ss.ap()[:, :], -0.5),
                                 reads=(ss,), writes=(ss,))
                    pt = self.pb[4]
                    self.transpose(pt, pt.ap()[0:16, 0:128], ss.ap()[:, :],
                                   self.identf.ap()[:, :], reads=(ss, self.identf))
                    self.dve(lambda e, pt=pt, dst=dst, cch=cch: e.tensor_copy(
                        dst.ap()[:, ts(cch, 128)], pt.ap()[0:16, 0:128]),
                        reads=(pt,), writes=(dst,))

    def emit_l1_rows(self):
        c = self.ctx
        ln_scr = self.dscr(f"ln_scr_{self.ctx.cnt['dve']}", [32, T], F32)
        c.dma("qact", ln_scr.ap()[0:16, :], self.lnik_hm.ap()[:, :],
              reads=(self.lnik_hm,), writes=(ln_scr,))
        c.dma("qact", ln_scr.ap()[16:32, :], self.lniq_hm.ap()[:, :],
              reads=(self.lniq_hm,), writes=(ln_scr,))
        with c.scope():
            ikrep = c.sbuf("ikrep", [32, T], F32)
            iqrep = c.sbuf("iqrep", [32, T], F32)
            t = ln_scr.ap().tensor
            c.dma("qsp", ikrep.ap()[:, :], bass.AP(t, 0, [[T, 16], [0, 2], [1, T]]),
                  reads=(ln_scr,), writes=(ikrep,))
            c.dma("qsp", iqrep.ap()[:, :], bass.AP(t, 16 * T, [[T, 16], [0, 2], [1, T]]),
                  reads=(ln_scr,), writes=(iqrep,))
            self.dve(lambda e: e.tensor_sub(self.La_hm.ap()[:, :], ikrep.ap()[:, :],
                                            self.cA1.ap()[:, :]),
                     reads=(ikrep, self.cA1), writes=(self.La_hm,))
            self.dve(lambda e: e.tensor_add(self.Ra_hm.ap()[:, :], ikrep.ap()[:, :],
                                            self.cA1.ap()[:, :]),
                     reads=(ikrep, self.cA1), writes=(self.Ra_hm,))
            self.dve(lambda e: e.tensor_add(self.Ra_hm.ap()[:, :], self.Ra_hm.ap()[:, :],
                                            self.lnb_hm.ap()[:, :]),
                     reads=(self.Ra_hm, self.lnb_hm), writes=(self.Ra_hm,))
            self.dve(lambda e: e.tensor_add(self.Rq_hm.ap()[:, :], iqrep.ap()[:, :],
                                            self.cA1.ap()[:, :]),
                     reads=(iqrep, self.cA1), writes=(self.Rq_hm,))
        for i, b in enumerate((self.La_hm, self.Ra_hm, self.Rq_hm)):
            c.dma("qact", self.lr_scr.ap()[i * HV:(i + 1) * HV, :], b.ap()[:, :],
                  reads=(b,), writes=(self.lr_scr,))

    def lr_row(self, which, cch, part):
        t = self.lr_scr.ap().tensor
        base = {"La": 0, "Ra": HV, "Rq": 2 * HV}[which] * T
        return bass.AP(t, base + part * 8 * T + cch * 128, [[1, 1], [T, 8], [1, 128]])

    return emit_l1_prepass, emit_l1_rows, lr_row


MK.emit_l1_prepass, MK.emit_l1_rows, MK.lr_row = _l1_chunk_methods()


def _l1_loop_methods():
    def _l1_smalls(self, cch, sb):
        hmw = sb["hmw"]
        pt = self.pb[4]
        last = self.cA1.ap()[:, cch * 128 + 127:cch * 128 + 128]
        specs = [(self.La_hm, 0, "wk_tm", last), (self.Ra_hm, 128, "bl_tm", None),
                 (self.Rq_hm, 256, "liq_tm", None), (self.beta_hm, 384, "beta_tm", None)]
        for (srcb, off, dst, bias) in specs:
            if dst == "beta_tm":
                self.dve(lambda e, srcb=srcb, off=off: e.tensor_copy(
                    hmw.ap()[:, off:off + 128], srcb.ap()[:, ts(cch, 128)]),
                    reads=(srcb,), writes=(hmw,))
            else:
                bb = 0.0 if bias is None else bias
                self.act(lambda e, srcb=srcb, off=off, bb=bb: e.activation(
                    hmw.ap()[:, off:off + 128], srcb.ap()[:, ts(cch, 128)], AF.Exp,
                    bias=bb),
                    reads=(srcb, self.cA1), writes=(hmw,))
            self.transpose(pt, pt.ap()[:, off // 4:off // 4 + 32],
                           hmw.ap()[:, off:off + 128], self.identf.ap()[0:32, 0:32],
                           reads=(hmw, self.identf))
            self.dve(lambda e, dst=dst, off=off, pt=pt: e.tensor_copy(
                sb[dst].ap()[:, :], pt.ap()[:, off // 4:off // 4 + 32]),
                reads=(pt,), writes=(sb[dst],))
        self.act(lambda e: e.activation(hmw.ap()[:, 384:385] if False else hmw.ap()[:, 440:441],
                                        last, AF.Exp),
                 reads=(self.cA1,), writes=(hmw,))
        self.transpose(pt, pt.ap()[0:1, 160:192], hmw.ap()[:, 440:441],
                       self.identf.ap()[0:32, 0:32], reads=(hmw, self.identf))
        self.dve(lambda e, pt=pt: e.tensor_copy(sb["lamrow"].ap()[:, :],
                                                pt.ap()[0:1, 160:192]),
                 reads=(pt,), writes=(sb["lamrow"],))
        self.kone(sb["lamrow"].ap()[0:1, :], pt, pt.ap()[:, 192:224], (sb["lamrow"],))
        self.dve(lambda e, pt=pt: e.tensor_copy(sb["lamL"].ap()[:, :], pt.ap()[:, 192:224]),
                 reads=(pt,), writes=(sb["lamL"],))

    def _l1_load_rows(self, cch, part, sb):
        c = self.ctx
        p = part % 2
        for name, which, row in (("Lp", "La", 0), ("Rap", "Ra", 1), ("Rqp", "Rq", 1)):
            tile = sb[name][p]
            c.dma("qsp", tile.ap()[row:row + 1, :], self.lr_row(which, cch, part),
                  reads=(self.lr_scr,), writes=(tile,))

    def emit_l1_chunk(self, cch, sb):
        c = self.ctx
        qf, kf, k_tm, v_tm = sb["qf"], sb["kf"], sb["k_tm"], sb["v_tm"]
        for dst, r0, r1, tr in ((qf, 0, 2048, False), (kf, 2048, 4096, False)):
            for q in range(2):
                src = self.qkv_scr.ap()[r0 + q * 1024:r0 + (q + 1) * 1024,
                                        ts(cch, 128)].rearrange("(a p) t -> p a t", p=128)
                c.dma("qsp", dst.ap()[:, :].rearrange("p (a t) -> p a t", a=16)
                      [:, q * 8:(q + 1) * 8, :], src,
                      reads=(self.qkv_scr,), writes=(dst,))
        c.dma("qact", k_tm.ap()[:, :], self.qkv_scr.ap()[2048:4096, ts(cch, 128)],
              reads=(self.qkv_scr,), writes=(k_tm,), transpose=True)
        c.dma("qact", v_tm.ap()[:, :], self.qkv_scr.ap()[4096:8192, ts(cch, 128)],
              reads=(self.qkv_scr,), writes=(v_tm,), transpose=True)
        self._l1_smalls(cch, sb)
        self.dve(lambda e: e.tensor_copy(self.S_bf.ap()[:, :], self.S_f32.ap()[:, :]),
                 reads=(self.S_f32,), writes=(self.S_bf,))
        self._l1_load_rows(cch, 0, sb)
        self._l1_load_rows(cch, 1, sb)
        for g in range(8):
            if g in (2, 4):
                self._l1_load_rows(cch, g // 2 + 1, sb)
            self._l1_group(cch, g, sb)
        self._l1_ychunk(cch, sb)

    def _l1_group(self, cch, g, sb):
        h0 = 4 * g
        p = (g // 2) % 2
        Lp, Rap, Rqp = sb["Lp"][p], sb["Rap"][p], sb["Rqp"][p]
        qf, kf, k_tm, v_tm = sb["qf"], sb["kf"], sb["k_tm"], sb["v_tm"]
        ph0 = (h0 % 8)
        kb1, kb5, kb0, kb2, kb3 = self.pb[1], self.pb[5], self.pb[0], self.pb[2], self.pb[3]
        for j, kk in enumerate((g * 2, g * 2 + 1)):
            self.mm_acc(kb1, kb1.ap()[:, ts(j, 128)], [
                (kf.ap()[:, ts(kk, 128)], kf.ap()[:, ts(kk, 128)], (kf,))])
            self.mm_acc(kb1, kb1.ap()[:, 256 + j * 128:256 + (j + 1) * 128], [
                (kf.ap()[:, ts(kk, 128)], qf.ap()[:, ts(kk, 128)], (kf, qf))])
        self.act(lambda e: e.activation(sb["KKq"].ap()[:, :], kb1.ap()[:, :], AF.Copy),
                 reads=(kb1,), writes=(sb["KKq"],))

        def kk_pat(off):
            a = sb["KKq"].ap()[:, off:off + 256]
            return bass.AP(a.tensor, a.offset, [list(a.ap[0]), [128, 2], [0, 2], [1, 128]])

        def build(dst, lhs, rhs, mask, pat):
            for j in range(4):
                hh = ph0 + j
                self.mm_acc(kb0, kb0.ap()[:, ts(j, 128)], [
                    (lhs.ap()[0:2, ts(hh, 128)], rhs.ap()[0:2, ts(hh, 128)], (lhs, rhs))])
            self.dve(lambda e: e.tensor_add(sb["dA"].ap()[:, :], kb0.ap()[:, :],
                                            self.mask(mask, rep=4)),
                     reads=(kb0, self.consts), writes=(sb["dA"],))
            self.act(lambda e: e.activation(sb["dA"].ap()[:, :], sb["dA"].ap()[:, :],
                                            AF.Exp),
                     reads=(sb["dA"],), writes=(sb["dA"],))
            self.dve(lambda e, pat=pat: e.tensor_mul(
                dst.ap()[:, :].rearrange("p (a b c) -> p a b c", a=2, b=2),
                sb["dA"].ap()[:, :].rearrange("p (a b c) -> p a b c", a=2, b=2), pat),
                reads=(sb["dA"], sb["KKq"]), writes=(dst,))
        build(sb["At4"], Lp, Rap, "upS", kk_pat(0))
        build(sb["A4"], Rap, Lp, "lowS", kk_pat(0))
        build(sb["Mqt4"], Lp, Rqp, "up0", kk_pat(256))
        for j in range(4):
            kk = g * 2 + j // 2
            self.mm_acc(kb5, kb5.ap()[:, ts(j, 128)], [
                (kf.ap()[:, ts(kk, 128)], self.S_bf.ap()[:, ts(h0 + j, 128)],
                 (kf, self.S_bf))])
        self.dve(lambda e: e.tensor_mul(sb["tf1"].ap()[:, :], kb5.ap()[:, :],
                                        ap_append0(sb["bl_tm"].ap()[:, h0:h0 + 4], 128)),
                 reads=(kb5, sb["bl_tm"]), writes=(sb["tf1"],))
        self.dve(lambda e: e.tensor_mul(sb["tf2"].ap()[:, :],
                                        v_tm.ap()[:, h0 * 128:(h0 + 4) * 128],
                                        ap_append0(sb["beta_tm"].ap()[:, h0:h0 + 4], 128)),
                 reads=(v_tm, sb["beta_tm"]), writes=(sb["tf2"],))
        V = sb["V4"][0]
        self.dve(lambda e, V=V: e.tensor_sub(V.ap()[:, :], sb["tf2"].ap()[:, :],
                                             sb["tf1"].ap()[:, :]),
                 reads=(sb["tf1"], sb["tf2"]), writes=(V,))
        Pc, Ptc = sb["A4"], sb["At4"]
        vcur = 0
        for j in range(4):
            self.mm_acc(kb2, kb2.ap()[:, ts(j, 128)], [
                (Ptc.ap()[:, ts(j, 128)], V.ap()[:, ts(j, 128)], (Ptc, V))])
        Vn = sb["V4"][1]
        self.dve(lambda e, V=V, Vn=Vn: e.tensor_sub(Vn.ap()[:, :], V.ap()[:, :],
                                                    kb2.ap()[:, :]),
                 reads=(V, kb2), writes=(Vn,))
        vcur = 1
        for k in range(1, 7):
            Pn = sb["P"][k % 2]
            Ptn = sb["Pt"][k % 2]
            for j in range(4):
                self.mm_acc(kb3, kb3.ap()[:, ts(j, 128)], [
                    (Ptc.ap()[:, ts(j, 128)], Pc.ap()[:, ts(j, 128)], (Ptc, Pc))])
            self.act(lambda e, Pn=Pn: e.activation(Pn.ap()[:, :], kb3.ap()[:, :], AF.Copy),
                     reads=(kb3,), writes=(Pn,))
            for j in range(4):
                self.mm_acc(kb0, kb0.ap()[:, ts(j, 128)], [
                    (Pc.ap()[:, ts(j, 128)], Ptc.ap()[:, ts(j, 128)], (Pc, Ptc))])
            self.act(lambda e, Ptn=Ptn: e.activation(Ptn.ap()[:, :], kb0.ap()[:, :],
                                                     AF.Copy),
                     reads=(kb0,), writes=(Ptn,))
            Pc, Ptc = Pn, Ptn
            V, Vn = sb["V4"][vcur], sb["V4"][1 - vcur]
            for j in range(4):
                self.mm_acc(kb2, kb2.ap()[:, ts(j, 128)], [
                    (Ptc.ap()[:, ts(j, 128)], V.ap()[:, ts(j, 128)], (Ptc, V))])
            self.dve(lambda e, V=V, Vn=Vn: e.tensor_add(Vn.ap()[:, :], V.ap()[:, :],
                                                        kb2.ap()[:, :]),
                     reads=(V, kb2), writes=(Vn,))
            vcur = 1 - vcur
        U = sb["V4"][vcur]
        for j in range(4):
            kk = g * 2 + j // 2
            self.mm_acc(kb5, kb5.ap()[:, ts(j, 128)], [
                (qf.ap()[:, ts(kk, 128)], self.S_bf.ap()[:, ts(h0 + j, 128)],
                 (qf, self.S_bf))])
        self.dve(lambda e: e.tensor_mul(sb["tf1"].ap()[:, :], kb5.ap()[:, :],
                                        ap_append0(sb["liq_tm"].ap()[:, h0:h0 + 4], 128)),
                 reads=(kb5, sb["liq_tm"]), writes=(sb["tf1"],))
        for j in range(4):
            self.mm_acc(kb2, kb2.ap()[:, ts(j, 128)], [
                (sb["Mqt4"].ap()[:, ts(j, 128)], U.ap()[:, ts(j, 128)],
                 (sb["Mqt4"], U))])
        self.dve(lambda e: e.tensor_add(sb["o_tm"].ap()[:, h0 * 128:(h0 + 4) * 128],
                                        sb["tf1"].ap()[:, :], kb2.ap()[:, :]),
                 reads=(sb["tf1"], kb2), writes=(sb["o_tm"],))
        ka = k_tm.ap()[:, :]
        kpat = bass.AP(ka.tensor, ka.offset + (g * 2) * 128,
                       [list(ka.ap[0]), [128, 2], [0, 2], [1, 128]])
        self.dve(lambda e, kpat=kpat: e.tensor_mul(
            sb["kw4"].ap()[:, :].rearrange("p (a b c) -> p a b c", a=2, b=2), kpat,
            ap_append0(sb["wk_tm"].ap()[:, h0:h0 + 4], 128).rearrange(
                "p (a b) c -> p a b c", a=2)),
            reads=(k_tm, sb["wk_tm"]), writes=(sb["kw4"],))
        for j in range(4):
            self.mm_acc(kb3, kb3.ap()[:, ts(j, 128)], [
                (sb["kw4"].ap()[:, ts(j, 128)], U.ap()[:, ts(j, 128)],
                 (sb["kw4"], U))])
        ssl = self.S_f32.ap()[:, h0 * 128:(h0 + 4) * 128]
        self.dve(lambda e: e.tensor_mul(sb["tf1"].ap()[:, :], ssl,
                                        ap_append0(sb["lamL"].ap()[:, h0:h0 + 4], 128)),
                 reads=(self.S_f32, sb["lamL"]), writes=(sb["tf1"],))
        self.dve(lambda e: e.tensor_add(ssl, sb["tf1"].ap()[:, :], kb3.ap()[:, :]),
                 reads=(sb["tf1"], kb3), writes=(self.S_f32,))

    def _l1_ychunk(self, cch, sb):
        c = self.ctx
        zt = sb["zt"]
        c.dma("qsp", zt.ap()[:, :], self.z_scr.ap()[:, ts(cch, 128)],
              reads=(self.z_scr,), writes=(zt,), transpose=True)
        self.act(lambda e: e.activation(zt.ap()[:, :], zt.ap()[:, :], AF.Silu),
                 reads=(zt,), writes=(zt,))
        self.pool(lambda e: e.tensor_mul(sb["o_tm"].ap()[:, :], sb["o_tm"].ap()[:, :],
                                         zt.ap()[:, :]),
                  reads=(sb["o_tm"], zt), writes=(sb["o_tm"],))
        st = self.stats
        self.act(lambda e: e.activation(sb["dump"].ap()[:, :], sb["o_tm"].ap()[:, :],
                                        AF.Square),
                 reads=(sb["o_tm"],), writes=(sb["dump"],))
        self.dve(lambda e: e.tensor_reduce(
            sb["ssy"].ap()[:, :].rearrange("p (g o) -> p g o", g=32),
            sb["dump"].ap()[:, :].rearrange("p (g n) -> p g n", g=32),
            axis=AX.X, op=OP.add),
            reads=(sb["dump"],), writes=(sb["ssy"],))
        self.act(lambda e: e.activation(sb["ssy"].ap()[:, :], sb["ssy"].ap()[:, :],
                                        AF.Ln, scale=1.0 / 128,
                                        bias=self.cb(1e-6).ap()[:, :]),
                 reads=(sb["ssy"], self.cb(1e-6)), writes=(sb["ssy"],))
        self.act(lambda e: e.activation(sb["ssy"].ap()[:, :], sb["ssy"].ap()[:, :],
                                        AF.Exp, scale=-0.5),
                 reads=(sb["ssy"],), writes=(sb["ssy"],))
        self.pool(lambda e: e.tensor_mul(
            sb["dump"].ap()[:, :].rearrange("p (g n) -> p g n", g=32),
            sb["o_tm"].ap()[:, :].rearrange("p (g n) -> p g n", g=32),
            ap_append0(sb["ssy"].ap()[:, :], 128)),
            reads=(sb["o_tm"], sb["ssy"]), writes=(sb["dump"],))
        yf3 = sb["yn_fm"].ap()[:, :].rearrange("p (ic t) -> p ic t", ic=32)
        for gg in range(8):
            ptb = self.pt[gg % 2]
            for j in range(4):
                ic = gg * 4 + j
                self.transpose(ptb, ptb.ap()[:, ts(j, 128)], sb["dump"].ap()[:, ts(ic, 128)],
                               self.identb.ap()[:, :], reads=(sb["dump"], self.identb))
            if gg % 2 == 0:
                self.dve(lambda e, gg=gg, ptb=ptb: e.tensor_copy(
                    yf3[:, gg * 4:(gg + 1) * 4, :],
                    ptb.ap()[:, 0:512].rearrange("p (a b) -> p a b", a=4)),
                    reads=(ptb,), writes=(sb["yn_fm"],))
            else:
                self.act(lambda e, gg=gg, ptb=ptb: e.activation(
                    yf3[:, gg * 4:(gg + 1) * 4, :],
                    ptb.ap()[:, 0:512].rearrange("p (a b) -> p a b", a=4), AF.Copy),
                    reads=(ptb,), writes=(sb["yn_fm"],))
        c.dma("qact", self.yn_scr.ap()[:, ts(cch, 4096)], sb["yn_fm"].ap()[:, :],
              reads=(sb["yn_fm"],), writes=(self.yn_scr,))

    def emit_l1_attn_loop(self):
        c = self.ctx
        with c.scope():
            sb = {}
            self.S_f32 = c.sbuf("S_f32", [128, HV * 128], F32)
            self.S_bf = c.sbuf("S_bf", [128, HV * 128], BF16)
            sb["qf"] = c.sbuf("qf", [128, 2048], BF16)
            sb["kf"] = c.sbuf("kf", [128, 2048], BF16)
            sb["k_tm"] = c.sbuf("k_tm1", [128, 2048], BF16)
            sb["v_tm"] = c.sbuf("v_tm1", [128, 4096], BF16)
            sb["hmw"] = c.sbuf("hmw", [32, 512], F32)
            for nm in ("wk_tm", "bl_tm", "liq_tm", "beta_tm", "lamL"):
                sb[nm] = c.sbuf(nm, [128, 32], F32)
            sb["lamrow"] = c.sbuf("lamrow", [1, 32], F32)
            sb["Lp"] = [c.sbuf(f"Lp{i}", [2, 1024], F32) for i in range(2)]
            sb["Rap"] = [c.sbuf(f"Rap{i}", [2, 1024], F32) for i in range(2)]
            sb["Rqp"] = [c.sbuf(f"Rqp{i}", [2, 1024], F32) for i in range(2)]
            for i in range(2):
                c.dma("qsp", sb["Lp"][i].ap()[1:2, :],
                      self.inp["ones_row"].ap()[:, 0:1024],
                      reads=(self.inp["ones_row"],), writes=(sb["Lp"][i],))
                self.dve(lambda e, i=i: e.memset(sb["Rap"][i].ap()[0:1, :], 1.0),
                         writes=(sb["Rap"][i],))
                self.dve(lambda e, i=i: e.memset(sb["Rqp"][i].ap()[0:1, :], 1.0),
                         writes=(sb["Rqp"][i],))
            sb["KKq"] = c.sbuf("KKq", [128, 512], F32)
            sb["dA"] = c.sbuf("dAx", [128, 512], F32)
            sb["At4"] = c.sbuf("At4", [128, 512], BF16)
            sb["A4"] = c.sbuf("A4", [128, 512], BF16)
            sb["Mqt4"] = c.sbuf("Mqt4", [128, 512], BF16)
            sb["V4"] = [c.sbuf(f"V4_{i}", [128, 512], BF16) for i in range(2)]
            sb["P"] = [c.sbuf(f"Pp{i}", [128, 512], BF16) for i in range(2)]
            sb["Pt"] = [c.sbuf(f"Ppt{i}", [128, 512], BF16) for i in range(2)]
            sb["kw4"] = c.sbuf("kw4", [128, 512], BF16)
            sb["tf1"] = c.sbuf("tf1", [128, 512], F32)
            sb["tf2"] = c.sbuf("tf2", [128, 512], F32)
            sb["o_tm"] = c.sbuf("o_tm", [128, 4096], F32)
            sb["zt"] = c.sbuf("ztl", [128, 4096], BF16)
            sb["dump"] = c.sbuf("dumpY", [128, 4096], BF16)
            sb["ssy"] = c.sbuf("ssy", [128, 32], F32)
            sb["yn_fm"] = c.sbuf("yn_fm", [128, 4096], BF16)
            self.dve(lambda e: e.memset(self.S_f32.ap()[:, :], 0.0), writes=(self.S_f32,))
            for cch in range(getattr(self, "l1_nch", NCH)):
                self.emit_l1_chunk(cch, sb)

    return (_l1_smalls, _l1_load_rows, emit_l1_chunk, _l1_group, _l1_ychunk,
            emit_l1_attn_loop)


(MK._l1_smalls, MK._l1_load_rows, MK.emit_l1_chunk, MK._l1_group, MK._l1_ychunk,
 MK.emit_l1_attn_loop) = _l1_loop_methods()


def build_l1_attn_program(dbg=()):
    nc = bass.Bass("TRN2", target_bir_lowering=False)
    mk = MK(nc, dbg=dbg)
    mk.ctx.dram("x1_in", [T, D], F32, kind="ExternalInput")
    mk.inp["x1_in"] = mk.ctx.bufs[-1]
    mk.din("maskcol", [128, 1])
    mk.din("ident_bf", [128, 128], BF16)
    mk.din("ident_f32", [128, 128])
    mk.din("masks3", [128, 512])
    mk.din("ones_row", [1, 2048])
    mk.declare_l1_io()
    mk.dout("y_out", [T, D])
    mk.ctx.alloc()
    mk.alloc_persist()
    mk.load_consts()
    with mk.ctx.scope():
        mk.alloc_l1_persist()
        mk.emit_l1_front(mk.inp["x1_in"])
        mk.emit_l1_prepass()
        mk.emit_l1_rows()
        mk.emit_l1_attn_loop()
    mk.emit_wout(zsrc=mk.yn_scr, w_key="wout1", nw_key="nw1",
                 res_buf=mk.inp["x1_in"], dest=mk.out["y_out"])
    mk.ctx.barrier_all()
    mk.ctx.finalize()
    return nc, mk


def host_l1_inputs(IN):
    f = np.float32
    shared = {
        "wqkv": tile_fm(np.asarray(IN["L1_attn_Wqkv"], f), 128),
        "wz": tile_fm(np.asarray(IN["L1_attn_Wz"], f), 128),
        "wab": np.concatenate([IN["L1_attn_Wa"], IN["L1_attn_Wb"]], axis=1).astype(f),
        "conv_w1": np.ascontiguousarray(IN["L1_attn_conv_w"]).astype(f),
        "alog": IN["L1_attn_A_log"].reshape(-1, 1).astype(f),
        "dtb1": IN["L1_attn_dt_bias"].reshape(-1, 1).astype(f),
        "nw1": np.ascontiguousarray(np.tile(IN["L1_attn_norm_w"], 32).reshape(32, 128).T).astype(f),
        "wout1": np.ascontiguousarray(IN["L1_attn_Wout"]).astype(f),
        "gwT": np.ascontiguousarray(IN["L1_ffn_gate_weight"].T).astype(f),
        "ebias": IN["L1_ffn_e_score_correction_bias"].reshape(1, -1).astype(f),
        "gate_wT": np.concatenate(
            [tile_fm(np.asarray(IN["L1_ffn_gate_w"][n], f).T, 512) for n in range(NE)]),
        "up_wT": np.concatenate(
            [tile_fm(np.asarray(IN["L1_ffn_up_w"][n], f).T, 512) for n in range(NE)]),
        "down_wT": np.concatenate(
            [tile_fm(np.asarray(IN["L1_ffn_down_w"][n], f).T, 512) for n in range(NE)]),
        "shgT": tile_fm(np.asarray(IN["L1_ffn_sh_gate_w"], f).T, 512),
        "shuT": tile_fm(np.asarray(IN["L1_ffn_sh_up_w"], f).T, 512),
        "shdT": tile_fm(np.asarray(IN["L1_ffn_sh_down_w"], f).T, 512),
    }
    return shared


def _moe_methods():
    def emit_moe_routing(self, xn_fm, xn3, sb, th):
        c = self.ctx
        gw = sb["gw_sb"]
        eb = sb["ebias_b"]
        for tt in range(8):
            gt = th * 8 + tt
            pa = self.pb[tt % 2]
            self.mm_acc(pa, pa.ap()[:, 0:16], [
                (xn3[:, dc, ts(tt, 128)], gw.ap()[:, dc * 16:(dc + 1) * 16],
                 (xn_fm, gw)) for dc in range(DC)])
            r = sb["rt"]
            self.act(lambda e, pa=pa: e.activation(r.ap()[:, 0:16], pa.ap()[:, 0:16],
                                                   AF.Sigmoid),
                     reads=(pa,), writes=(r,))
            self.dve(lambda e: e.tensor_add(r.ap()[:, 16:32], r.ap()[:, 0:16],
                                            eb.ap()[:, :]),
                     reads=(r, eb), writes=(r,))
            for gg in range(4):
                gsl = r.ap()[:, 16 + gg * 4:16 + (gg + 1) * 4]
                self.dve(lambda e, gsl=gsl, gg=gg: e.reduce_max(
                    r.ap()[:, 32 + gg:33 + gg], gsl, axis=AX.X),
                    reads=(r,), writes=(r,))
                self.dve(lambda e, gsl=gsl, gg=gg: e.tensor_single_scalar(
                    r.ap()[:, 48 + gg * 4:48 + (gg + 1) * 4], gsl,
                    r.ap()[:, 32 + gg:33 + gg], op=OP.is_ge),
                    reads=(r,), writes=(r,))
                self.dve(lambda e, gsl=gsl, gg=gg: e.scalar_tensor_tensor(
                    r.ap()[:, 48 + gg * 4:48 + (gg + 1) * 4],
                    r.ap()[:, 48 + gg * 4:48 + (gg + 1) * 4], NEG, gsl,
                    op0=OP.mult, op1=OP.add),
                    reads=(r,), writes=(r,))
                self.dve(lambda e, gg=gg: e.reduce_max(
                    r.ap()[:, 36 + gg:37 + gg],
                    r.ap()[:, 48 + gg * 4:48 + (gg + 1) * 4], axis=AX.X),
                    reads=(r,), writes=(r,))
            self.dve(lambda e: e.tensor_add(r.ap()[:, 40:44], r.ap()[:, 32:36],
                                            r.ap()[:, 36:40]),
                     reads=(r,), writes=(r,))
            self.dve(lambda e: e.reduce_max(r.ap()[:, 44:45], r.ap()[:, 40:44],
                                            axis=AX.X),
                     reads=(r,), writes=(r,))
            self.dve(lambda e: e.tensor_single_scalar(r.ap()[:, 48:52], r.ap()[:, 40:44],
                                                      r.ap()[:, 44:45], op=OP.is_ge),
                     reads=(r,), writes=(r,))
            self.dve(lambda e: e.scalar_tensor_tensor(r.ap()[:, 48:52], r.ap()[:, 48:52],
                                                      NEG, r.ap()[:, 40:44],
                                                      op0=OP.mult, op1=OP.add),
                     reads=(r,), writes=(r,))
            self.dve(lambda e: e.reduce_max(r.ap()[:, 44:45], r.ap()[:, 48:52],
                                            axis=AX.X),
                     reads=(r,), writes=(r,))
            self.dve(lambda e: e.tensor_single_scalar(r.ap()[:, 45:49], r.ap()[:, 40:44],
                                                      r.ap()[:, 44:45], op=OP.is_ge),
                     reads=(r,), writes=(r,))
            self.dve(lambda e: e.tensor_scalar(r.ap()[:, 52:56], r.ap()[:, 45:49],
                                               -NEG, NEG, op0=OP.mult, op1=OP.add),
                     reads=(r,), writes=(r,))
            self.dve(lambda e: e.tensor_add(
                r.ap()[:, 64:80].rearrange("p (g e) -> p g e", g=4),
                r.ap()[:, 16:32].rearrange("p (g e) -> p g e", g=4),
                ap_append0(r.ap()[:, 52:56], 4)),
                reads=(r,), writes=(r,))
            self.dve(lambda e: e.tensor_copy(r.ap()[:, 80:96], r.ap()[:, 64:80]),
                     reads=(r,), writes=(r,))
            for _ in range(3):
                self.dve(lambda e: e.reduce_max(r.ap()[:, 44:45], r.ap()[:, 64:80],
                                                axis=AX.X),
                         reads=(r,), writes=(r,))
                self.dve(lambda e: e.tensor_single_scalar(
                    r.ap()[:, 96:112], r.ap()[:, 64:80], r.ap()[:, 44:45], op=OP.is_ge),
                    reads=(r,), writes=(r,))
                self.dve(lambda e: e.scalar_tensor_tensor(
                    r.ap()[:, 64:80], r.ap()[:, 96:112], NEG, r.ap()[:, 64:80],
                    op0=OP.mult, op1=OP.add),
                    reads=(r,), writes=(r,))
            self.dve(lambda e: e.reduce_max(r.ap()[:, 44:45], r.ap()[:, 64:80],
                                            axis=AX.X),
                     reads=(r,), writes=(r,))
            self.dve(lambda e: e.tensor_single_scalar(r.ap()[:, 96:112], r.ap()[:, 80:96],
                                                      r.ap()[:, 44:45], op=OP.is_ge),
                     reads=(r,), writes=(r,))
            self.dve(lambda e: e.tensor_mul(r.ap()[:, 96:112], r.ap()[:, 96:112],
                                            r.ap()[:, 0:16]),
                     reads=(r,), writes=(r,))
            self.dve(lambda e: e.reduce_sum(r.ap()[:, 44:45], r.ap()[:, 96:112],
                                            axis=AX.X),
                     reads=(r,), writes=(r,))
            self.dve(lambda e: e.tensor_scalar_add(r.ap()[:, 44:45], r.ap()[:, 44:45],
                                                   1e-20),
                     reads=(r,), writes=(r,))
            self.dve(lambda e: e.reciprocal(r.ap()[:, 45:46], r.ap()[:, 44:45]),
                     reads=(r,), writes=(r,))
            self.dve(lambda e: e.tensor_scalar(r.ap()[:, 96:112], r.ap()[:, 96:112],
                                               r.ap()[:, 45:46], 2.5,
                                               op0=OP.mult, op1=OP.mult),
                     reads=(r,), writes=(r,))
            pt = self.pb[4]
            self.transpose(pt, pt.ap()[0:16, 0:128], r.ap()[:, 96:112],
                           self.identf.ap()[:, :], reads=(r, self.identf))
            self.dve(lambda e, pt=pt: e.tensor_copy(sb["wrow"].ap()[:, :],
                                                    pt.ap()[0:16, 0:128]),
                     reads=(pt,), writes=(sb["wrow"],))
            c.dma("qact", self.w_scr.ap()[:, ts(gt, 128)], sb["wrow"].ap()[:, :],
                  reads=(sb["wrow"],), writes=(self.w_scr,))

    def emit_moe(self, x2a_buf, dest):
        c = self.ctx
        self.w_scr = self.dscr("w_scr", [16, T], F32)
        with c.scope():
            sb = {}
            sb["gw_sb"] = c.sbuf("gw_sb", [128, DC * 16], BF16)
            sb["ebias_b"] = c.sbuf("ebias_b", [128, 16], F32)
            sb["rt"] = c.sbuf("rt", [128, 112], F32)
            sb["wrow"] = c.sbuf("wrow", [16, 128], F32)
            src = self.inp["gwT"].ap()[:, :].rearrange("(a p) c -> p a c", p=128)
            c.dma("qpool", sb["gw_sb"].ap()[:, :].rearrange("p (a c) -> p a c", a=DC),
                  src, reads=(self.inp["gwT"],), writes=(sb["gw_sb"],))
            ea = self.inp["ebias"].ap()
            c.dma("qsp", sb["ebias_b"].ap()[:, :],
                  bass.AP(ea.tensor, 0, [[0, 128], [1, 16]]),
                  reads=(self.inp["ebias"],), writes=(sb["ebias_b"],))
            xn_th = c.sbuf("xn_moe", [128, DC * 1024], BF16)
            acc = c.sbuf("acc_moe", [128, 8 * D], F32)
            wgu = [c.sbuf(f"wgu{i}", [128, DC * 512], BF16) for i in range(3)]
            wdn = [c.sbuf(f"wdn{i}", [128, 8 * 512], BF16) for i in range(2)]
            actn = c.sbuf("actn", [128, 8 * 1024], BF16)
            sgt = c.sbuf("sgt", [128, 512], F32)
            wut = c.sbuf("wut", [128, 512], F32)
            wbt = [c.sbuf(f"wbt{i}", [128, 1024], F32) for i in range(2)]
            outt = [c.sbuf(f"outt_m{i}", [128, 512], F32) for i in range(2)]
            x2t = [c.sbuf(f"x2t{i}", [128, 512], F32) for i in range(2)]
            ring = [0]

            def slab():
                r = wgu[ring[0] % 3]
                ring[0] += 1
                return r
            for th in range(2):
                xa = self.xn1_scr.ap()[:, :].rearrange("p (dc t) -> p dc t", dc=DC)
                c.dma("qsp", xn_th.ap()[:, :].rearrange("p (dc t) -> p dc t", dc=DC),
                      xa[:, :, th * 1024:(th + 1) * 1024],
                      reads=(self.xn1_scr,), writes=(xn_th,))
                xn3 = xn_th.ap()[:, :].rearrange("p (dc t) -> p dc t", dc=DC)
                self.emit_moe_routing(xn_th, xn3, sb, th)
                for n in range(NE):
                    wb = wbt[n % 2]
                    wa = self.w_scr.ap()
                    c.dma("qsp", wb.ap()[:, :],
                          bass.AP(wa.tensor, n * T + th * 1024, [[0, 128], [1, 1024]]),
                          reads=(self.w_scr,), writes=(wb,))
                    for fh2 in range(2):
                        pair = []
                        for wn in ("gate_wT", "up_wT"):
                            w = slab()
                            c.dma("qpool", w.ap()[:, :],
                                  self.inp[wn].ap()[ts(n * 2 + fh2, 128), :],
                                  reads=(self.inp[wn],), writes=(w,))
                            pair.append(w)
                        wg, wu = pair
                        for fcl in range(4):
                            fc = fh2 * 4 + fcl
                            w0v = wg.ap()[:, :].rearrange("p (a c) -> p a c", a=DC)
                            w1v = wu.ap()[:, :].rearrange("p (a c) -> p a c", a=DC)
                            self.mm_kouter(
                                [(self.pb[tgg], self.pb[tgg].ap()[:, :])
                                 for tgg in range(2)],
                                [(w0v[:, dc, ts(fcl, 128)],
                                  [xn3[:, dc, ts(tgg, 512)] for tgg in range(2)],
                                  (wg, xn_th)) for dc in range(DC)])
                            self.mm_kouter(
                                [(self.pb[2 + tgg], self.pb[2 + tgg].ap()[:, :])
                                 for tgg in range(2)],
                                [(w1v[:, dc, ts(fcl, 128)],
                                  [xn3[:, dc, ts(tgg, 512)] for tgg in range(2)],
                                  (wu, xn_th)) for dc in range(DC)])
                            for tgg in range(2):
                                pg = self.pb[tgg]
                                pu = self.pb[2 + tgg]
                                self.act(lambda e, pg=pg: e.activation(
                                    sgt.ap()[:, :], pg.ap()[:, :], AF.Silu),
                                    reads=(pg,), writes=(sgt,))
                                self.dve(lambda e, pu=pu, tgg=tgg, wb=wb: e.tensor_mul(
                                    wut.ap()[:, :], pu.ap()[:, :],
                                    wb.ap()[:, ts(tgg, 512)]),
                                    reads=(pu, wb), writes=(wut,))
                                self.dve(lambda e, fc=fc, tgg=tgg: e.tensor_mul(
                                    actn.ap()[:, fc * 1024 + tgg * 512:
                                              fc * 1024 + (tgg + 1) * 512],
                                    sgt.ap()[:, :], wut.ap()[:, :]),
                                    reads=(sgt, wut), writes=(actn,))
                    for ds in range(4):
                        wd = wdn[ds % 2]
                        c.dma("qpool", wd.ap()[:, :],
                              self.inp["down_wT"].ap()[ts(n * 4 + ds, 128), :],
                              reads=(self.inp["down_wT"],), writes=(wd,))
                        for tt in range(8):
                            pa = self.pb[4 + tt % 2]
                            self.mm_acc(pa, pa.ap()[:, :], [
                                (actn.ap()[:, fcc * 1024:(fcc + 1) * 1024]
                                 .rearrange("p (g t) -> p g t", g=2)
                                 [:, tt // 4, ts(tt % 4, 128)],
                                 wd.ap()[:, ts(fcc, 512)], (actn, wd))
                                for fcc in range(8)])
                            asl = acc.ap()[:, :].rearrange("p (tt d) -> p tt d", tt=8)
                            if n == 0:
                                self.dve(lambda e, pa=pa, tt=tt, ds=ds, asl=asl:
                                         e.tensor_copy(asl[:, tt, ts(ds, 512)],
                                                       pa.ap()[:, :]),
                                         reads=(pa,), writes=(acc,))
                            else:
                                self.dve(lambda e, pa=pa, tt=tt, ds=ds, asl=asl:
                                         e.tensor_add(asl[:, tt, ts(ds, 512)],
                                                      asl[:, tt, ts(ds, 512)],
                                                      pa.ap()[:, :]),
                                         reads=(pa, acc), writes=(acc,))
                self._moe_shared(th, xn_th, xn3, acc, slab, wdn, sgt, actn, c)
                for tt in range(8):
                    gt = th * 8 + tt
                    for ds in range(4):
                        asl = acc.ap()[:, :].rearrange("p (tt d) -> p tt d", tt=8)
                        xt = x2t[ds % 2]
                        ot = outt[ds % 2]
                        c.dma("qsp", xt.ap()[:, :],
                              x2a_buf.ap()[ts(gt, 128), ts(ds, 512)],
                              reads=(x2a_buf,), writes=(xt,))
                        self.dve(lambda e, tt=tt, ds=ds, asl=asl, xt=xt, ot=ot:
                                 e.tensor_add(ot.ap()[:, :], asl[:, tt, ts(ds, 512)],
                                              xt.ap()[:, :]),
                                 reads=(acc, xt), writes=(ot,))
                        c.dma("qact", dest.ap()[ts(gt, 128), ts(ds, 512)],
                              ot.ap()[:, :], reads=(ot,), writes=(dest,))

    def _moe_shared(self, th, xn_fm, xn3, acc, slab, wdn, sgt, actn, c):
        for fh in range(2):
            for fh2 in range(2):
                pair = []
                for wn in ("shgT", "shuT"):
                    w = slab()
                    c.dma("qpool", w.ap()[:, :],
                          self.inp[wn].ap()[ts(fh * 2 + fh2, 128), :],
                          reads=(self.inp[wn],), writes=(w,))
                    pair.append(w)
                wg, wu = pair
                for fcl in range(4):
                    fc = fh2 * 4 + fcl
                    w0v = wg.ap()[:, :].rearrange("p (a c) -> p a c", a=DC)
                    w1v = wu.ap()[:, :].rearrange("p (a c) -> p a c", a=DC)
                    self.mm_kouter(
                        [(self.pb[tgg], self.pb[tgg].ap()[:, :]) for tgg in range(2)],
                        [(w0v[:, dc, ts(fcl, 128)],
                          [xn3[:, dc, ts(tgg, 512)] for tgg in range(2)],
                          (wg, xn_fm)) for dc in range(DC)])
                    self.mm_kouter(
                        [(self.pb[2 + tgg], self.pb[2 + tgg].ap()[:, :])
                         for tgg in range(2)],
                        [(w1v[:, dc, ts(fcl, 128)],
                          [xn3[:, dc, ts(tgg, 512)] for tgg in range(2)],
                          (wu, xn_fm)) for dc in range(DC)])
                    for tgg in range(2):
                        pg = self.pb[tgg]
                        pu = self.pb[2 + tgg]
                        self.act(lambda e, pg=pg: e.activation(sgt.ap()[:, :],
                                                               pg.ap()[:, :], AF.Silu),
                                 reads=(pg,), writes=(sgt,))
                        self.dve(lambda e, pu=pu, fc=fc, tgg=tgg: e.tensor_mul(
                            actn.ap()[:, fc * 1024 + tgg * 512:
                                      fc * 1024 + (tgg + 1) * 512],
                            sgt.ap()[:, :], pu.ap()[:, :]),
                            reads=(sgt, pu), writes=(actn,))
            for ds in range(4):
                wd = wdn[ds % 2]
                c.dma("qpool", wd.ap()[:, :],
                      self.inp["shdT"].ap()[ts(ds, 128), ts(fh, 4096)],
                      reads=(self.inp["shdT"],), writes=(wd,))
                for tt in range(8):
                    pa = self.pb[4 + tt % 2]
                    self.mm_acc(pa, pa.ap()[:, :], [
                        (actn.ap()[:, fcc * 1024:(fcc + 1) * 1024]
                         .rearrange("p (g t) -> p g t", g=2)
                         [:, tt // 4, ts(tt % 4, 128)],
                         wd.ap()[:, ts(fcc, 512)], (actn, wd))
                        for fcc in range(8)])
                    asl = acc.ap()[:, :].rearrange("p (tt d) -> p tt d", tt=8)
                    self.dve(lambda e, pa=pa, tt=tt, ds=ds, asl=asl: e.tensor_add(
                        asl[:, tt, ts(ds, 512)], asl[:, tt, ts(ds, 512)], pa.ap()[:, :]),
                        reads=(pa, acc), writes=(acc,))

    return emit_moe_routing, emit_moe, _moe_shared


MK.emit_moe_routing, MK.emit_moe, MK._moe_shared = _moe_methods()


def build_l1_program(dbg=()):
    nc = bass.Bass("TRN2", target_bir_lowering=False)
    mk = MK(nc, dbg=dbg)
    mk.ctx.dram("x1_in", [T, D], F32, kind="ExternalInput")
    mk.inp["x1_in"] = mk.ctx.bufs[-1]
    mk.din("maskcol", [128, 1])
    mk.din("ident_bf", [128, 128], BF16)
    mk.din("ident_f32", [128, 128])
    mk.din("masks3", [128, 512])
    mk.din("ones_row", [1, 2048])
    mk.declare_l1_io()
    mk.dout("y_out", [T, D])
    mk.ctx.alloc()
    mk.alloc_persist()
    mk.load_consts()
    with mk.ctx.scope():
        mk.alloc_l1_persist()
        mk.emit_l1_front(mk.inp["x1_in"])
        mk.emit_l1_prepass()
        mk.emit_l1_rows()
        mk.emit_l1_attn_loop()
    mk.emit_wout(zsrc=mk.yn_scr, w_key="wout1", nw_key="nw1",
                 res_buf=mk.inp["x1_in"], dest=mk.x2a_scr)
    mk.emit_moe(mk.x2a_scr, mk.out["y_out"])
    mk.ctx.barrier_all()
    mk.ctx.finalize()
    return nc, mk


_CACHE = {}


def _get_programs():
    if "l0" not in _CACHE:
        _CACHE["l0"] = build_l0_program()
    if "l1" not in _CACHE:
        _CACHE["l1"] = build_l1_program()
    return _CACHE["l0"][0], _CACHE["l1"][0]


def kernel_two_launch(**inputs):
    IN = inputs
    f = np.float32
    nc0, nc1 = _get_programs()
    sh0 = host_l0_inputs(IN)
    maps0 = [core_l0_inputs(IN, sh0, c) for c in range(8)]
    res0 = run_bass_kernel_spmd(nc0, maps0, core_ids=list(range(8)))
    x1s = [res0.results[c]["y_out"] for c in range(8)]
    sh1 = host_l1_inputs(IN)
    maps1 = []
    for c in range(8):
        m = dict(sh1)
        for k in ("ident_bf", "ident_f32", "masks3", "ones_row"):
            m[k] = sh0[k]
        m["maskcol"] = np.ones((128, 1), f)
        m["x1_in"] = np.ascontiguousarray(x1s[c], dtype=f)
        maps1.append(m)
    res1 = run_bass_kernel_spmd(nc1, maps1, core_ids=list(range(8)))
    out = np.zeros((4, 4096, 2048), f)
    for c in range(8):
        out[c // 2, (c % 2) * T:((c % 2) + 1) * T] = res1.results[c]["y_out"]
    return out


def build_full_program(dbg=()):
    nc = bass.Bass("TRN2", target_bir_lowering=False)
    mk = MK(nc, dbg=dbg)
    mk.declare_io()
    mk.declare_l1_io()
    x1_full = mk.dscr("x1_full", [T, D], F32)
    mk.ctx.alloc()
    mk.alloc_persist()
    mk.load_consts()
    mk.emit_gamma()
    with mk.ctx.scope():
        mk.alloc_l0_wide()
        mk.emit_l0_attn()
    mk.emit_ffn(x1_full)
    with mk.ctx.scope():
        mk.alloc_l1_persist()
        mk.emit_l1_front(x1_full)
        mk.emit_l1_prepass()
        mk.emit_l1_rows()
        mk.emit_l1_attn_loop()
    mk.emit_wout(zsrc=mk.yn_scr, w_key="wout1", nw_key="nw1",
                 res_buf=x1_full, dest=mk.x2a_scr)
    mk.emit_moe(mk.x2a_scr, mk.out["y_out"])
    mk.ctx.barrier_all()
    mk.ctx.finalize()
    return nc, mk


def kernel_fused(**inputs):
    IN = inputs
    f = np.float32
    cu = np.asarray(IN["L1_attn_cu_seqlens"]).ravel()
    seg = int(cu[1]) - int(cu[0])
    assert seg == T and all(int(cu[i + 1]) - int(cu[i]) == seg
                            for i in range(len(cu) - 1)), "unexpected cu_seqlens"
    if "full" not in _CACHE:
        _CACHE["full"] = build_full_program()
    nc = _CACHE["full"][0]
    sh0 = host_l0_inputs(IN)
    sh1 = host_l1_inputs(IN)
    maps = []
    for c in range(8):
        m = core_l0_inputs(IN, sh0, c)
        m.update(sh1)
        maps.append(m)
    res = run_bass_kernel_spmd(nc, maps, core_ids=list(range(8)))
    out = np.zeros((4, 4096, 2048), f)
    for c in range(8):
        out[c // 2, (c % 2) * T:((c % 2) + 1) * T] = res.results[c]["y_out"]
    return out


def kernel(**inputs):
    return kernel_fused(**inputs)


def _opt_methods():
    def mm_kouter(self, banks, kterms):
        nk = len(kterms)
        nb = len(banks)
        all_reads = []
        for ki, (l, rhss, rb) in enumerate(kterms):
            for b in rb:
                if b not in all_reads:
                    all_reads.append(b)
            for bi in range(nb):
                pbuf, oap = banks[bi]
                tok = self.ctx.op(
                    "pe",
                    lambda eng, l=l, r=rhss[bi], o=oap, ki=ki, nk=nk: eng.matmul(
                        o, l, r, start=(ki == 0), stop=(ki == nk - 1)),
                    reads=rb, writes=tuple(p for p, _ in banks) if ki == 0 and bi == 0 else (),
                    inc=(ki == nk - 1 and bi == nb - 1), commit=False)
        for pbuf, _ in banks:
            self.ctx.commit(tok, reads=all_reads, writes=(pbuf,))
        return tok
    return (mm_kouter,)


(MK.mm_kouter,) = _opt_methods()


# version 06
# speedup vs baseline: 1.1417x; 1.1417x over previous
import contextlib
import math
import numpy as np

import concourse.bass as bass
import concourse.mybir as mybir
from concourse.bass_utils import run_bass_kernel_spmd

F32 = mybir.dt.float32
BF16 = mybir.dt.bfloat16
AF = mybir.ActivationFunctionType
OP = mybir.AluOpType
AX = mybir.AxisListType

NEG = -1e30


ENGINES = ("pe", "dve", "act", "pool", "sp")
DMA_QUEUES = ("qsp", "qact", "qpool")
QUEUE_ENGINE = {"qsp": "sp", "qact": "act", "qpool": "pool"}
QUEUE_TYPE = {"qsp": "hw", "qact": "hw", "qpool": "sw"}
N_HW_DMA_SEMS = 40
N_SW_DMA_SEMS = 14
DMA_KEYS = tuple(f"dh{i}" for i in range(N_HW_DMA_SEMS)) + tuple(
    f"dsw{i}" for i in range(N_SW_DMA_SEMS))


class Buf:
    __slots__ = ("name", "ap_fn", "last_w", "readers", "dma_key", "dma_qt")

    def __init__(self, name, ap_fn):
        self.name = name
        self.ap_fn = ap_fn
        self.last_w = None
        self.readers = []
        self.dma_key = None
        self.dma_qt = None

    def ap(self):
        return self.ap_fn()


class Ctx:
    def __init__(self, nc):
        self.nc = nc
        self.streams = {e: [] for e in ENGINES}
        self.cnt = {}
        self.sems = {}
        self.waited = {e: {} for e in ENGINES}
        self._exit = contextlib.ExitStack()
        for e in ENGINES:
            self.cnt[e] = 0
        for q in DMA_QUEUES:
            self.cnt[q] = 0
        self._sem_names = list(ENGINES) + list(DMA_KEYS)
        for k in DMA_KEYS:
            self.cnt[k] = 0
        self._free_dma = {"hw": [f"dh{i}" for i in range(N_HW_DMA_SEMS)],
                          "sw": [f"dsw{i}" for i in range(N_SW_DMA_SEMS)]}
        self.bufs = []
        self._scopes = [self._exit]
        self._scope_bufs = [[]]

    def alloc(self):
        for s in self._sem_names:
            self.sems[s] = self._exit.enter_context(self.nc.semaphore("sem_" + s))

    @contextlib.contextmanager
    def scope(self):
        st = contextlib.ExitStack()
        self._scopes.append(st)
        self._scope_bufs.append([])
        try:
            yield
        finally:
            self.barrier_all()
            self._scopes.pop()
            for b in self._scope_bufs.pop():
                if b.dma_key is not None:
                    self._free_dma[b.dma_qt].append(b.dma_key)
                    b.dma_key = None
            st.close()

    def sbuf_root(self, name, shape, dtype):
        self._uid = getattr(self, "_uid", 0) + 1
        name = f"{name}_u{self._uid}"
        t = self._scopes[0].enter_context(self.nc.sbuf_tensor(name, shape, dtype))
        b = Buf(name, lambda t=t: t)
        self.bufs.append(b)
        return b

    def sbuf(self, name, shape, dtype):
        self._uid = getattr(self, "_uid", 0) + 1
        name = f"{name}_u{self._uid}"
        t = self._scopes[-1].enter_context(self.nc.sbuf_tensor(name, shape, dtype))
        b = Buf(name, lambda t=t: t)
        self.bufs.append(b)
        self._scope_bufs[-1].append(b)
        return b

    def psum(self, name, shape, dtype=F32):
        t = self._exit.enter_context(self.nc.psum_tensor(name, shape, dtype))
        b = Buf(name, lambda t=t: t)
        self.bufs.append(b)
        return b

    def dram(self, name, shape, dtype, kind="Internal"):
        t = self.nc.dram_tensor(name, shape, dtype, kind=kind)
        b = Buf(name, lambda t=t: t.ap())
        self.bufs.append(b)
        return b

    def _waits_for(self, reads, writes):
        toks = []
        for b in reads:
            if b.last_w is not None:
                toks.append(b.last_w)
        for b in writes:
            if b.last_w is not None:
                toks.append(b.last_w)
            toks.extend(b.readers)
        best = {}
        for (k, c) in toks:
            if best.get(k, -1) < c:
                best[k] = c
        return best

    def _commit(self, tok, reads, writes):
        for b in reads:
            b.readers.append(tok)
        for b in writes:
            b.last_w = tok
            b.readers = []

    def op(self, engine, fn, reads=(), writes=(), inc=True, commit=True):
        best = self._waits_for(reads, writes)
        waited = self.waited[engine]
        needed = [(k, c) for (k, c) in best.items() if waited.get(k, 0) < c]
        for (k, c) in needed:
            waited[k] = c
        if inc:
            self.cnt[engine] += 1
        tok = (engine, self.cnt[engine])
        sems = self.sems

        def emit(eng, needed=tuple(needed), fn=fn, inc=inc, engine=engine):
            for (k, c) in needed:
                eng.wait_ge(sems[k], c * 16 if k in DMA_KEYS else c)
            instr = fn(eng)
            if inc:
                instr.then_inc(sems[engine], 1)
        self.streams[engine].append(emit)
        if commit:
            assert inc, "op(commit=True) requires inc=True"
            self._commit(tok, reads, writes)
        return tok

    def commit(self, tok, reads=(), writes=()):
        self._commit(tok, reads, writes)

    def _dma_key_for(self, buf, qt):
        if buf.dma_key is None:
            assert self._free_dma[qt], f"out of {qt} dma sems"
            buf.dma_key = self._free_dma[qt].pop()
            buf.dma_qt = qt
        assert buf.dma_qt == qt, f"{buf.name} used from both queue types"
        return buf.dma_key

    def dma(self, queue, out_ap, in_ap, reads=(), writes=(), **kw):
        engine = QUEUE_ENGINE[queue]
        key = self._dma_key_for(writes[0], QUEUE_TYPE[queue])
        best = self._waits_for(reads, writes)
        waited = self.waited[engine]
        needed = [(k, c) for (k, c) in best.items() if waited.get(k, 0) < c]
        for (k, c) in needed:
            waited[k] = c
        self.cnt[key] += 1
        tok = (key, self.cnt[key])
        sems = self.sems

        def emit(eng, needed=tuple(needed), key=key):
            for (k, c) in needed:
                eng.wait_ge(sems[k], c * 16 if k in DMA_KEYS else c)
            eng.dma_start(out=out_ap, in_=in_ap, **kw).then_inc(sems[key], 16)
        self.streams[engine].append(emit)
        self._commit(tok, reads, writes)
        return tok

    def raw(self, engine, fn):
        self.streams[engine].append(fn)

    def barrier_all(self, engines=ENGINES):
        snapshot = dict(self.cnt)
        sems = self.sems
        for e in engines:
            waited = self.waited[e]
            needed = []
            for k, c in snapshot.items():
                if c > 0 and waited.get(k, 0) < c:
                    needed.append((k, c))
                    waited[k] = c

            def emit(eng, needed=tuple(needed)):
                for (k, c) in needed:
                    eng.wait_ge(sems[k], c * 16 if k in DMA_KEYS else c)
            self.streams[e].append(emit)

    def finalize(self):
        nc = self.nc
        with nc.Block() as block:
            for key, attr in (("sp", "sync"), ("pe", "tensor"), ("dve", "vector"),
                              ("act", "scalar"), ("pool", "gpsimd")):
                stream = self.streams[key]
                if not stream:
                    continue

                def body(eng, stream=stream):
                    for fn in stream:
                        fn(eng)
                getattr(block, attr)(body)
        self._exit.close()


T = 2048
TT = T // 128
D = 2048
DC = D // 128
INTER = 4096
GN = 128
NH, HD = 64, 64
L0C = 128
NCH = T // L0C
CONV_DIM = INTER + 2 * GN
DIN_PAD = 8576
XBC0, XBC1 = INTER, INTER + CONV_DIM
DT0 = INTER + CONV_DIM
F1 = 5632
FC1 = F1 // 128
HK, HV, DK, DV = 16, 32, 128, 128
QKVD = 8192
L1C = 128
NE, FMOE, FSH = 16, 1024, 2048


def ts(i, n):
    return slice(i * n, (i + 1) * n)


def ap_append0(ap, n):
    return bass.AP(ap.tensor, ap.offset, list(ap.ap) + [[0, n]])


def ap_mid0(ap, n):
    a = list(ap.ap)
    return bass.AP(ap.tensor, ap.offset, [a[0], [0, n]] + a[1:])


def tile_fm(w, cols_per):
    Dd, C = w.shape
    nc_ = C // cols_per
    t = w.reshape(Dd // 128, 128, nc_, cols_per).transpose(2, 1, 0, 3)
    return np.ascontiguousarray(t).reshape(nc_ * 128, (Dd // 128) * cols_per)




class MK:

    def __init__(self, nc, dbg=()):
        self.nc = nc
        self.ctx = Ctx(nc)
        self.inp = {}
        self.out = {}
        self.scr = {}
        self.dbg = set(dbg)

    def din(self, name, shape, dtype=F32):
        b = self.ctx.dram(name, shape, dtype, kind="ExternalInput")
        self.inp[name] = b
        return b

    def dout(self, name, shape, dtype=F32):
        b = self.ctx.dram(name, shape, dtype, kind="ExternalOutput")
        self.out[name] = b
        return b

    def dscr(self, name, shape, dtype):
        b = self.ctx.dram(name, shape, dtype, kind="Internal")
        self.scr[name] = b
        return b

    def dbg_dump(self, name, src_buf, ap, shape, dtype=F32):
        if name in self.dbg:
            o = self.dout("dbg_" + name, shape, dtype)
            self.ctx.dma("qact", o.ap(), ap, reads=(src_buf,), writes=(o,))

    def cb(self, val):
        cache = getattr(self, "_cbcache", None)
        if cache is None:
            cache = self._cbcache = {}
        assert val in cache, f"const {val} not preallocated"
        return cache[val]

    def prealloc_consts(self, vals):
        cache = getattr(self, "_cbcache", None)
        if cache is None:
            cache = self._cbcache = {}
        for val in vals:
            if val in cache:
                continue
            b = self.ctx.sbuf("cbv", [128, 1], F32)
            self.dve(lambda e, b=b, val=val: e.memset(b.ap()[:, :], float(val)),
                     writes=(b,))
            cache[val] = b

    def dve(self, fn, reads=(), writes=()):
        return self.ctx.op("dve", fn, reads=reads, writes=writes)

    def act(self, fn, reads=(), writes=()):
        return self.ctx.op("act", fn, reads=reads, writes=writes)

    def pool(self, fn, reads=(), writes=()):
        return self.ctx.op("pool", fn, reads=reads, writes=writes)

    def pe(self, fn, reads=(), writes=()):
        return self.ctx.op("pe", fn, reads=reads, writes=writes)

    def mm_acc(self, psum_buf, out_ap, terms):
        n = len(terms)
        all_reads = []
        for i, (l, r, rb) in enumerate(terms):
            for b in rb:
                if b not in all_reads:
                    all_reads.append(b)
            tok = self.ctx.op(
                "pe",
                lambda eng, l=l, r=r, i=i, n=n: eng.matmul(
                    out_ap, l, r, start=(i == 0), stop=(i == n - 1)),
                reads=rb, writes=(psum_buf,) if i == 0 else (),
                inc=(i == n - 1), commit=False)
        self.ctx.commit(tok, reads=all_reads, writes=(psum_buf,))
        return tok

    def transpose(self, psum_buf, out_ap, in_ap, ident_ap, reads):
        return self.ctx.op(
            "pe", lambda eng: eng.transpose(out_ap, in_ap, ident_ap),
            reads=reads, writes=(psum_buf,))

    def declare_io(self):
        self.din("x_own", [T, D])
        self.din("x_prev", [T, D])
        self.din("maskcol", [128, 1])
        self.din("ident_bf", [128, 128], BF16)
        self.din("ident_f32", [128, 128])
        self.din("masks3", [128, 512])
        self.din("ones_row", [1, 2048])
        self.din("w_in", [67 * 128, D])
        self.din("conv_w0", [CONV_DIM, 4])
        self.din("conv_b0", [CONV_DIM, 1])
        self.din("dt_bias0", [NH, 1])
        self.din("a0", [NH, 1])
        self.din("d0_row", [1, NH])
        self.din("nw0", [128, 32])
        self.din("w_out", [INTER, D])
        self.din("ffn_w1t", [F1, D])
        self.din("ffn_w2t", [F1, D])
        self.din("ffn_w3t", [F1, D])
        self.dout("y_out", [T, D])
        self.x_scr = self.dscr("x_scr", [INTER, T], BF16)
        self.gate_scr = self.dscr("gate_scr", [INTER, T], BF16)
        self.cA_scr = self.dscr("cA_scr", [NH, T], F32)
        self.rr_scr = self.dscr("rr_scr", [NH, T], F32)
        self.zn_scr = self.dscr("zn_scr", [128, NCH * 32 * 128], BF16)
        self.x1a_scr = self.dscr("x1a_scr", [T, D], F32)
        self.xn_scr = self.dscr("xn_scr", [128, DC * T], BF16)
        self.h_scr = self.dscr("h_scr", [F1, T], BF16)
        self.q3_scr = self.dscr("q3_scr", [F1, D], BF16)

    def alloc_persist(self):
        c = self.ctx
        self.consts = c.sbuf("consts", [128, 512], F32)
        self.identb = c.sbuf("identb", [128, 128], BF16)
        self.identf = c.sbuf("identf", [128, 128], F32)
        self.onesr = c.sbuf("onesr", [1, 512], F32)
        self.maskc = c.sbuf("maskc", [128, 1], F32)
        self.stats = c.sbuf("stats", [128, 16], F32)
        self.gthr = c.sbuf("gthr", [128, 8], F32)
        self.prealloc_consts((1e-5, 1.0, 1e-6, 1e-20, 0.5))
        self.pb = [c.psum(f"pb{i}", [128, 512], F32) for i in range(6)]
        self.pt = [c.psum(f"pt{i}", [128, 1024], BF16) for i in range(2)]

    def load_consts(self):
        c = self.ctx
        for dst, srcn in ((self.consts, "masks3"), (self.identb, "ident_bf"),
                          (self.identf, "ident_f32"), (self.maskc, "maskcol")):
            c.dma("qsp", dst.ap()[:, :], self.inp[srcn].ap(), reads=(self.inp[srcn],),
                  writes=(dst,))
        c.dma("qsp", self.onesr.ap()[:, :], self.inp["ones_row"].ap()[:, 0:512],
              reads=(self.inp["ones_row"],), writes=(self.onesr,))

    def m_low0(self, rep=1):
        a = self.consts.ap()[:, 0:128]
        return ap_mid0(a, rep) if rep > 1 else a

    def mask(self, which, rep=1):
        i = {"low0": 0, "up0": 1, "upS": 2, "lowS": 3}[which]
        a = self.consts.ap()[:, i * 128:(i + 1) * 128]
        return ap_mid0(a, rep) if rep > 1 else a

    def kone(self, row_ap, out_buf, out_ap, reads):
        self.mm_acc(out_buf, out_ap,
                    [(self.onesr.ap()[0:1, 0:128], row_ap, tuple(reads) + (self.onesr,))])

    def emit_ln(self, src_buf, xn_fm, xld, xnstage, dump):
        c = self.ctx
        st = self.stats
        xn3 = xn_fm.ap()[:, :].rearrange("p (dc t) -> p dc t", dc=DC)
        for i in range(TT):
            xb = xld
            c.dma("qsp", xb.ap()[:, :], src_buf.ap()[ts(i, 128), :],
                  reads=(src_buf,), writes=(xb,))
            self.act(lambda e, xb=xb: e.activation(dump.ap()[:, 0:D], xb.ap()[:, :],
                                                   AF.Square, accum_out=st.ap()[:, 0:1]),
                     reads=(xb,), writes=(dump, st))
            self.dve(lambda e, xb=xb: e.reduce_sum(st.ap()[:, 1:2], xb.ap()[:, :], axis=AX.X),
                     reads=(xb,), writes=(st,))
            self.dve(lambda e: e.tensor_scalar_mul(st.ap()[:, 2:3], st.ap()[:, 1:2], -1.0 / D),
                     reads=(st,), writes=(st,))
            self.dve(lambda e: e.tensor_mul(st.ap()[:, 3:4], st.ap()[:, 2:3], st.ap()[:, 2:3]),
                     reads=(st,), writes=(st,))
            self.dve(lambda e: e.scalar_tensor_tensor(st.ap()[:, 4:5], st.ap()[:, 0:1],
                                                      1.0 / D, st.ap()[:, 3:4],
                                                      op0=OP.mult, op1=OP.subtract),
                     reads=(st,), writes=(st,))
            self.act(lambda e: e.activation(st.ap()[:, 5:6], st.ap()[:, 4:5], AF.Ln, bias=self.cb(1e-5).ap()[:, :]),
                     reads=(st, self.cb(1e-5)), writes=(st,))
            self.act(lambda e: e.activation(st.ap()[:, 6:7], st.ap()[:, 5:6], AF.Exp, scale=-0.5),
                     reads=(st,), writes=(st,))
            self.dve(lambda e, xb=xb: e.tensor_scalar(xnstage.ap()[:, 0:D], xb.ap()[:, :],
                                                      st.ap()[:, 2:3], st.ap()[:, 6:7],
                                                      op0=OP.add, op1=OP.mult),
                     reads=(xb, st), writes=(xnstage,))
            for g in range(4):
                ptb = self.pt[g % 2]
                for j in range(4):
                    dc = g * 4 + j
                    self.transpose(ptb, ptb.ap()[:, ts(j, 128)],
                                   xnstage.ap()[:, ts(dc, 128)], self.identb.ap()[:, :],
                                   reads=(xnstage, self.identb))
                self.dve(lambda e, g=g, i=i, ptb=ptb:
                         e.tensor_copy(xn3[:, g * 4:(g + 1) * 4, ts(i, 128)],
                                       ptb.ap()[:, 0:512].rearrange("p (a b) -> p a b", a=4)),
                         reads=(ptb,), writes=(xn_fm,))

    def emit_front(self, is_prev, xn_fm, sc):
        c = self.ctx
        w_ap = self.inp["w_in"].ap()
        w_buf = self.inp["w_in"]
        xn3 = xn_fm.ap()[:, :].rearrange("p (dc t) -> p dc t", dc=DC)
        for idx, cc in enumerate(range(32, 67)):
            wc = sc["wcol"][idx % 2]
            c.dma("qpool", wc.ap()[:, :], w_ap[ts(cc, 128), :],
                  reads=(w_buf,), writes=(wc,))
            self.mm_kouter(
                [(self.pb[tg], self.pb[tg].ap()[:, :]) for tg in range(4)],
                [(wc.ap()[:, ts(dc, 128)],
                  [xn3[:, dc, ts(tg, 512)] for tg in range(4)], (wc, xn_fm))
                 for dc in range(DC)])
            for tg in range(4):
                pa = self.pb[tg]
                if cc < 66:
                    self.act(lambda e, tg=tg, pa=pa: e.activation(
                        sc["conv_in"].ap()[:, 3 + tg * 512:3 + (tg + 1) * 512],
                        pa.ap()[:, :], AF.Copy),
                        reads=(pa,), writes=(sc["conv_in"],))
                else:
                    self.act(lambda e, tg=tg, pa=pa: e.activation(
                        sc["dtraw"].ap()[:, ts(tg, 512)], pa.ap()[0:64, :], AF.Copy),
                        reads=(pa,), writes=(sc["dtraw"],))
            if cc < 66:
                self.emit_conv_chunk(cc - 32, is_prev, sc)
        self.emit_dt_pipeline(is_prev, sc)

    def emit_conv_chunk(self, rel, is_prev, sc):
        c = self.ctx
        ci, cw, cb = sc["conv_in"], sc["cw_sb"], sc["cb_sb"]
        c.dma("qsp", cw.ap()[:, :], self.inp["conv_w0"].ap()[ts(rel, 128), :],
              reads=(self.inp["conv_w0"],), writes=(cw,))
        c.dma("qsp", cb.ap()[:, :], self.inp["conv_b0"].ap()[ts(rel, 128), :],
              reads=(self.inp["conv_b0"],), writes=(cb,))
        if is_prev:
            self.dve(lambda e: e.memset(ci.ap()[:, 0:3], 0.0), writes=(ci,))
        else:
            if rel < 34:
                self.dve(lambda e, rel=rel: e.tensor_copy(
                    ci.ap()[:, 0:3], self.halobuf.ap()[:, rel * 3:rel * 3 + 3]),
                    reads=(self.halobuf,), writes=(ci,))
        acc = sc["acc"]
        self.dve(lambda e: e.tensor_scalar(acc.ap()[:, 0:T], ci.ap()[:, 0:T],
                                           cw.ap()[:, 0:1], cb.ap()[:, 0:1],
                                           op0=OP.mult, op1=OP.add),
                 reads=(ci, cw, cb), writes=(acc,))
        for k in range(1, 4):
            self.dve(lambda e, k=k: e.scalar_tensor_tensor(
                acc.ap()[:, 0:T], ci.ap()[:, k:k + T], cw.ap()[:, k:k + 1],
                acc.ap()[:, 0:T], op0=OP.mult, op1=OP.add),
                reads=(ci, cw, acc), writes=(acc,))
        if is_prev and rel < 34:
            self.dve(lambda e, rel=rel: e.tensor_scalar_mul(
                self.halobuf.ap()[:, rel * 3:rel * 3 + 3], ci.ap()[:, T:T + 3],
                self.maskc.ap()[:, 0:1]),
                reads=(ci, self.maskc), writes=(self.halobuf,))
        if rel < 32:
            self.act(lambda e: e.activation(sc["xbcs"].ap()[:, :], acc.ap()[:, 0:T], AF.Silu),
                     reads=(acc,), writes=(sc["xbcs"],))
            c.dma("qact", self.x_scr.ap()[ts(rel, 128), :], sc["xbcs"].ap()[:, :],
                  reads=(sc["xbcs"],), writes=(self.x_scr,))
        elif rel == 32:
            self.act(lambda e: e.activation(self.Bfm.ap()[:, :], acc.ap()[:, 0:T], AF.Silu),
                     reads=(acc,), writes=(self.Bfm,))
        else:
            self.act(lambda e: e.activation(self.Cfm.ap()[:, :], acc.ap()[:, 0:T], AF.Silu),
                     reads=(acc,), writes=(self.Cfm,))

    def emit_dt_pipeline(self, is_prev, sc):
        c = self.ctx
        dtr, dtb = sc["dtraw"], sc["dtb_sb"]
        w0 = sc["dtw0"]; w1 = sc["dtw1"]
        self.dve(lambda e: e.tensor_scalar_add(w0.ap(), dtr.ap()[:, :], dtb.ap()[:, 0:1]),
                 reads=(dtr, dtb), writes=(w0,))
        self.act(lambda e: e.activation(w1.ap(), w0.ap(), AF.Abs),
                 reads=(w0,), writes=(w1,))
        self.act(lambda e: e.activation(w1.ap(), w1.ap(), AF.Exp, scale=-1.0),
                 reads=(w1,), writes=(w1,))
        self.act(lambda e: e.activation(w1.ap(), w1.ap(), AF.Ln,
                                        bias=self.cb(1.0).ap()[0:64, :]),
                 reads=(w1, self.cb(1.0)), writes=(w1,))
        self.dve(lambda e: e.tensor_scalar_max(w0.ap(), w0.ap(), 0.0),
                 reads=(w0,), writes=(w0,))
        self.dve(lambda e: e.tensor_add(self.dt_sb.ap()[:, :], w0.ap(), w1.ap()),
                 reads=(w0, w1), writes=(self.dt_sb,))
        self.act(lambda e: e.activation(w1.ap(), self.dt_sb.ap()[:, :], AF.Ln),
                 reads=(self.dt_sb,), writes=(w1,))
        self.dve(lambda e: e.tensor_scalar_mul(w0.ap(), self.dt_sb.ap()[:, :],
                                               sc["a0_sb"].ap()[:, 0:1]),
                 reads=(self.dt_sb, sc["a0_sb"]), writes=(w0,))
        for cch in range(NCH):
            self.dve(lambda e, cch=cch: e.tensor_tensor_scan(
                self.cA.ap()[:, ts(cch, 128)], w0.ap()[:, ts(cch, 128)],
                self.zeros64.ap()[:, :], 0.0, op0=OP.add, op1=OP.add),
                reads=(w0, self.zeros64), writes=(self.cA,))
        self.dve(lambda e: e.tensor_sub(w1.ap(), w1.ap(), self.cA.ap()[:, :]),
                 reads=(w1, self.cA), writes=(w1,))
        c.dma("qact", self.cA_scr.ap()[:, :], self.cA.ap()[:, :],
              reads=(self.cA,), writes=(self.cA_scr,))
        c.dma("qact", self.rr_scr.ap()[:, :], w1.ap(),
              reads=(w1,), writes=(self.rr_scr,))

    def flat_row(self, scr_buf, cch, part):
        t = scr_buf.ap().tensor
        return bass.AP(t, part * 16 * T + cch * 128, [[1, 1], [T, 16], [1, 128]])

    def emit_chunk_smalls(self, cch, need_decay, sb):
        w0 = sb["csc"]
        cAc = self.cA.ap()[:, ts(cch, 128)]
        last = self.cA.ap()[:, cch * 128 + 127:cch * 128 + 128]
        pa = self.pb[4]
        self.act(lambda e: e.activation(w0.ap()[:, 0:128], cAc, AF.Exp, scale=-1.0,
                                        bias=last),
                 reads=(self.cA,), writes=(w0,))
        self.dve(lambda e: e.tensor_mul(w0.ap()[:, 128:256], w0.ap()[:, 0:128],
                                        self.dt_sb.ap()[:, ts(cch, 128)]),
                 reads=(w0, self.dt_sb), writes=(w0,))
        self.transpose(pa, pa.ap()[:, 0:64], w0.ap()[:, 128:256],
                       self.identf.ap()[0:64, 0:64], reads=(w0, self.identf))
        self.dve(lambda e, pa=pa: e.tensor_copy(sb["w_tm"].ap()[:, :], pa.ap()[:, 0:64]),
                 reads=(pa,), writes=(sb["w_tm"],))
        if need_decay:
            self.act(lambda e: e.activation(w0.ap()[:, 256:384], cAc, AF.Exp),
                     reads=(self.cA,), writes=(w0,))
            self.transpose(pa, pa.ap()[:, 64:128], w0.ap()[:, 256:384],
                           self.identf.ap()[0:64, 0:64], reads=(w0, self.identf))
            self.dve(lambda e, pa=pa: e.tensor_copy(sb["expCA_tm"].ap()[:, :],
                                                    pa.ap()[:, 64:128]),
                     reads=(pa,), writes=(sb["expCA_tm"],))
        self.act(lambda e: e.activation(w0.ap()[:, 384:385], last, AF.Exp),
                 reads=(self.cA,), writes=(w0,))
        self.transpose(pa, pa.ap()[0:1, 128:192], w0.ap()[:, 384:385],
                       self.identf.ap()[0:64, 0:64], reads=(w0, self.identf))
        self.dve(lambda e, pa=pa: e.tensor_copy(sb["cdrow"].ap()[:, :], pa.ap()[0:1, 128:192]),
                 reads=(pa,), writes=(sb["cdrow"],))
        self.kone(sb["cdrow"].ap()[0:1, :], pa, pa.ap()[:, 192:256], (sb["cdrow"],))
        self.dve(lambda e, pa=pa: e.tensor_copy(sb["cdcols"].ap()[:, :], pa.ap()[:, 192:256]),
                 reads=(pa,), writes=(sb["cdcols"],))

    def emit_ssd_chunk(self, cch, full, sb):
        c = self.ctx
        x_tm = sb["x_tm"][cch % 2]
        c.dma("qsp", x_tm.ap()[:, :], self.x_scr.ap()[:, ts(cch, 128)],
              reads=(self.x_scr,), writes=(x_tm,), transpose=True)
        self.emit_chunk_smalls(cch, full, sb)
        ptb = self.pt[0]
        self.transpose(ptb, ptb.ap()[:, 0:128], self.Bfm.ap()[:, ts(cch, 128)],
                       self.identb.ap()[:, :], reads=(self.Bfm, self.identb))
        self.act(lambda e, ptb=ptb: e.activation(sb["B_tm"].ap()[:, :], ptb.ap()[:, 0:128],
                                                 AF.Copy),
                 reads=(ptb,), writes=(sb["B_tm"],))
        if full:
            pcb = self.pb[5]
            self.mm_acc(pcb, pcb.ap()[:, 0:128], [
                (self.Cfm.ap()[:, ts(cch, 128)], self.Bfm.ap()[:, ts(cch, 128)],
                 (self.Cfm, self.Bfm))])
            self.act(lambda e, pcb=pcb: e.activation(sb["CBt"].ap()[:, :],
                                                     pcb.ap()[:, 0:128], AF.Copy),
                     reads=(pcb,), writes=(sb["CBt"],))
            for part in range(2):
                self._load_rows(cch, part, sb)
        for g4 in range(16):
            if full and g4 in (4, 8):
                self._load_rows(cch, g4 // 4 + 1, sb)
            self._ssd_group(cch, g4, full, sb)

    def _load_rows(self, cch, part, sb):
        c = self.ctx
        l2 = sb["L2"][part % 2]
        r2 = sb["R2"][part % 2]
        c.dma("qsp", l2.ap()[0:1, :], self.flat_row(self.cA_scr, cch, part),
              reads=(self.cA_scr,), writes=(l2,))
        c.dma("qsp", r2.ap()[1:2, :], self.flat_row(self.rr_scr, cch, part),
              reads=(self.rr_scr,), writes=(r2,))

    def _ssd_group(self, cch, g4, full, sb):
        h0 = 4 * g4
        x_tm = sb["x_tm"][cch % 2]
        carry = self.carry
        xsl = x_tm.ap()[:, h0 * 64:(h0 + 4) * 64]
        csl = carry.ap()[:, h0 * 64:(h0 + 4) * 64]
        if full:
            l2 = sb["L2"][(h0 // 16) % 2]
            r2 = sb["R2"][(h0 // 16) % 2]
            pdiff = self.pb[g4 % 2]
            for j in range(4):
                hh = (h0 + j) % 16
                self.mm_acc(pdiff, pdiff.ap()[:, ts(j, 128)], [
                    (l2.ap()[0:2, ts(hh, 128)], r2.ap()[0:2, ts(hh, 128)], (l2, r2))])
            self.dve(lambda e, pdiff=pdiff: e.tensor_add(
                sb["D4"].ap()[:, :], pdiff.ap()[:, :], self.m_low0(rep=4)),
                reads=(pdiff, self.consts), writes=(sb["D4"],))
            self.act(lambda e: e.activation(sb["E4"].ap()[:, :], sb["D4"].ap()[:, :], AF.Exp),
                     reads=(sb["D4"],), writes=(sb["E4"],))
            self.dve(lambda e: e.tensor_mul(sb["M4"].ap()[:, :], sb["E4"].ap()[:, :],
                                            ap_mid0(sb["CBt"].ap()[:, :], 4)),
                     reads=(sb["E4"], sb["CBt"]), writes=(sb["M4"],))
        self.dve(lambda e: e.tensor_mul(
            sb["xw4"].ap()[:, :], xsl, ap_append0(sb["w_tm"].ap()[:, h0:h0 + 4], 64)),
            reads=(x_tm, sb["w_tm"]), writes=(sb["xw4"],))
        pstat = self.pb[3]
        self.mm_acc(pstat, pstat.ap()[:, 0:256], [
            (sb["B_tm"].ap()[:, :], sb["xw4"].ap()[:, 0:256],
             (sb["B_tm"], sb["xw4"]))])
        if full:
            pY = self.pb[2]
            for j in range(4):
                self.mm_acc(pY, pY.ap()[:, ts(j, 64)], [
                    (sb["M4"].ap()[:, ts(j, 128)],
                     x_tm.ap()[:, (h0 + j) * 64:(h0 + j + 1) * 64],
                     (sb["M4"], x_tm))])
            self.dve(lambda e: e.tensor_copy(sb["carry_bf"].ap()[:, :], csl),
                     reads=(carry,), writes=(sb["carry_bf"],))
            pI = self.pb[5]
            self.mm_acc(pI, pI.ap()[:, 0:256], [
                (self.Cfm.ap()[:, ts(cch, 128)],
                 sb["carry_bf"].ap()[:, 0:256],
                 (self.Cfm, sb["carry_bf"]))])
            self.dve(lambda e, pI=pI: e.tensor_mul(
                sb["t1"].ap()[:, :], pI.ap()[:, 0:256],
                ap_append0(sb["expCA_tm"].ap()[:, h0:h0 + 4], 64)),
                reads=(pI, sb["expCA_tm"]), writes=(sb["t1"],))
            self.dve(lambda e, pY=pY: e.tensor_add(sb["t2"].ap()[:, :], sb["t1"].ap()[:, :],
                                                   pY.ap()[:, 0:256]),
                     reads=(sb["t1"], pY), writes=(sb["t2"],))
            self.dve(lambda e: e.tensor_mul(
                sb["xD4"].ap()[:, :], xsl, ap_append0(sb["dcols"].ap()[:, h0:h0 + 4], 64)),
                reads=(x_tm, sb["dcols"]), writes=(sb["xD4"],))
            self.dve(lambda e: e.tensor_add(
                sb["Y_tm"].ap()[:, h0 * 64:(h0 + 4) * 64], sb["t2"].ap()[:, :],
                sb["xD4"].ap()[:, :]),
                reads=(sb["t2"], sb["xD4"]), writes=(sb["Y_tm"],))
        self.dve(lambda e: e.tensor_mul(
            sb["tscan"].ap()[:, :], csl, ap_append0(sb["cdcols"].ap()[:, h0:h0 + 4], 64)),
            reads=(carry, sb["cdcols"]), writes=(sb["tscan"],))
        self.dve(lambda e, pstat=pstat: e.tensor_add(csl, sb["tscan"].ap()[:, :],
                                                     pstat.ap()[:, 0:256]),
                 reads=(sb["tscan"], pstat), writes=(carry,))

    def emit_z_chunk(self, cch, sb):
        c = self.ctx
        gt = sb["gate_tile"][0]
        c.dma("qsp", gt.ap()[:, :],
              self.gate_scr.ap()[:, ts(cch, 128)], reads=(self.gate_scr,), writes=(gt,),
              transpose=True)
        self.act(lambda e, gt=gt: e.activation(gt.ap()[:, :], gt.ap()[:, :], AF.Silu),
                 reads=(gt,), writes=(gt,))
        self.dve(lambda e, gt=gt: e.tensor_mul(sb["Y_tm"].ap()[:, :], sb["Y_tm"].ap()[:, :],
                                               gt.ap()[:, :]),
                 reads=(sb["Y_tm"], gt), writes=(sb["Y_tm"],))
        st = self.stats
        self.act(lambda e: e.activation(sb["dump"].ap()[:, :], sb["Y_tm"].ap()[:, :],
                                        AF.Square, accum_out=st.ap()[:, 8:9]),
                 reads=(sb["Y_tm"],), writes=(sb["dump"], st))
        self.act(lambda e: e.activation(st.ap()[:, 9:10], st.ap()[:, 8:9], AF.Ln,
                                        scale=1.0 / INTER, bias=self.cb(1e-5).ap()[:, :]),
                 reads=(st, self.cb(1e-5)), writes=(st,))
        self.act(lambda e: e.activation(st.ap()[:, 10:11], st.ap()[:, 9:10], AF.Exp,
                                        scale=-0.5),
                 reads=(st,), writes=(st,))
        self.dve(lambda e: e.tensor_scalar_mul(sb["dump"].ap()[:, :], sb["Y_tm"].ap()[:, :],
                                               st.ap()[:, 10:11]),
                 reads=(sb["Y_tm"], st), writes=(sb["dump"],))
        zf3 = sb["zn_fm"].ap()[:, :].rearrange("p (ic t) -> p ic t", ic=32)
        for g in range(8):
            ptb = self.pt[g % 2]
            for j in range(4):
                ic = g * 4 + j
                self.transpose(ptb, ptb.ap()[:, ts(j, 128)], sb["dump"].ap()[:, ts(ic, 128)],
                               self.identb.ap()[:, :], reads=(sb["dump"], self.identb))
            if g % 2 == 0:
                self.dve(lambda e, g=g, ptb=ptb: e.tensor_copy(
                    zf3[:, g * 4:(g + 1) * 4, :], ptb.ap()[:, 0:512].rearrange(
                        "p (a b) -> p a b", a=4)),
                    reads=(ptb,), writes=(sb["zn_fm"],))
            else:
                self.act(lambda e, g=g, ptb=ptb: e.activation(
                    zf3[:, g * 4:(g + 1) * 4, :], ptb.ap()[:, 0:512].rearrange(
                        "p (a b) -> p a b", a=4), AF.Copy),
                    reads=(ptb,), writes=(sb["zn_fm"],))
        c.dma("qact", self.zn_scr.ap()[:, ts(cch, 4096)], sb["zn_fm"].ap()[:, :],
              reads=(sb["zn_fm"],), writes=(self.zn_scr,))

    def emit_gate(self, xn_fm, sc):
        c = self.ctx
        w_ap = self.inp["w_in"].ap()
        xn3 = xn_fm.ap()[:, :].rearrange("p (dc t) -> p dc t", dc=DC)
        for cc in range(32):
            wc = sc["wcol"][cc % 2]
            c.dma("qpool", wc.ap()[:, :], w_ap[ts(cc, 128), :],
                  reads=(self.inp["w_in"],), writes=(wc,))
            self.mm_kouter(
                [(self.pb[tg], self.pb[tg].ap()[:, :]) for tg in range(4)],
                [(wc.ap()[:, ts(dc, 128)],
                  [xn3[:, dc, ts(tg, 512)] for tg in range(4)], (wc, xn_fm))
                 for dc in range(DC)])
            for tg in range(4):
                pa = self.pb[tg]
                self.act(lambda e, pa=pa, tg=tg: e.activation(
                    sc["gstage"].ap()[:, ts(tg, 512)], pa.ap()[:, :], AF.Copy),
                    reads=(pa,), writes=(sc["gstage"],))
            c.dma("qact", self.gate_scr.ap()[ts(cc, 128), :],
                  sc["gstage"].ap()[:, :], reads=(sc["gstage"],),
                  writes=(self.gate_scr,))

    def emit_wout(self, zsrc=None, w_key="w_out", nw_key="nw0", res_buf=None,
                  dest=None):
        c = self.ctx
        if zsrc is None:
            zsrc = self.zn_scr
        if res_buf is None:
            res_buf = self.inp["x_own"]
        if dest is None:
            dest = self.x1a_scr
        with c.scope():
            nw_sb = c.sbuf("nw_sb", [128, 32], F32)
            c.dma("qsp", nw_sb.ap()[:, :], self.inp[nw_key].ap(),
                  reads=(self.inp[nw_key],), writes=(nw_sb,))
            wof = [c.sbuf(f"wof{i}", [128, 512], F32) for i in range(2)]
            wos = [c.sbuf(f"wos{i}", [128, 32 * 512], BF16) for i in range(2)]
            zld = [c.sbuf(f"zld{i}", [128, 4096], BF16) for i in range(2)]
            xres = [c.sbuf(f"xres{i}", [128, 512], F32) for i in range(2)]
            x1t = [c.sbuf(f"x1t{i}", [128, 512], F32) for i in range(2)]
            for ds in range(4):
                wo = wos[ds % 2]
                for ic in range(32):
                    wf = wof[ic % 2]
                    c.dma("qsp", wf.ap()[:, :],
                          self.inp[w_key].ap()[ts(ic, 128), ts(ds, 512)],
                          reads=(self.inp[w_key],), writes=(wf,))
                    self.dve(lambda e, wf=wf, ic=ic, wo=wo: e.tensor_scalar_mul(
                        wo.ap()[:, ts(ic, 512)], wf.ap()[:, :], nw_sb.ap()[:, ic:ic + 1]),
                        reads=(wf, nw_sb), writes=(wo,))
                for tt in range(TT):
                    zl = zld[tt % 2]
                    c.dma("qsp", zl.ap()[:, :], zsrc.ap()[:, ts(tt, 4096)],
                          reads=(zsrc,), writes=(zl,))
                    xr = xres[tt % 2]
                    c.dma("qsp", xr.ap()[:, :],
                          res_buf.ap()[ts(tt, 128), ts(ds, 512)],
                          reads=(res_buf,), writes=(xr,))
                    pa = self.pb[tt % 2]
                    self.mm_acc(pa, pa.ap()[:, :], [
                        (zl.ap()[:, ts(ic, 128)], wo.ap()[:, ts(ic, 512)], (zl, wo))
                        for ic in range(32)])
                    xt = x1t[tt % 2]
                    self.dve(lambda e, pa=pa, xr=xr, xt=xt: e.tensor_add(
                        xt.ap()[:, :], pa.ap()[:, :], xr.ap()[:, :]),
                        reads=(pa, xr), writes=(xt,))
                    c.dma("qact", dest.ap()[ts(tt, 128), ts(ds, 512)],
                          xt.ap()[:, :], reads=(xt,), writes=(dest,))

    def alloc_l0_wide(self):
        c = self.ctx
        self.carry = c.sbuf("carry", [128, NH * HD], F32)
        self.halobuf = c.sbuf("halobuf", [128, 3 * 34], F32)
        self.Bfm = c.sbuf("Bfm", [128, T], BF16)
        self.Cfm = c.sbuf("Cfm", [128, T], BF16)
        self.dt_sb = c.sbuf("dt_sb", [64, T], F32)
        self.cA = c.sbuf("cA", [64, T], F32)
        self.zeros64 = c.sbuf("zeros64", [64, 128], F32)
        self.dve(lambda e: e.memset(self.zeros64.ap()[:, :], 0.0), writes=(self.zeros64,))

    def emit_l0_attn(self):
        c = self.ctx
        for phase in ("prev", "own"):
            is_prev = phase == "prev"
            with c.scope():
                sc = {}
                xn_fm = c.sbuf("xn_fm", [128, DC * T], BF16)
                sc["wcol"] = [c.sbuf(f"wcol{i}", [128, D], BF16) for i in range(2)]
                sc["conv_in"] = c.sbuf("conv_in", [128, T + 3], F32)
                sc["xbcs"] = c.sbuf("xbcs", [128, T], BF16)
                sc["acc"] = c.sbuf("acc", [128, T], F32)
                sc["dtraw"] = c.sbuf("dtraw", [64, T], F32)
                sc["dtw0"] = DTView(sc["conv_in"], 0)
                sc["dtw1"] = DTView(sc["acc"], 0)
                sc["dump"] = c.sbuf("dumpA", [128, D], BF16)
                sc["cw_sb"] = c.sbuf("cw_sb", [128, 4], F32)
                sc["cb_sb"] = c.sbuf("cb_sb", [128, 1], F32)
                sc["dtb_sb"] = c.sbuf("dtb_sb", [64, 1], F32)
                sc["a0_sb"] = c.sbuf("a0_sb", [64, 1], F32)
                xld = c.sbuf("xld", [128, D], F32)
                xnstage = c.sbuf("xnstage", [128, D], BF16)
                c.dma("qsp", sc["dtb_sb"].ap()[:, :], self.inp["dt_bias0"].ap(),
                      reads=(self.inp["dt_bias0"],), writes=(sc["dtb_sb"],))
                c.dma("qsp", sc["a0_sb"].ap()[:, :], self.inp["a0"].ap(),
                      reads=(self.inp["a0"],), writes=(sc["a0_sb"],))
                src = self.inp["x_prev" if is_prev else "x_own"]
                self.emit_ln(src, Buf("xn_fm_view", xn_fm.ap_fn) if False else xn_fm,
                             xld, xnstage, sc["dump"])
                if not is_prev:
                    self.dbg_dump("xn_fm", xn_fm, xn_fm.ap()[:, :], [128, DC * T], BF16)
                self.emit_front(is_prev, xn_fm, sc)
                if not is_prev:
                    sc["gstage"] = c.sbuf("gstage", [128, 2048], BF16)
                    self.emit_gate(xn_fm, sc)
                    c.dma("qact", self.xn_scr.ap(), xn_fm.ap()[:, :],
                          reads=(xn_fm,), writes=(self.xn_scr,))
            with c.scope():
                sb = {}
                sb["x_tm"] = [c.sbuf(f"x_tm{i}", [128, INTER], BF16) for i in range(2)]
                sb["B_tm"] = c.sbuf("B_tm", [128, 128], BF16)
                sb["w_tm"] = c.sbuf("w_tm", [128, NH], F32)
                sb["expCA_tm"] = c.sbuf("expCA_tm", [128, NH], F32)
                sb["cdrow"] = c.sbuf("cdrow", [1, NH], F32)
                sb["cdcols"] = c.sbuf("cdcols", [128, NH], F32)
                sb["csc"] = c.sbuf("csc", [64, 512], F32)
                sb["L2"] = [c.sbuf(f"L2_{i}", [2, 2048], F32) for i in range(2)]
                sb["R2"] = [c.sbuf(f"R2_{i}", [2, 2048], F32) for i in range(2)]
                sb["CBt"] = c.sbuf("CBt", [128, 128], F32)
                sb["D4"] = c.sbuf("D4", [128, 512], F32)
                sb["E4"] = c.sbuf("E4", [128, 512], F32)
                sb["M4"] = c.sbuf("M4", [128, 512], BF16)
                sb["xw4"] = c.sbuf("xw4", [128, 256], BF16)
                sb["xD4"] = c.sbuf("xD4", [128, 256], F32)
                sb["t1"] = c.sbuf("t1", [128, 256], F32)
                sb["t2"] = c.sbuf("t2", [128, 256], F32)
                sb["tscan"] = c.sbuf("tscan", [128, 256], F32)
                for i in range(2):
                    c.dma("qsp", sb["L2"][i].ap()[1:2, :], self.inp["ones_row"].ap(),
                          reads=(self.inp["ones_row"],), writes=(sb["L2"][i],))
                    self.dve(lambda e, i=i: e.memset(sb["R2"][i].ap()[0:1, :], 1.0),
                             writes=(sb["R2"][i],))
                if is_prev:
                    self.dve(lambda e: e.memset(self.carry.ap()[:, :], 0.0),
                             writes=(self.carry,))
                else:
                    sb["Y_tm"] = c.sbuf("Y_tm", [128, INTER], F32)
                    sb["carry_bf"] = c.sbuf("carry_bf", [128, 256], BF16)
                    sb["gate_tile"] = [c.sbuf("gt0", [128, INTER], BF16)]
                    sb["dump"] = c.sbuf("dumpB", [128, INTER], BF16)
                    sb["zn_fm"] = c.sbuf("zn_fm", [128, INTER], BF16)
                    sb["drow"] = c.sbuf("drow", [1, NH], F32)
                    sb["dcols"] = c.sbuf("dcols", [128, NH], F32)
                    c.dma("qsp", sb["drow"].ap()[:, :], self.inp["d0_row"].ap(),
                          reads=(self.inp["d0_row"],), writes=(sb["drow"],))
                    pa = self.pb[4]
                    self.kone(sb["drow"].ap()[0:1, :], pa, pa.ap()[:, 0:64], (sb["drow"],))
                    self.dve(lambda e, pa=pa: e.tensor_copy(sb["dcols"].ap()[:, :],
                                                            pa.ap()[:, 0:64]),
                             reads=(pa,), writes=(sb["dcols"],))
                    self.dve(lambda e: e.tensor_scalar_mul(self.carry.ap()[:, :],
                                                           self.carry.ap()[:, :],
                                                           self.maskc.ap()[:, 0:1]),
                             reads=(self.carry, self.maskc), writes=(self.carry,))
                for cch in range(NCH):
                    self.emit_ssd_chunk(cch, not is_prev, sb)
                    if not is_prev:
                        self.emit_z_chunk(cch, sb)
                if not is_prev:
                    self.dbg_dump("Y_l0", sb["Y_tm"], sb["Y_tm"].ap()[:, :],
                                  [128, INTER], F32)
        self.emit_wout()


class DTView:
    def __init__(self, buf, p0):
        self.buf = buf
        self.p0 = p0
        self.name = buf.name + f"@{p0}"

    def ap(self):
        return self.buf.ap()[self.p0:self.p0 + 64, 0:T]

    @property
    def last_w(self):
        return self.buf.last_w

    @last_w.setter
    def last_w(self, v):
        self.buf.last_w = v

    @property
    def readers(self):
        return self.buf.readers

    @readers.setter
    def readers(self, v):
        self.buf.readers = v


class MKFFN:
    pass


def _ffn_methods():
    def emit_gamma(self):
        c = self.ctx
        with c.scope():
            gld = [c.sbuf(f"gld{i}", [128, 2048], F32) for i in range(2)]
            gs = c.sbuf("gs", [128, 8], F32)
            gtmp = c.sbuf("gtmp", [128, 2048], BF16)
            onescol = c.sbuf("onescol", [128, 1], F32)
            grow = c.sbuf("grow", [1, 8], F32)
            self.dve(lambda e: e.memset(onescol.ap()[:, :], 1.0), writes=(onescol,))
            self.dve(lambda e: e.memset(gs.ap()[:, :], 0.0), writes=(gs,))
            for wi, wname in enumerate(("ffn_w1t", "ffn_w2t", "ffn_w3t")):
                w = self.inp[wname]
                rows, cols = (F1, D)
                nch = cols // 2048 + (1 if cols % 2048 else 0)
                for rs in range(rows // 128):
                    for ch in range(nch):
                        c0 = ch * 2048
                        c1 = min(c0 + 2048, cols)
                        g = gld[(rs * nch + ch) % 2]
                        c.dma("qact", g.ap()[:, 0:c1 - c0], w.ap()[ts(rs, 128), c0:c1],
                              reads=(w,), writes=(g,))
                        self.act(lambda e, g=g, c0=c0, c1=c1, wi=wi: e.activation(
                            gtmp.ap()[:, 0:c1 - c0], g.ap()[:, 0:c1 - c0], AF.Abs,
                            accum_out=gs.ap()[:, 4:5]),
                            reads=(g,), writes=(gtmp, gs))
                        self.dve(lambda e, wi=wi: e.tensor_add(
                            gs.ap()[:, wi:wi + 1], gs.ap()[:, wi:wi + 1],
                            gs.ap()[:, 4:5]),
                            reads=(gs,), writes=(gs,))
            pa = self.pb[4]
            self.mm_acc(pa, pa.ap()[0:1, 0:3], [
                (onescol.ap()[:, :], gs.ap()[:, 0:3], (onescol, gs))])
            self.dve(lambda e, pa=pa: e.tensor_copy(grow.ap()[0:1, 0:3], pa.ap()[0:1, 0:3]),
                     reads=(pa,), writes=(grow,))
            self.kone(grow.ap()[0:1, 0:3], pa, pa.ap()[:, 8:11], (grow,))
            n1 = float(D * F1)
            for wi, n in enumerate((n1, n1, n1)):
                self.dve(lambda e, wi=wi, n=n, pa=pa: e.tensor_scalar(
                    self.gthr.ap()[:, wi:wi + 1], pa.ap()[:, 8 + wi:9 + wi],
                    0.5 / n, 0.5e-8, op0=OP.mult, op1=OP.add),
                    reads=(pa,), writes=(self.gthr,))
                self.dve(lambda e, wi=wi: e.tensor_scalar_mul(
                    self.gthr.ap()[:, 3 + wi:4 + wi], self.gthr.ap()[:, wi:wi + 1], -1.0),
                    reads=(self.gthr,), writes=(self.gthr,))

    def _quant(self, dst_buf, src_buf, wi, ta, tb):
        self.dve(lambda e: e.tensor_single_scalar(ta.ap()[:, :], src_buf.ap()[:, :],
                                                  self.gthr.ap()[:, wi:wi + 1], op=OP.is_gt),
                 reads=(src_buf, self.gthr), writes=(ta,))
        self.dve(lambda e: e.tensor_single_scalar(tb.ap()[:, :], src_buf.ap()[:, :],
                                                  self.gthr.ap()[:, 3 + wi:4 + wi], op=OP.is_lt),
                 reads=(src_buf, self.gthr), writes=(tb,))
        self.dve(lambda e: e.tensor_sub(dst_buf.ap()[:, :], ta.ap()[:, :], tb.ap()[:, :]),
                 reads=(ta, tb), writes=(dst_buf,))

    def emit_ffn(self, dest_scr):
        c = self.ctx
        with c.scope():
            xn_fm = c.sbuf("xn_fm1", [128, DC * T], BF16)
            c.dma("qsp", xn_fm.ap()[:, :], self.xn_scr.ap(), reads=(self.xn_scr,),
                  writes=(xn_fm,))
            xn3 = xn_fm.ap()[:, :].rearrange("p (dc t) -> p dc t", dc=DC)
            wld = [c.sbuf(f"wld{i}", [128, 2048], F32) for i in range(2)]
            qt = c.sbuf("qt", [128, 2048], BF16)
            qt2 = c.sbuf("qt2", [128, 2048], BF16)
            q1 = c.sbuf("q1", [128, 2048], BF16)
            q2 = c.sbuf("q2", [128, 2048], BF16)
            sgm = c.sbuf("sgm", [128, 512], F32)
            hst = c.sbuf("hst", [128, 512], BF16)
            for fc in range(FC1):
                for wi, (wn, qdst) in enumerate((("ffn_w1t", q1), ("ffn_w2t", q2))):
                    wl = wld[wi]
                    c.dma("qsp", wl.ap()[:, :], self.inp[wn].ap()[ts(fc, 128), :],
                          reads=(self.inp[wn],), writes=(wl,))
                    self._quant(qdst, wl, wi, qt, qt2)
                kt = []
                for dc in range(DC):
                    kt.append((q1.ap()[:, ts(dc, 128)],
                               [xn3[:, dc, ts(tg, 512)] for tg in range(4)], (q1, xn_fm)))
                banks1 = [(self.pb[tg], self.pb[tg].ap()[:, :]) for tg in range(4)]
                self.mm_kouter(banks1, kt)
                kt2 = []
                for dc in range(DC):
                    kt2.append((q2.ap()[:, ts(dc, 128)],
                                [xn3[:, dc, ts(tg, 512)] for tg in range(2)], (q2, xn_fm)))
                banks2 = [(self.pb[4 + tg], self.pb[4 + tg].ap()[:, :]) for tg in range(2)]
                self.mm_kouter(banks2, kt2)
                kt3 = []
                for dc in range(DC):
                    kt3.append((q2.ap()[:, ts(dc, 128)],
                                [xn3[:, dc, ts(2 + tg, 512)] for tg in range(2)], (q2, xn_fm)))
                for tg in range(4):
                    p1 = self.pb[tg]
                    if tg == 2:
                        self.mm_kouter(banks2, kt3)
                    p2 = self.pb[4 + tg % 2]
                    self.act(lambda e, p1=p1: e.activation(sgm.ap()[:, :], p1.ap()[:, :],
                                                           AF.Sigmoid),
                             reads=(p1,), writes=(sgm,))
                    self.dve(lambda e, p2=p2: e.tensor_mul(hst.ap()[:, :], sgm.ap()[:, :],
                                                           p2.ap()[:, :]),
                             reads=(sgm, p2), writes=(hst,))
                    c.dma("qact", self.h_scr.ap()[ts(fc, 128), ts(tg, 512)],
                          hst.ap()[:, :], reads=(hst,), writes=(self.h_scr,))
        with c.scope():
            wld = [c.sbuf(f"w3ld{i}", [128, 2048], F32) for i in range(2)]
            qt = c.sbuf("qt3", [128, 2048], BF16)
            qt2 = c.sbuf("qt32", [128, 2048], BF16)
            q3 = c.sbuf("q3s", [128, 2048], BF16)
            for fcc in range(FC1):
                wl = wld[fcc % 2]
                c.dma("qsp", wl.ap()[:, :], self.inp["ffn_w3t"].ap()[ts(fcc, 128), :],
                      reads=(self.inp["ffn_w3t"],), writes=(wl,))
                self._quant(q3, wl, 2, qt, qt2)
                c.dma("qact", self.q3_scr.ap()[ts(fcc, 128), :], q3.ap()[:, :],
                      reads=(q3,), writes=(self.q3_scr,))
        with c.scope():
            q3sb = c.sbuf("q3sb", [128, FC1 * 512], BF16)
            hblk = [c.sbuf(f"hblk{i}", [128, FC1 * 128], BF16) for i in range(2)]
            xres = [c.sbuf(f"xres{i}", [128, 512], F32) for i in range(2)]
            outt = c.sbuf("outt", [128, 512], F32)
            for dsl in range(4):
                for fcc in range(FC1):
                    c.dma("qpool", q3sb.ap()[:, ts(fcc, 512)],
                          self.q3_scr.ap()[ts(fcc, 128), ts(dsl, 512)],
                          reads=(self.q3_scr,), writes=(q3sb,))
                for tt in range(TT):
                    hb = hblk[tt % 2]
                    for qq in range(4):
                        src = self.h_scr.ap()[ts(qq, 1408), ts(tt, 128)].rearrange(
                            "(a p) t -> p a t", p=128)
                        c.dma("qsp", hb.ap()[:, :].rearrange(
                            "p (a t) -> p a t", a=FC1)[:, qq * 11:(qq + 1) * 11, :],
                            src, reads=(self.h_scr,), writes=(hb,))
                    xr = xres[tt % 2]
                    c.dma("qsp", xr.ap()[:, :],
                          self.x1a_scr.ap()[ts(tt, 128), ts(dsl, 512)],
                          reads=(self.x1a_scr,), writes=(xr,))
                    pa = self.pb[tt % 2]
                    self.mm_acc(pa, pa.ap()[:, :], [
                        (hb.ap()[:, ts(fcc, 128)], q3sb.ap()[:, ts(fcc, 512)], (hb, q3sb))
                        for fcc in range(FC1)])
                    self.dve(lambda e, pa=pa, xr=xr: e.tensor_add(
                        outt.ap()[:, :], pa.ap()[:, :], xr.ap()[:, :]),
                        reads=(pa, xr), writes=(outt,))
                    c.dma("qact", dest_scr.ap()[ts(tt, 128), ts(dsl, 512)],
                          outt.ap()[:, :], reads=(outt,), writes=(dest_scr,))

    return emit_gamma, _quant, emit_ffn


MK.emit_gamma, MK._quant, MK.emit_ffn = _ffn_methods()



def build_l0_program(dbg=()):
    nc = bass.Bass("TRN2", target_bir_lowering=False)
    mk = MK(nc, dbg=dbg)
    mk.declare_io()
    mk.ctx.alloc()
    mk.alloc_persist()
    mk.load_consts()
    mk.emit_gamma()
    with mk.ctx.scope():
        mk.alloc_l0_wide()
        mk.emit_l0_attn()
    mk.emit_ffn(mk.out["y_out"])
    mk.ctx.barrier_all()
    mk.ctx.finalize()
    return nc, mk


def host_l0_inputs(IN):
    f = np.float32
    import ml_dtypes
    bf = ml_dtypes.bfloat16
    r = np.arange(128)
    low0 = np.where(r[:, None] >= r[None, :], 0.0, NEG).astype(f)
    up0 = np.where(r[None, :] >= r[:, None], 0.0, NEG).astype(f)
    upS = np.where(r[None, :] > r[:, None], 0.0, NEG).astype(f)
    lowS = np.where(r[:, None] > r[None, :], 0.0, NEG).astype(f)
    shared = {
        "ident_bf": np.eye(128).astype(bf),
        "ident_f32": np.eye(128).astype(f),
        "masks3": np.concatenate([low0, up0, upS, lowS], axis=1),
        "ones_row": np.ones((1, 2048), f),
        "w_in": tile_fm(np.pad(np.asarray(IN["L0_attn_W_in"], f), ((0, 0), (0, 64))), 128),
        "conv_w0": np.ascontiguousarray(IN["L0_attn_conv_w"]).astype(f),
        "conv_b0": np.ascontiguousarray(IN["L0_attn_conv_b"]).reshape(-1, 1).astype(f),
        "dt_bias0": IN["L0_attn_dt_bias"].reshape(-1, 1).astype(f),
        "a0": IN["L0_attn_A"].reshape(-1, 1).astype(f),
        "d0_row": IN["L0_attn_D"].reshape(1, -1).astype(f),
        "nw0": np.ascontiguousarray(IN["L0_attn_norm_weight"].reshape(32, 128).T).astype(f),
        "w_out": np.ascontiguousarray(IN["L0_attn_W_out"]).astype(f),
        "ffn_w1t": tile_fm(np.asarray(IN["L0_ffn_w1"], f).T, 128),
        "ffn_w2t": tile_fm(np.asarray(IN["L0_ffn_w2"], f).T, 128),
        "ffn_w3t": np.ascontiguousarray(IN["L0_ffn_w3"].T).astype(f),
    }
    return shared


def core_l0_inputs(IN, shared, core):
    b, h = core // 2, core % 2
    x = IN["L0_attn_hidden_states"]
    d = dict(shared)
    d["x_own"] = np.ascontiguousarray(x[b, h * T:(h + 1) * T]).astype(np.float32)
    d["x_prev"] = np.ascontiguousarray(x[b, 0:T]).astype(np.float32)
    d["maskcol"] = np.full((128, 1), 1.0 if h == 1 else 0.0, np.float32)
    return d


def _l1_methods():
    def declare_l1_io(self):
        self.din("wqkv", [QKVD, D])
        self.din("wz", [4096, D])
        self.din("wab", [D, 64])
        self.din("conv_w1", [QKVD, 4])
        self.din("alog", [HV, 1])
        self.din("dtb1", [HV, 1])
        self.din("nw1", [128, 32])
        self.din("wout1", [4096, D])
        self.din("gwT", [D, 16])
        self.din("ebias", [1, 16])
        self.din("gate_wT", [NE * 2 * 128, DC * 512])
        self.din("up_wT", [NE * 2 * 128, DC * 512])
        self.din("down_wT", [NE * 4 * 128, 8 * 512])
        self.din("shgT", [4 * 128, DC * 512])
        self.din("shuT", [4 * 128, DC * 512])
        self.din("shdT", [4 * 128, DC * 512])
        self.qkv_scr = self.dscr("qkv_scr", [QKVD, T], BF16)
        self.z_scr = self.dscr("z_scr", [4096, T], BF16)
        self.yn_scr = self.dscr("yn_scr", [128, NCH * 32 * 128], BF16)
        self.lr_scr = self.dscr("lr_scr", [3 * HV, T], F32)
        self.x2a_scr = self.dscr("x2a_scr", [T, D], F32)
        self.xn1_scr = self.dscr("xn1_scr", [128, DC * T], BF16)

    def alloc_l1_persist(self):
        c = self.ctx
        self.cA1 = c.sbuf("cA1", [32, T], F32)
        self.beta_hm = c.sbuf("beta_hm", [32, T], F32)
        self.La_hm = c.sbuf("La_hm", [32, T], F32)
        self.Ra_hm = c.sbuf("Ra_hm", [32, T], F32)
        self.Rq_hm = c.sbuf("Rq_hm", [32, T], F32)
        self.g_hm = c.sbuf("g_hm", [32, T], F32)
        self.lnb_hm = c.sbuf("lnb_hm", [32, T], F32)
        self.lnik_hm = c.sbuf("lnik_hm", [16, T], F32)
        self.lniq_hm = c.sbuf("lniq_hm", [16, T], F32)
        self.zeros32 = c.sbuf("zeros32", [32, 128], F32)

    def emit_l1_front(self, x1_buf):
        c = self.ctx
        self.dve(lambda e: e.memset(self.zeros32.ap()[:, :], 0.0), writes=(self.zeros32,))
        with c.scope():
            sc = {}
            xn_fm = c.sbuf("xn1_fm", [128, DC * T], BF16)
            with c.scope():
                xld = c.sbuf("xld1", [128, D], F32)
                xnstage = c.sbuf("xnstage1", [128, D], BF16)
                dump = c.sbuf("dumpL1", [128, D], BF16)
                self.emit_ln(x1_buf, xn_fm, xld, xnstage, dump)
            c.dma("qact", self.xn1_scr.ap(), xn_fm.ap()[:, :],
                  reads=(xn_fm,), writes=(self.xn1_scr,))
            xn3 = xn_fm.ap()[:, :].rearrange("p (dc t) -> p dc t", dc=DC)
            self._l1_qkv_conv(xn3, xn_fm)
            self._l1_zab(xn3, xn_fm)

    def _l1_qkv_conv(self, xn3, xn_fm):
        c = self.ctx
        with c.scope():
            wcol = [c.sbuf(f"wcol1{i}", [128, D], BF16) for i in range(2)]
            conv_in = c.sbuf("conv_in1", [128, T + 3], F32)
            acc = c.sbuf("acc1", [128, T], F32)
            xbcs = c.sbuf("xbcs1", [128, T], BF16)
            cw = c.sbuf("cw1", [128, 4], F32)
            for cc in range(64):
                wc = wcol[cc % 2]
                c.dma("qpool", wc.ap()[:, :], self.inp["wqkv"].ap()[ts(cc, 128), :],
                      reads=(self.inp["wqkv"],), writes=(wc,))
                self.mm_kouter(
                    [(self.pb[tg], self.pb[tg].ap()[:, :]) for tg in range(4)],
                    [(wc.ap()[:, ts(dc, 128)],
                      [xn3[:, dc, ts(tg, 512)] for tg in range(4)], (wc, xn_fm))
                     for dc in range(DC)])
                for tg in range(4):
                    pa = self.pb[tg]
                    self.act(lambda e, tg=tg, pa=pa: e.activation(
                        conv_in.ap()[:, 3 + tg * 512:3 + (tg + 1) * 512],
                        pa.ap()[:, :], AF.Copy),
                        reads=(pa,), writes=(conv_in,))
                c.dma("qsp", cw.ap()[:, :], self.inp["conv_w1"].ap()[ts(cc, 128), :],
                      reads=(self.inp["conv_w1"],), writes=(cw,))
                self.dve(lambda e: e.memset(conv_in.ap()[:, 0:3], 0.0), writes=(conv_in,))
                self.dve(lambda e: e.tensor_scalar(acc.ap()[:, 0:T], conv_in.ap()[:, 0:T],
                                                   cw.ap()[:, 0:1], None, op0=OP.mult),
                         reads=(conv_in, cw), writes=(acc,))
                for k in range(1, 4):
                    self.dve(lambda e, k=k: e.scalar_tensor_tensor(
                        acc.ap()[:, 0:T], conv_in.ap()[:, k:k + T], cw.ap()[:, k:k + 1],
                        acc.ap()[:, 0:T], op0=OP.mult, op1=OP.add),
                        reads=(conv_in, cw, acc), writes=(acc,))
                self.act(lambda e: e.activation(xbcs.ap()[:, :], acc.ap()[:, 0:T], AF.Silu),
                         reads=(acc,), writes=(xbcs,))
                c.dma("qact", self.qkv_scr.ap()[ts(cc, 128), :], xbcs.ap()[:, :],
                      reads=(xbcs,), writes=(self.qkv_scr,))

    def _l1_zab(self, xn3, xn_fm):
        c = self.ctx
        with c.scope():
            wcol = [c.sbuf(f"wzc{i}", [128, D], BF16) for i in range(2)]
            gstage = c.sbuf("gstage1", [128, 2048], BF16)
            for cc in range(32):
                wc = wcol[cc % 2]
                c.dma("qpool", wc.ap()[:, :], self.inp["wz"].ap()[ts(cc, 128), :],
                      reads=(self.inp["wz"],), writes=(wc,))
                self.mm_kouter(
                    [(self.pb[tg], self.pb[tg].ap()[:, :]) for tg in range(4)],
                    [(wc.ap()[:, ts(dc, 128)],
                      [xn3[:, dc, ts(tg, 512)] for tg in range(4)], (wc, xn_fm))
                     for dc in range(DC)])
                for tg in range(4):
                    pa = self.pb[tg]
                    self.act(lambda e, pa=pa, tg=tg: e.activation(
                        gstage.ap()[:, ts(tg, 512)], pa.ap()[:, :], AF.Copy),
                        reads=(pa,), writes=(gstage,))
                c.dma("qact", self.z_scr.ap()[ts(cc, 128), :],
                      gstage.ap()[:, :], reads=(gstage,), writes=(self.z_scr,))
            wabs = c.sbuf("wabs", [128, DC * 64], BF16)
            wab3 = wabs.ap()[:, :].rearrange("p (dc g) -> p dc g", dc=DC)
            ab_hm = c.sbuf("ab_hm", [64, T], F32)
            abst = c.sbuf("abst", [128, 64], F32)
            for q in range(4):
                src = self.inp["wab"].ap()[ts(q, 512), :].rearrange(
                    "(a p) c -> p a c", p=128)
                c.dma("qpool", wab3[:, q * 4:(q + 1) * 4, :], src,
                      reads=(self.inp["wab"],), writes=(wabs,))
            for tt in range(TT):
                pa = self.pb[tt % 2]
                self.mm_acc(pa, pa.ap()[:, 0:64], [
                    (xn3[:, dc, ts(tt, 128)], wab3[:, dc, :], (xn_fm, wabs))
                    for dc in range(DC)])
                self.dve(lambda e, pa=pa: e.tensor_copy(abst.ap()[:, :], pa.ap()[:, 0:64]),
                         reads=(pa,), writes=(abst,))
                pt = self.pb[2 + tt % 2]
                self.transpose(pt, pt.ap()[0:64, 0:128], abst.ap()[:, :],
                               self.identf.ap()[:, :], reads=(abst, self.identf))
                self.dve(lambda e, pt=pt, tt=tt: e.tensor_copy(
                    ab_hm.ap()[:, ts(tt, 128)], pt.ap()[0:64, 0:128]),
                    reads=(pt,), writes=(ab_hm,))
            al = c.sbuf("al", [32, 2], F32)
            c.dma("qsp", al.ap()[:, 0:1], self.inp["alog"].ap(),
                  reads=(self.inp["alog"],), writes=(al,))
            c.dma("qsp", al.ap()[1:2, 1:2] if False else al.ap()[:, 1:2],
                  self.inp["dtb1"].ap(), reads=(self.inp["dtb1"],), writes=(al,))
            self.act(lambda e: e.activation(al.ap()[:, 0:1], al.ap()[:, 0:1], AF.Exp),
                     reads=(al,), writes=(al,))
            w0 = c.sbuf("spw0", [32, T], F32)
            w1 = c.sbuf("spw1", [32, T], F32)
            self.dve(lambda e: e.tensor_scalar_add(w0.ap()[:, :], ab_hm.ap()[0:32, :],
                                                   al.ap()[:, 1:2]),
                     reads=(ab_hm, al), writes=(w0,))
            self.act(lambda e: e.activation(w1.ap()[:, :], w0.ap()[:, :], AF.Abs),
                     reads=(w0,), writes=(w1,))
            self.act(lambda e: e.activation(w1.ap()[:, :], w1.ap()[:, :], AF.Exp, scale=-1.0),
                     reads=(w1,), writes=(w1,))
            self.act(lambda e: e.activation(w1.ap()[:, :], w1.ap()[:, :], AF.Ln,
                                            bias=self.cb(1.0).ap()[0:32, :]),
                     reads=(w1, self.cb(1.0)), writes=(w1,))
            self.dve(lambda e: e.tensor_scalar_max(w0.ap()[:, :], w0.ap()[:, :], 0.0),
                     reads=(w0,), writes=(w0,))
            self.dve(lambda e: e.tensor_add(w0.ap()[:, :], w0.ap()[:, :], w1.ap()[:, :]),
                     reads=(w0, w1), writes=(w0,))
            self.dve(lambda e: e.tensor_scalar(self.g_hm.ap()[:, :], w0.ap()[:, :],
                                               al.ap()[:, 0:1], -1.0,
                                               op0=OP.mult, op1=OP.mult),
                     reads=(w0, al), writes=(self.g_hm,))
            for cch in range(NCH):
                self.dve(lambda e, cch=cch: e.tensor_tensor_scan(
                    self.cA1.ap()[:, ts(cch, 128)], self.g_hm.ap()[:, ts(cch, 128)],
                    self.zeros32.ap()[:, :], 0.0, op0=OP.add, op1=OP.add),
                    reads=(self.g_hm, self.zeros32), writes=(self.cA1,))
            self.dve(lambda e: e.tensor_copy(w0.ap()[:, :], ab_hm.ap()[32:64, :]),
                     reads=(ab_hm,), writes=(w0,))
            self.act(lambda e: e.activation(self.beta_hm.ap()[:, :], w0.ap()[:, :],
                                            AF.Sigmoid),
                     reads=(w0,), writes=(self.beta_hm,))
            self.act(lambda e: e.activation(w1.ap()[:, :], w0.ap()[:, :], AF.Abs),
                     reads=(w0,), writes=(w1,))
            self.act(lambda e: e.activation(w1.ap()[:, :], w1.ap()[:, :], AF.Exp, scale=-1.0),
                     reads=(w1,), writes=(w1,))
            self.act(lambda e: e.activation(w1.ap()[:, :], w1.ap()[:, :], AF.Ln,
                                            bias=self.cb(1.0).ap()[0:32, :]),
                     reads=(w1, self.cb(1.0)), writes=(w1,))
            self.dve(lambda e: e.tensor_scalar(w0.ap()[:, :], w0.ap()[:, :],
                                               -1.0, 0.0, op0=OP.mult, op1=OP.max),
                     reads=(w0,), writes=(w0,))
            self.dve(lambda e: e.scalar_tensor_tensor(self.lnb_hm.ap()[:, :],
                                                      w0.ap()[:, :], -1.0, w1.ap()[:, :],
                                                      op0=OP.bypass, op1=OP.add),
                     reads=(w0, w1), writes=(self.lnb_hm,))
            self.dve(lambda e: e.tensor_scalar_mul(self.lnb_hm.ap()[:, :],
                                                   self.lnb_hm.ap()[:, :], -1.0),
                     reads=(self.lnb_hm,), writes=(self.lnb_hm,))
    return declare_l1_io, alloc_l1_persist, emit_l1_front, _l1_qkv_conv, _l1_zab


(MK.declare_l1_io, MK.alloc_l1_persist, MK.emit_l1_front, MK._l1_qkv_conv,
 MK._l1_zab) = _l1_methods()


def _l1_chunk_methods():
    def emit_l1_prepass(self):
        c = self.ctx
        with c.scope():
            qk_tm = [c.sbuf(f"qkt{i}", [128, 2048], BF16) for i in range(2)]
            sqf = c.sbuf("sqf", [128, 2048], F32)
            ss = c.sbuf("ssn", [128, 16], F32)
            for cch in range(NCH):
                for qi, dst in ((0, self.lniq_hm), (1, self.lnik_hm)):
                    qt = qk_tm[qi]
                    c.dma("qsp", qt.ap()[:, :],
                          self.qkv_scr.ap()[qi * 2048:(qi + 1) * 2048, ts(cch, 128)],
                          reads=(self.qkv_scr,), writes=(qt,), transpose=True)
                    self.act(lambda e, qt=qt: e.activation(sqf.ap()[:, :], qt.ap()[:, :],
                                                           AF.Square),
                             reads=(qt,), writes=(sqf,))
                    self.dve(lambda e: e.tensor_reduce(
                        ss.ap()[:, :].rearrange("p (g o) -> p g o", g=16),
                        sqf.ap()[:, :].rearrange("p (g n) -> p g n", g=16),
                        axis=AX.X, op=OP.add),
                        reads=(sqf,), writes=(ss,))
                    self.act(lambda e: e.activation(ss.ap()[:, :], ss.ap()[:, :], AF.Ln,
                                                    bias=self.cb(1e-6).ap()[:, :]),
                             reads=(ss, self.cb(1e-6)), writes=(ss,))
                    if qi == 0:
                        self.dve(lambda e: e.tensor_scalar(
                            ss.ap()[:, :], ss.ap()[:, :], float(np.log(128.0)), -0.5,
                            op0=OP.add, op1=OP.mult),
                            reads=(ss,), writes=(ss,))
                    else:
                        self.dve(lambda e: e.tensor_scalar_mul(ss.ap()[:, :],
                                                               ss.ap()[:, :], -0.5),
                                 reads=(ss,), writes=(ss,))
                    pt = self.pb[4]
                    self.transpose(pt, pt.ap()[0:16, 0:128], ss.ap()[:, :],
                                   self.identf.ap()[:, :], reads=(ss, self.identf))
                    self.dve(lambda e, pt=pt, dst=dst, cch=cch: e.tensor_copy(
                        dst.ap()[:, ts(cch, 128)], pt.ap()[0:16, 0:128]),
                        reads=(pt,), writes=(dst,))

    def emit_l1_rows(self):
        c = self.ctx
        ln_scr = self.dscr(f"ln_scr_{self.ctx.cnt['dve']}", [32, T], F32)
        c.dma("qact", ln_scr.ap()[0:16, :], self.lnik_hm.ap()[:, :],
              reads=(self.lnik_hm,), writes=(ln_scr,))
        c.dma("qact", ln_scr.ap()[16:32, :], self.lniq_hm.ap()[:, :],
              reads=(self.lniq_hm,), writes=(ln_scr,))
        with c.scope():
            ikrep = c.sbuf("ikrep", [32, T], F32)
            iqrep = c.sbuf("iqrep", [32, T], F32)
            t = ln_scr.ap().tensor
            c.dma("qsp", ikrep.ap()[:, :], bass.AP(t, 0, [[T, 16], [0, 2], [1, T]]),
                  reads=(ln_scr,), writes=(ikrep,))
            c.dma("qsp", iqrep.ap()[:, :], bass.AP(t, 16 * T, [[T, 16], [0, 2], [1, T]]),
                  reads=(ln_scr,), writes=(iqrep,))
            self.dve(lambda e: e.tensor_sub(self.La_hm.ap()[:, :], ikrep.ap()[:, :],
                                            self.cA1.ap()[:, :]),
                     reads=(ikrep, self.cA1), writes=(self.La_hm,))
            self.dve(lambda e: e.tensor_add(self.Ra_hm.ap()[:, :], ikrep.ap()[:, :],
                                            self.cA1.ap()[:, :]),
                     reads=(ikrep, self.cA1), writes=(self.Ra_hm,))
            self.dve(lambda e: e.tensor_add(self.Ra_hm.ap()[:, :], self.Ra_hm.ap()[:, :],
                                            self.lnb_hm.ap()[:, :]),
                     reads=(self.Ra_hm, self.lnb_hm), writes=(self.Ra_hm,))
            self.dve(lambda e: e.tensor_add(self.Rq_hm.ap()[:, :], iqrep.ap()[:, :],
                                            self.cA1.ap()[:, :]),
                     reads=(iqrep, self.cA1), writes=(self.Rq_hm,))
        for i, b in enumerate((self.La_hm, self.Ra_hm, self.Rq_hm)):
            c.dma("qact", self.lr_scr.ap()[i * HV:(i + 1) * HV, :], b.ap()[:, :],
                  reads=(b,), writes=(self.lr_scr,))

    def lr_row(self, which, cch, part):
        t = self.lr_scr.ap().tensor
        base = {"La": 0, "Ra": HV, "Rq": 2 * HV}[which] * T
        return bass.AP(t, base + part * 8 * T + cch * 128, [[1, 1], [T, 8], [1, 128]])

    return emit_l1_prepass, emit_l1_rows, lr_row


MK.emit_l1_prepass, MK.emit_l1_rows, MK.lr_row = _l1_chunk_methods()


def _l1_loop_methods():
    def _l1_smalls(self, cch, sb):
        hmw = sb["hmw"]
        pt = self.pb[4]
        last = self.cA1.ap()[:, cch * 128 + 127:cch * 128 + 128]
        specs = [(self.La_hm, 0, "wk_tm", last), (self.Ra_hm, 128, "bl_tm", None),
                 (self.Rq_hm, 256, "liq_tm", None), (self.beta_hm, 384, "beta_tm", None)]
        for (srcb, off, dst, bias) in specs:
            if dst == "beta_tm":
                self.dve(lambda e, srcb=srcb, off=off: e.tensor_copy(
                    hmw.ap()[:, off:off + 128], srcb.ap()[:, ts(cch, 128)]),
                    reads=(srcb,), writes=(hmw,))
            else:
                bb = 0.0 if bias is None else bias
                self.act(lambda e, srcb=srcb, off=off, bb=bb: e.activation(
                    hmw.ap()[:, off:off + 128], srcb.ap()[:, ts(cch, 128)], AF.Exp,
                    bias=bb),
                    reads=(srcb, self.cA1), writes=(hmw,))
            self.transpose(pt, pt.ap()[:, off // 4:off // 4 + 32],
                           hmw.ap()[:, off:off + 128], self.identf.ap()[0:32, 0:32],
                           reads=(hmw, self.identf))
            self.dve(lambda e, dst=dst, off=off, pt=pt: e.tensor_copy(
                sb[dst].ap()[:, :], pt.ap()[:, off // 4:off // 4 + 32]),
                reads=(pt,), writes=(sb[dst],))
        self.act(lambda e: e.activation(hmw.ap()[:, 384:385] if False else hmw.ap()[:, 440:441],
                                        last, AF.Exp),
                 reads=(self.cA1,), writes=(hmw,))
        self.transpose(pt, pt.ap()[0:1, 160:192], hmw.ap()[:, 440:441],
                       self.identf.ap()[0:32, 0:32], reads=(hmw, self.identf))
        self.dve(lambda e, pt=pt: e.tensor_copy(sb["lamrow"].ap()[:, :],
                                                pt.ap()[0:1, 160:192]),
                 reads=(pt,), writes=(sb["lamrow"],))
        self.kone(sb["lamrow"].ap()[0:1, :], pt, pt.ap()[:, 192:224], (sb["lamrow"],))
        self.dve(lambda e, pt=pt: e.tensor_copy(sb["lamL"].ap()[:, :], pt.ap()[:, 192:224]),
                 reads=(pt,), writes=(sb["lamL"],))

    def _l1_load_rows(self, cch, part, sb):
        c = self.ctx
        p = part % 2
        for name, which, row in (("Lp", "La", 0), ("Rap", "Ra", 1), ("Rqp", "Rq", 1)):
            tile = sb[name][p]
            c.dma("qsp", tile.ap()[row:row + 1, :], self.lr_row(which, cch, part),
                  reads=(self.lr_scr,), writes=(tile,))

    def emit_l1_chunk(self, cch, sb):
        c = self.ctx
        qf, kf, k_tm, v_tm = sb["qf"], sb["kf"], sb["k_tm"], sb["v_tm"]
        for dst, r0, r1, tr in ((qf, 0, 2048, False), (kf, 2048, 4096, False)):
            for q in range(2):
                src = self.qkv_scr.ap()[r0 + q * 1024:r0 + (q + 1) * 1024,
                                        ts(cch, 128)].rearrange("(a p) t -> p a t", p=128)
                c.dma("qsp", dst.ap()[:, :].rearrange("p (a t) -> p a t", a=16)
                      [:, q * 8:(q + 1) * 8, :], src,
                      reads=(self.qkv_scr,), writes=(dst,))
        c.dma("qact", k_tm.ap()[:, :], self.qkv_scr.ap()[2048:4096, ts(cch, 128)],
              reads=(self.qkv_scr,), writes=(k_tm,), transpose=True)
        c.dma("qact", v_tm.ap()[:, :], self.qkv_scr.ap()[4096:8192, ts(cch, 128)],
              reads=(self.qkv_scr,), writes=(v_tm,), transpose=True)
        self._l1_smalls(cch, sb)
        self.dve(lambda e: e.tensor_copy(self.S_bf.ap()[:, :], self.S_f32.ap()[:, :]),
                 reads=(self.S_f32,), writes=(self.S_bf,))
        self._l1_load_rows(cch, 0, sb)
        self._l1_load_rows(cch, 1, sb)
        import itertools
        for pr in range(4):
            if pr in (1, 2):
                self._l1_load_rows(cch, pr + 1, sb)
            it0 = self._l1_group_stages(cch, 2 * pr, sb, sb["PS"][0])
            it1 = self._l1_group_stages(cch, 2 * pr + 1, sb, sb["PS"][1])
            for _ in itertools.zip_longest(it0, it1):
                pass
        self._l1_ychunk(cch, sb)

    def _l1_group_stages(self, cch, g, sb, P):
        h0 = 4 * g
        p = (g // 2) % 2
        Lp, Rap, Rqp = sb["Lp"][p], sb["Rap"][p], sb["Rqp"][p]
        qf, kf, k_tm, v_tm = sb["qf"], sb["kf"], sb["k_tm"], sb["v_tm"]
        ph0 = (h0 % 8)
        bA, bB, bC = P["banks"]
        for j, kk in enumerate((g * 2, g * 2 + 1)):
            self.mm_acc(bA, bA.ap()[:, ts(j, 128)], [
                (kf.ap()[:, ts(kk, 128)], kf.ap()[:, ts(kk, 128)], (kf,))])
            self.mm_acc(bA, bA.ap()[:, 256 + j * 128:256 + (j + 1) * 128], [
                (kf.ap()[:, ts(kk, 128)], qf.ap()[:, ts(kk, 128)], (kf, qf))])
        self.act(lambda e: e.activation(P["KKq"].ap()[:, :], bA.ap()[:, :], AF.Copy),
                 reads=(bA,), writes=(P["KKq"],))
        yield

        def kk_pat(off):
            a = P["KKq"].ap()[:, off:off + 256]
            return bass.AP(a.tensor, a.offset, [list(a.ap[0]), [128, 2], [0, 2], [1, 128]])

        def build(dst, lhs, rhs, mask, pat, bank):
            for j in range(4):
                hh = ph0 + j
                self.mm_acc(bank, bank.ap()[:, ts(j, 128)], [
                    (lhs.ap()[0:2, ts(hh, 128)], rhs.ap()[0:2, ts(hh, 128)], (lhs, rhs))])
            self.dve(lambda e: e.tensor_add(P["dA"].ap()[:, :], bank.ap()[:, :],
                                            self.mask(mask, rep=4)),
                     reads=(bank, self.consts), writes=(P["dA"],))
            self.act(lambda e: e.activation(P["dA"].ap()[:, :], P["dA"].ap()[:, :],
                                            AF.Exp),
                     reads=(P["dA"],), writes=(P["dA"],))
            self.dve(lambda e, pat=pat: e.tensor_mul(
                dst.ap()[:, :].rearrange("p (a b c) -> p a b c", a=2, b=2),
                P["dA"].ap()[:, :].rearrange("p (a b c) -> p a b c", a=2, b=2), pat),
                reads=(P["dA"], P["KKq"]), writes=(dst,))
        build(P["At4"], Lp, Rap, "upS", kk_pat(0), bB)
        yield
        build(P["A4"], Rap, Lp, "lowS", kk_pat(0), bC)
        yield
        build(P["Mqt4"], Lp, Rqp, "up0", kk_pat(256), bB)
        yield
        for j in range(4):
            kk = g * 2 + j // 2
            self.mm_acc(bA, bA.ap()[:, ts(j, 128)], [
                (kf.ap()[:, ts(kk, 128)], self.S_bf.ap()[:, ts(h0 + j, 128)],
                 (kf, self.S_bf))])
        self.dve(lambda e: e.tensor_mul(P["tf1"].ap()[:, :], bA.ap()[:, :],
                                        ap_append0(sb["bl_tm"].ap()[:, h0:h0 + 4], 128)),
                 reads=(bA, sb["bl_tm"]), writes=(P["tf1"],))
        self.dve(lambda e: e.tensor_mul(P["tf2"].ap()[:, :],
                                        v_tm.ap()[:, h0 * 128:(h0 + 4) * 128],
                                        ap_append0(sb["beta_tm"].ap()[:, h0:h0 + 4], 128)),
                 reads=(v_tm, sb["beta_tm"]), writes=(P["tf2"],))
        V = P["V4"][0]
        self.dve(lambda e, V=V: e.tensor_sub(V.ap()[:, :], P["tf2"].ap()[:, :],
                                             P["tf1"].ap()[:, :]),
                 reads=(P["tf1"], P["tf2"]), writes=(V,))
        yield
        Pc, Ptc = P["A4"], P["At4"]
        vcur = 0
        for j in range(4):
            self.mm_acc(bC, bC.ap()[:, ts(j, 128)], [
                (Ptc.ap()[:, ts(j, 128)], V.ap()[:, ts(j, 128)], (Ptc, V))])
        Vn = P["V4"][1]
        self.dve(lambda e, V=V, Vn=Vn: e.tensor_sub(Vn.ap()[:, :], V.ap()[:, :],
                                                    bC.ap()[:, :]),
                 reads=(V, bC), writes=(Vn,))
        vcur = 1
        yield
        for k in range(1, 7):
            Pn = P["P"][k % 2]
            Ptn = P["Pt"][k % 2]
            for j in range(4):
                self.mm_acc(bB, bB.ap()[:, ts(j, 128)], [
                    (Ptc.ap()[:, ts(j, 128)], Pc.ap()[:, ts(j, 128)], (Ptc, Pc))])
            self.act(lambda e, Pn=Pn: e.activation(Pn.ap()[:, :], bB.ap()[:, :], AF.Copy),
                     reads=(bB,), writes=(Pn,))
            for j in range(4):
                self.mm_acc(bC, bC.ap()[:, ts(j, 128)], [
                    (Pc.ap()[:, ts(j, 128)], Ptc.ap()[:, ts(j, 128)], (Pc, Ptc))])
            self.act(lambda e, Ptn=Ptn: e.activation(Ptn.ap()[:, :], bC.ap()[:, :],
                                                     AF.Copy),
                     reads=(bC,), writes=(Ptn,))
            yield
            Pc, Ptc = Pn, Ptn
            V, Vn = P["V4"][vcur], P["V4"][1 - vcur]
            for j in range(4):
                self.mm_acc(bA, bA.ap()[:, ts(j, 128)], [
                    (Ptc.ap()[:, ts(j, 128)], V.ap()[:, ts(j, 128)], (Ptc, V))])
            self.dve(lambda e, V=V, Vn=Vn: e.tensor_add(Vn.ap()[:, :], V.ap()[:, :],
                                                        bA.ap()[:, :]),
                     reads=(V, bA), writes=(Vn,))
            vcur = 1 - vcur
            yield
        U = P["V4"][vcur]
        for j in range(4):
            kk = g * 2 + j // 2
            self.mm_acc(bB, bB.ap()[:, ts(j, 128)], [
                (qf.ap()[:, ts(kk, 128)], self.S_bf.ap()[:, ts(h0 + j, 128)],
                 (qf, self.S_bf))])
        self.dve(lambda e: e.tensor_mul(P["tf1"].ap()[:, :], bB.ap()[:, :],
                                        ap_append0(sb["liq_tm"].ap()[:, h0:h0 + 4], 128)),
                 reads=(bB, sb["liq_tm"]), writes=(P["tf1"],))
        for j in range(4):
            self.mm_acc(bC, bC.ap()[:, ts(j, 128)], [
                (P["Mqt4"].ap()[:, ts(j, 128)], U.ap()[:, ts(j, 128)],
                 (P["Mqt4"], U))])
        self.dve(lambda e: e.tensor_add(sb["o_tm"].ap()[:, h0 * 128:(h0 + 4) * 128],
                                        P["tf1"].ap()[:, :], bC.ap()[:, :]),
                 reads=(P["tf1"], bC), writes=(sb["o_tm"],))
        yield
        ka = k_tm.ap()[:, :]
        kpat = bass.AP(ka.tensor, ka.offset + (g * 2) * 128,
                       [list(ka.ap[0]), [128, 2], [0, 2], [1, 128]])
        self.dve(lambda e, kpat=kpat: e.tensor_mul(
            P["kw4"].ap()[:, :].rearrange("p (a b c) -> p a b c", a=2, b=2), kpat,
            ap_append0(sb["wk_tm"].ap()[:, h0:h0 + 4], 128).rearrange(
                "p (a b) c -> p a b c", a=2)),
            reads=(k_tm, sb["wk_tm"]), writes=(P["kw4"],))
        for j in range(4):
            self.mm_acc(bA, bA.ap()[:, ts(j, 128)], [
                (P["kw4"].ap()[:, ts(j, 128)], U.ap()[:, ts(j, 128)],
                 (P["kw4"], U))])
        ssl = self.S_f32.ap()[:, h0 * 128:(h0 + 4) * 128]
        self.dve(lambda e: e.tensor_mul(P["tf1"].ap()[:, :], ssl,
                                        ap_append0(sb["lamL"].ap()[:, h0:h0 + 4], 128)),
                 reads=(self.S_f32, sb["lamL"]), writes=(P["tf1"],))
        self.dve(lambda e: e.tensor_add(ssl, P["tf1"].ap()[:, :], bA.ap()[:, :]),
                 reads=(P["tf1"], bA), writes=(self.S_f32,))

    def _l1_ychunk(self, cch, sb):
        c = self.ctx
        zt = sb["zt"]
        c.dma("qsp", zt.ap()[:, :], self.z_scr.ap()[:, ts(cch, 128)],
              reads=(self.z_scr,), writes=(zt,), transpose=True)
        self.act(lambda e: e.activation(zt.ap()[:, :], zt.ap()[:, :], AF.Silu),
                 reads=(zt,), writes=(zt,))
        self.dve(lambda e: e.tensor_mul(sb["o_tm"].ap()[:, :], sb["o_tm"].ap()[:, :],
                                        zt.ap()[:, :]),
                 reads=(sb["o_tm"], zt), writes=(sb["o_tm"],))
        st = self.stats
        self.act(lambda e: e.activation(zt.ap()[:, :], sb["o_tm"].ap()[:, :],
                                        AF.Square),
                 reads=(sb["o_tm"],), writes=(zt,))
        self.dve(lambda e: e.tensor_reduce(
            sb["ssy"].ap()[:, :].rearrange("p (g o) -> p g o", g=32),
            zt.ap()[:, :].rearrange("p (g n) -> p g n", g=32),
            axis=AX.X, op=OP.add),
            reads=(zt,), writes=(sb["ssy"],))
        self.act(lambda e: e.activation(sb["ssy"].ap()[:, :], sb["ssy"].ap()[:, :],
                                        AF.Ln, scale=1.0 / 128,
                                        bias=self.cb(1e-6).ap()[:, :]),
                 reads=(sb["ssy"], self.cb(1e-6)), writes=(sb["ssy"],))
        self.act(lambda e: e.activation(sb["ssy"].ap()[:, :], sb["ssy"].ap()[:, :],
                                        AF.Exp, scale=-0.5),
                 reads=(sb["ssy"],), writes=(sb["ssy"],))
        self.dve(lambda e: e.tensor_mul(
            zt.ap()[:, :].rearrange("p (g n) -> p g n", g=32),
            sb["o_tm"].ap()[:, :].rearrange("p (g n) -> p g n", g=32),
            ap_append0(sb["ssy"].ap()[:, :], 128)),
            reads=(sb["o_tm"], sb["ssy"]), writes=(zt,))
        yf3 = sb["yn_fm"].ap()[:, :].rearrange("p (ic t) -> p ic t", ic=32)
        for gg in range(8):
            ptb = self.pt[gg % 2]
            for j in range(4):
                ic = gg * 4 + j
                self.transpose(ptb, ptb.ap()[:, ts(j, 128)], zt.ap()[:, ts(ic, 128)],
                               self.identb.ap()[:, :], reads=(zt, self.identb))
            if gg % 2 == 0:
                self.dve(lambda e, gg=gg, ptb=ptb: e.tensor_copy(
                    yf3[:, gg * 4:(gg + 1) * 4, :],
                    ptb.ap()[:, 0:512].rearrange("p (a b) -> p a b", a=4)),
                    reads=(ptb,), writes=(sb["yn_fm"],))
            else:
                self.act(lambda e, gg=gg, ptb=ptb: e.activation(
                    yf3[:, gg * 4:(gg + 1) * 4, :],
                    ptb.ap()[:, 0:512].rearrange("p (a b) -> p a b", a=4), AF.Copy),
                    reads=(ptb,), writes=(sb["yn_fm"],))
        c.dma("qact", self.yn_scr.ap()[:, ts(cch, 4096)], sb["yn_fm"].ap()[:, :],
              reads=(sb["yn_fm"],), writes=(self.yn_scr,))

    def emit_l1_attn_loop(self):
        c = self.ctx
        with c.scope():
            sb = {}
            self.S_f32 = c.sbuf("S_f32", [128, HV * 128], F32)
            self.S_bf = c.sbuf("S_bf", [128, HV * 128], BF16)
            sb["qf"] = c.sbuf("qf", [128, 2048], BF16)
            sb["kf"] = c.sbuf("kf", [128, 2048], BF16)
            sb["k_tm"] = c.sbuf("k_tm1", [128, 2048], BF16)
            sb["v_tm"] = c.sbuf("v_tm1", [128, 4096], BF16)
            sb["hmw"] = c.sbuf("hmw", [32, 512], F32)
            for nm in ("wk_tm", "bl_tm", "liq_tm", "beta_tm", "lamL"):
                sb[nm] = c.sbuf(nm, [128, 32], F32)
            sb["lamrow"] = c.sbuf("lamrow", [1, 32], F32)
            sb["Lp"] = [c.sbuf(f"Lp{i}", [2, 1024], F32) for i in range(2)]
            sb["Rap"] = [c.sbuf(f"Rap{i}", [2, 1024], F32) for i in range(2)]
            sb["Rqp"] = [c.sbuf(f"Rqp{i}", [2, 1024], F32) for i in range(2)]
            for i in range(2):
                c.dma("qsp", sb["Lp"][i].ap()[1:2, :],
                      self.inp["ones_row"].ap()[:, 0:1024],
                      reads=(self.inp["ones_row"],), writes=(sb["Lp"][i],))
                self.dve(lambda e, i=i: e.memset(sb["Rap"][i].ap()[0:1, :], 1.0),
                         writes=(sb["Rap"][i],))
                self.dve(lambda e, i=i: e.memset(sb["Rqp"][i].ap()[0:1, :], 1.0),
                         writes=(sb["Rqp"][i],))
            sb["PS"] = []
            for sfx in ("a", "b"):
                P = {}
                P["KKq"] = c.sbuf("KKq" + sfx, [128, 512], F32)
                P["dA"] = c.sbuf("dAx" + sfx, [128, 512], F32)
                P["At4"] = c.sbuf("At4" + sfx, [128, 512], BF16)
                P["A4"] = c.sbuf("A4" + sfx, [128, 512], BF16)
                P["Mqt4"] = c.sbuf("Mqt4" + sfx, [128, 512], BF16)
                P["V4"] = [c.sbuf(f"V4_{sfx}{i}", [128, 512], BF16) for i in range(2)]
                P["P"] = [c.sbuf(f"Pp{sfx}", [128, 512], BF16), P["A4"]]
                P["Pt"] = [c.sbuf(f"Ppt{sfx}", [128, 512], BF16), P["At4"]]
                P["tf1"] = c.sbuf("tf1" + sfx, [128, 512], F32)
                sb["PS"].append(P)
            for nm, shp, dt in (("tf2", [128, 512], F32), ("kw4", [128, 512], BF16)):
                shared = c.sbuf(nm + "s", shp, dt)
                sb["PS"][0][nm] = shared
                sb["PS"][1][nm] = shared
            sb["PS"][0]["banks"] = (self.pb[0], self.pb[1], self.pb[2])
            sb["PS"][1]["banks"] = (self.pb[3], self.pb[4], self.pb[5])
            sb["o_tm"] = c.sbuf("o_tm", [128, 4096], F32)
            sb["zt"] = c.sbuf("ztl", [128, 4096], BF16)
            sb["ssy"] = c.sbuf("ssy", [128, 32], F32)
            sb["yn_fm"] = sb["v_tm"]
            self.dve(lambda e: e.memset(self.S_f32.ap()[:, :], 0.0), writes=(self.S_f32,))
            for cch in range(getattr(self, "l1_nch", NCH)):
                self.emit_l1_chunk(cch, sb)

    return (_l1_smalls, _l1_load_rows, emit_l1_chunk, _l1_group_stages, _l1_ychunk,
            emit_l1_attn_loop)


(MK._l1_smalls, MK._l1_load_rows, MK.emit_l1_chunk, MK._l1_group_stages, MK._l1_ychunk,
 MK.emit_l1_attn_loop) = _l1_loop_methods()


def build_l1_attn_program(dbg=()):
    nc = bass.Bass("TRN2", target_bir_lowering=False)
    mk = MK(nc, dbg=dbg)
    mk.ctx.dram("x1_in", [T, D], F32, kind="ExternalInput")
    mk.inp["x1_in"] = mk.ctx.bufs[-1]
    mk.din("maskcol", [128, 1])
    mk.din("ident_bf", [128, 128], BF16)
    mk.din("ident_f32", [128, 128])
    mk.din("masks3", [128, 512])
    mk.din("ones_row", [1, 2048])
    mk.declare_l1_io()
    mk.dout("y_out", [T, D])
    mk.ctx.alloc()
    mk.alloc_persist()
    mk.load_consts()
    with mk.ctx.scope():
        mk.alloc_l1_persist()
        mk.emit_l1_front(mk.inp["x1_in"])
        mk.emit_l1_prepass()
        mk.emit_l1_rows()
        mk.emit_l1_attn_loop()
    mk.emit_wout(zsrc=mk.yn_scr, w_key="wout1", nw_key="nw1",
                 res_buf=mk.inp["x1_in"], dest=mk.out["y_out"])
    mk.ctx.barrier_all()
    mk.ctx.finalize()
    return nc, mk


def host_l1_inputs(IN):
    f = np.float32
    shared = {
        "wqkv": tile_fm(np.asarray(IN["L1_attn_Wqkv"], f), 128),
        "wz": tile_fm(np.asarray(IN["L1_attn_Wz"], f), 128),
        "wab": np.concatenate([IN["L1_attn_Wa"], IN["L1_attn_Wb"]], axis=1).astype(f),
        "conv_w1": np.ascontiguousarray(IN["L1_attn_conv_w"]).astype(f),
        "alog": IN["L1_attn_A_log"].reshape(-1, 1).astype(f),
        "dtb1": IN["L1_attn_dt_bias"].reshape(-1, 1).astype(f),
        "nw1": np.ascontiguousarray(np.tile(IN["L1_attn_norm_w"], 32).reshape(32, 128).T).astype(f),
        "wout1": np.ascontiguousarray(IN["L1_attn_Wout"]).astype(f),
        "gwT": np.ascontiguousarray(IN["L1_ffn_gate_weight"].T).astype(f),
        "ebias": IN["L1_ffn_e_score_correction_bias"].reshape(1, -1).astype(f),
        "gate_wT": np.concatenate(
            [tile_fm(np.asarray(IN["L1_ffn_gate_w"][n], f).T, 512) for n in range(NE)]),
        "up_wT": np.concatenate(
            [tile_fm(np.asarray(IN["L1_ffn_up_w"][n], f).T, 512) for n in range(NE)]),
        "down_wT": np.concatenate(
            [tile_fm(np.asarray(IN["L1_ffn_down_w"][n], f).T, 512) for n in range(NE)]),
        "shgT": tile_fm(np.asarray(IN["L1_ffn_sh_gate_w"], f).T, 512),
        "shuT": tile_fm(np.asarray(IN["L1_ffn_sh_up_w"], f).T, 512),
        "shdT": tile_fm(np.asarray(IN["L1_ffn_sh_down_w"], f).T, 512),
    }
    return shared


def _moe_methods():
    def emit_moe_routing(self, xn_fm, xn3, sb, th):
        c = self.ctx
        gw = sb["gw_sb"]
        eb = sb["ebias_b"]
        for tt in range(8):
            gt = th * 8 + tt
            pa = self.pb[tt % 2]
            self.mm_acc(pa, pa.ap()[:, 0:16], [
                (xn3[:, dc, ts(tt, 128)], gw.ap()[:, dc * 16:(dc + 1) * 16],
                 (xn_fm, gw)) for dc in range(DC)])
            r = sb["rt"]
            self.act(lambda e, pa=pa: e.activation(r.ap()[:, 0:16], pa.ap()[:, 0:16],
                                                   AF.Sigmoid),
                     reads=(pa,), writes=(r,))
            self.dve(lambda e: e.tensor_add(r.ap()[:, 16:32], r.ap()[:, 0:16],
                                            eb.ap()[:, :]),
                     reads=(r, eb), writes=(r,))
            for gg in range(4):
                gsl = r.ap()[:, 16 + gg * 4:16 + (gg + 1) * 4]
                self.dve(lambda e, gsl=gsl, gg=gg: e.reduce_max(
                    r.ap()[:, 32 + gg:33 + gg], gsl, axis=AX.X),
                    reads=(r,), writes=(r,))
                self.dve(lambda e, gsl=gsl, gg=gg: e.tensor_single_scalar(
                    r.ap()[:, 48 + gg * 4:48 + (gg + 1) * 4], gsl,
                    r.ap()[:, 32 + gg:33 + gg], op=OP.is_ge),
                    reads=(r,), writes=(r,))
                self.dve(lambda e, gsl=gsl, gg=gg: e.scalar_tensor_tensor(
                    r.ap()[:, 48 + gg * 4:48 + (gg + 1) * 4],
                    r.ap()[:, 48 + gg * 4:48 + (gg + 1) * 4], NEG, gsl,
                    op0=OP.mult, op1=OP.add),
                    reads=(r,), writes=(r,))
                self.dve(lambda e, gg=gg: e.reduce_max(
                    r.ap()[:, 36 + gg:37 + gg],
                    r.ap()[:, 48 + gg * 4:48 + (gg + 1) * 4], axis=AX.X),
                    reads=(r,), writes=(r,))
            self.dve(lambda e: e.tensor_add(r.ap()[:, 40:44], r.ap()[:, 32:36],
                                            r.ap()[:, 36:40]),
                     reads=(r,), writes=(r,))
            self.dve(lambda e: e.reduce_max(r.ap()[:, 44:45], r.ap()[:, 40:44],
                                            axis=AX.X),
                     reads=(r,), writes=(r,))
            self.dve(lambda e: e.tensor_single_scalar(r.ap()[:, 48:52], r.ap()[:, 40:44],
                                                      r.ap()[:, 44:45], op=OP.is_ge),
                     reads=(r,), writes=(r,))
            self.dve(lambda e: e.scalar_tensor_tensor(r.ap()[:, 48:52], r.ap()[:, 48:52],
                                                      NEG, r.ap()[:, 40:44],
                                                      op0=OP.mult, op1=OP.add),
                     reads=(r,), writes=(r,))
            self.dve(lambda e: e.reduce_max(r.ap()[:, 44:45], r.ap()[:, 48:52],
                                            axis=AX.X),
                     reads=(r,), writes=(r,))
            self.dve(lambda e: e.tensor_single_scalar(r.ap()[:, 45:49], r.ap()[:, 40:44],
                                                      r.ap()[:, 44:45], op=OP.is_ge),
                     reads=(r,), writes=(r,))
            self.dve(lambda e: e.tensor_scalar(r.ap()[:, 52:56], r.ap()[:, 45:49],
                                               -NEG, NEG, op0=OP.mult, op1=OP.add),
                     reads=(r,), writes=(r,))
            self.dve(lambda e: e.tensor_add(
                r.ap()[:, 64:80].rearrange("p (g e) -> p g e", g=4),
                r.ap()[:, 16:32].rearrange("p (g e) -> p g e", g=4),
                ap_append0(r.ap()[:, 52:56], 4)),
                reads=(r,), writes=(r,))
            self.dve(lambda e: e.tensor_copy(r.ap()[:, 80:96], r.ap()[:, 64:80]),
                     reads=(r,), writes=(r,))
            for _ in range(3):
                self.dve(lambda e: e.reduce_max(r.ap()[:, 44:45], r.ap()[:, 64:80],
                                                axis=AX.X),
                         reads=(r,), writes=(r,))
                self.dve(lambda e: e.tensor_single_scalar(
                    r.ap()[:, 96:112], r.ap()[:, 64:80], r.ap()[:, 44:45], op=OP.is_ge),
                    reads=(r,), writes=(r,))
                self.dve(lambda e: e.scalar_tensor_tensor(
                    r.ap()[:, 64:80], r.ap()[:, 96:112], NEG, r.ap()[:, 64:80],
                    op0=OP.mult, op1=OP.add),
                    reads=(r,), writes=(r,))
            self.dve(lambda e: e.reduce_max(r.ap()[:, 44:45], r.ap()[:, 64:80],
                                            axis=AX.X),
                     reads=(r,), writes=(r,))
            self.dve(lambda e: e.tensor_single_scalar(r.ap()[:, 96:112], r.ap()[:, 80:96],
                                                      r.ap()[:, 44:45], op=OP.is_ge),
                     reads=(r,), writes=(r,))
            self.dve(lambda e: e.tensor_mul(r.ap()[:, 96:112], r.ap()[:, 96:112],
                                            r.ap()[:, 0:16]),
                     reads=(r,), writes=(r,))
            self.dve(lambda e: e.reduce_sum(r.ap()[:, 44:45], r.ap()[:, 96:112],
                                            axis=AX.X),
                     reads=(r,), writes=(r,))
            self.dve(lambda e: e.tensor_scalar_add(r.ap()[:, 44:45], r.ap()[:, 44:45],
                                                   1e-20),
                     reads=(r,), writes=(r,))
            self.dve(lambda e: e.reciprocal(r.ap()[:, 45:46], r.ap()[:, 44:45]),
                     reads=(r,), writes=(r,))
            self.dve(lambda e: e.tensor_scalar(r.ap()[:, 96:112], r.ap()[:, 96:112],
                                               r.ap()[:, 45:46], 2.5,
                                               op0=OP.mult, op1=OP.mult),
                     reads=(r,), writes=(r,))
            pt = self.pb[4]
            self.transpose(pt, pt.ap()[0:16, 0:128], r.ap()[:, 96:112],
                           self.identf.ap()[:, :], reads=(r, self.identf))
            self.dve(lambda e, pt=pt: e.tensor_copy(sb["wrow"].ap()[:, :],
                                                    pt.ap()[0:16, 0:128]),
                     reads=(pt,), writes=(sb["wrow"],))
            c.dma("qact", self.w_scr.ap()[:, ts(gt, 128)], sb["wrow"].ap()[:, :],
                  reads=(sb["wrow"],), writes=(self.w_scr,))

    def emit_moe(self, x2a_buf, dest):
        c = self.ctx
        self.w_scr = self.dscr("w_scr", [16, T], F32)
        with c.scope():
            sb = {}
            sb["gw_sb"] = c.sbuf("gw_sb", [128, DC * 16], BF16)
            sb["ebias_b"] = c.sbuf("ebias_b", [128, 16], F32)
            sb["rt"] = c.sbuf("rt", [128, 112], F32)
            sb["wrow"] = c.sbuf("wrow", [16, 128], F32)
            src = self.inp["gwT"].ap()[:, :].rearrange("(a p) c -> p a c", p=128)
            c.dma("qpool", sb["gw_sb"].ap()[:, :].rearrange("p (a c) -> p a c", a=DC),
                  src, reads=(self.inp["gwT"],), writes=(sb["gw_sb"],))
            ea = self.inp["ebias"].ap()
            c.dma("qsp", sb["ebias_b"].ap()[:, :],
                  bass.AP(ea.tensor, 0, [[0, 128], [1, 16]]),
                  reads=(self.inp["ebias"],), writes=(sb["ebias_b"],))
            xn_th = c.sbuf("xn_moe", [128, DC * 1024], BF16)
            acc = c.sbuf("acc_moe", [128, 8 * D], F32)
            wgu = [c.sbuf(f"wgu{i}", [128, DC * 512], BF16) for i in range(3)]
            wdn = [c.sbuf(f"wdn{i}", [128, 8 * 512], BF16) for i in range(2)]
            actn = c.sbuf("actn", [128, 8 * 1024], BF16)
            sgt = c.sbuf("sgt", [128, 512], F32)
            wut = c.sbuf("wut", [128, 512], F32)
            wbt = [c.sbuf(f"wbt{i}", [128, 1024], F32) for i in range(2)]
            outt = [c.sbuf(f"outt_m{i}", [128, 512], F32) for i in range(2)]
            x2t = [c.sbuf(f"x2t{i}", [128, 512], F32) for i in range(2)]
            ring = [0]

            def slab():
                r = wgu[ring[0] % 3]
                ring[0] += 1
                return r
            for th in range(2):
                xa = self.xn1_scr.ap()[:, :].rearrange("p (dc t) -> p dc t", dc=DC)
                c.dma("qsp", xn_th.ap()[:, :].rearrange("p (dc t) -> p dc t", dc=DC),
                      xa[:, :, th * 1024:(th + 1) * 1024],
                      reads=(self.xn1_scr,), writes=(xn_th,))
                xn3 = xn_th.ap()[:, :].rearrange("p (dc t) -> p dc t", dc=DC)
                self.emit_moe_routing(xn_th, xn3, sb, th)
                for n in range(NE):
                    wb = wbt[n % 2]
                    wa = self.w_scr.ap()
                    c.dma("qsp", wb.ap()[:, :],
                          bass.AP(wa.tensor, n * T + th * 1024, [[0, 128], [1, 1024]]),
                          reads=(self.w_scr,), writes=(wb,))
                    for fh2 in range(2):
                        pair = []
                        for wn in ("gate_wT", "up_wT"):
                            w = slab()
                            c.dma("qpool", w.ap()[:, :],
                                  self.inp[wn].ap()[ts(n * 2 + fh2, 128), :],
                                  reads=(self.inp[wn],), writes=(w,))
                            pair.append(w)
                        wg, wu = pair
                        for fcl in range(4):
                            fc = fh2 * 4 + fcl
                            w0v = wg.ap()[:, :].rearrange("p (a c) -> p a c", a=DC)
                            w1v = wu.ap()[:, :].rearrange("p (a c) -> p a c", a=DC)
                            self.mm_kouter(
                                [(self.pb[tgg], self.pb[tgg].ap()[:, :])
                                 for tgg in range(2)],
                                [(w0v[:, dc, ts(fcl, 128)],
                                  [xn3[:, dc, ts(tgg, 512)] for tgg in range(2)],
                                  (wg, xn_th)) for dc in range(DC)])
                            self.mm_kouter(
                                [(self.pb[2 + tgg], self.pb[2 + tgg].ap()[:, :])
                                 for tgg in range(2)],
                                [(w1v[:, dc, ts(fcl, 128)],
                                  [xn3[:, dc, ts(tgg, 512)] for tgg in range(2)],
                                  (wu, xn_th)) for dc in range(DC)])
                            for tgg in range(2):
                                pg = self.pb[tgg]
                                pu = self.pb[2 + tgg]
                                self.act(lambda e, pg=pg: e.activation(
                                    sgt.ap()[:, :], pg.ap()[:, :], AF.Silu),
                                    reads=(pg,), writes=(sgt,))
                                self.dve(lambda e, pu=pu, tgg=tgg, wb=wb: e.tensor_mul(
                                    wut.ap()[:, :], pu.ap()[:, :],
                                    wb.ap()[:, ts(tgg, 512)]),
                                    reads=(pu, wb), writes=(wut,))
                                self.dve(lambda e, fc=fc, tgg=tgg: e.tensor_mul(
                                    actn.ap()[:, fc * 1024 + tgg * 512:
                                              fc * 1024 + (tgg + 1) * 512],
                                    sgt.ap()[:, :], wut.ap()[:, :]),
                                    reads=(sgt, wut), writes=(actn,))
                    for ds in range(4):
                        wd = wdn[ds % 2]
                        c.dma("qpool", wd.ap()[:, :],
                              self.inp["down_wT"].ap()[ts(n * 4 + ds, 128), :],
                              reads=(self.inp["down_wT"],), writes=(wd,))
                        for tt in range(8):
                            pa = self.pb[4 + tt % 2]
                            self.mm_acc(pa, pa.ap()[:, :], [
                                (actn.ap()[:, fcc * 1024:(fcc + 1) * 1024]
                                 .rearrange("p (g t) -> p g t", g=2)
                                 [:, tt // 4, ts(tt % 4, 128)],
                                 wd.ap()[:, ts(fcc, 512)], (actn, wd))
                                for fcc in range(8)])
                            asl = acc.ap()[:, :].rearrange("p (tt d) -> p tt d", tt=8)
                            if n == 0:
                                self.dve(lambda e, pa=pa, tt=tt, ds=ds, asl=asl:
                                         e.tensor_copy(asl[:, tt, ts(ds, 512)],
                                                       pa.ap()[:, :]),
                                         reads=(pa,), writes=(acc,))
                            else:
                                self.dve(lambda e, pa=pa, tt=tt, ds=ds, asl=asl:
                                         e.tensor_add(asl[:, tt, ts(ds, 512)],
                                                      asl[:, tt, ts(ds, 512)],
                                                      pa.ap()[:, :]),
                                         reads=(pa, acc), writes=(acc,))
                self._moe_shared(th, xn_th, xn3, acc, slab, wdn, sgt, actn, c)
                for tt in range(8):
                    gt = th * 8 + tt
                    for ds in range(4):
                        asl = acc.ap()[:, :].rearrange("p (tt d) -> p tt d", tt=8)
                        xt = x2t[ds % 2]
                        ot = outt[ds % 2]
                        c.dma("qsp", xt.ap()[:, :],
                              x2a_buf.ap()[ts(gt, 128), ts(ds, 512)],
                              reads=(x2a_buf,), writes=(xt,))
                        self.dve(lambda e, tt=tt, ds=ds, asl=asl, xt=xt, ot=ot:
                                 e.tensor_add(ot.ap()[:, :], asl[:, tt, ts(ds, 512)],
                                              xt.ap()[:, :]),
                                 reads=(acc, xt), writes=(ot,))
                        c.dma("qact", dest.ap()[ts(gt, 128), ts(ds, 512)],
                              ot.ap()[:, :], reads=(ot,), writes=(dest,))

    def _moe_shared(self, th, xn_fm, xn3, acc, slab, wdn, sgt, actn, c):
        for fh in range(2):
            for fh2 in range(2):
                pair = []
                for wn in ("shgT", "shuT"):
                    w = slab()
                    c.dma("qpool", w.ap()[:, :],
                          self.inp[wn].ap()[ts(fh * 2 + fh2, 128), :],
                          reads=(self.inp[wn],), writes=(w,))
                    pair.append(w)
                wg, wu = pair
                for fcl in range(4):
                    fc = fh2 * 4 + fcl
                    w0v = wg.ap()[:, :].rearrange("p (a c) -> p a c", a=DC)
                    w1v = wu.ap()[:, :].rearrange("p (a c) -> p a c", a=DC)
                    self.mm_kouter(
                        [(self.pb[tgg], self.pb[tgg].ap()[:, :]) for tgg in range(2)],
                        [(w0v[:, dc, ts(fcl, 128)],
                          [xn3[:, dc, ts(tgg, 512)] for tgg in range(2)],
                          (wg, xn_fm)) for dc in range(DC)])
                    self.mm_kouter(
                        [(self.pb[2 + tgg], self.pb[2 + tgg].ap()[:, :])
                         for tgg in range(2)],
                        [(w1v[:, dc, ts(fcl, 128)],
                          [xn3[:, dc, ts(tgg, 512)] for tgg in range(2)],
                          (wu, xn_fm)) for dc in range(DC)])
                    for tgg in range(2):
                        pg = self.pb[tgg]
                        pu = self.pb[2 + tgg]
                        self.act(lambda e, pg=pg: e.activation(sgt.ap()[:, :],
                                                               pg.ap()[:, :], AF.Silu),
                                 reads=(pg,), writes=(sgt,))
                        self.dve(lambda e, pu=pu, fc=fc, tgg=tgg: e.tensor_mul(
                            actn.ap()[:, fc * 1024 + tgg * 512:
                                      fc * 1024 + (tgg + 1) * 512],
                            sgt.ap()[:, :], pu.ap()[:, :]),
                            reads=(sgt, pu), writes=(actn,))
            for ds in range(4):
                wd = wdn[ds % 2]
                c.dma("qpool", wd.ap()[:, :],
                      self.inp["shdT"].ap()[ts(ds, 128), ts(fh, 4096)],
                      reads=(self.inp["shdT"],), writes=(wd,))
                for tt in range(8):
                    pa = self.pb[4 + tt % 2]
                    self.mm_acc(pa, pa.ap()[:, :], [
                        (actn.ap()[:, fcc * 1024:(fcc + 1) * 1024]
                         .rearrange("p (g t) -> p g t", g=2)
                         [:, tt // 4, ts(tt % 4, 128)],
                         wd.ap()[:, ts(fcc, 512)], (actn, wd))
                        for fcc in range(8)])
                    asl = acc.ap()[:, :].rearrange("p (tt d) -> p tt d", tt=8)
                    self.dve(lambda e, pa=pa, tt=tt, ds=ds, asl=asl: e.tensor_add(
                        asl[:, tt, ts(ds, 512)], asl[:, tt, ts(ds, 512)], pa.ap()[:, :]),
                        reads=(pa, acc), writes=(acc,))

    return emit_moe_routing, emit_moe, _moe_shared


MK.emit_moe_routing, MK.emit_moe, MK._moe_shared = _moe_methods()


def build_l1_program(dbg=()):
    nc = bass.Bass("TRN2", target_bir_lowering=False)
    mk = MK(nc, dbg=dbg)
    mk.ctx.dram("x1_in", [T, D], F32, kind="ExternalInput")
    mk.inp["x1_in"] = mk.ctx.bufs[-1]
    mk.din("maskcol", [128, 1])
    mk.din("ident_bf", [128, 128], BF16)
    mk.din("ident_f32", [128, 128])
    mk.din("masks3", [128, 512])
    mk.din("ones_row", [1, 2048])
    mk.declare_l1_io()
    mk.dout("y_out", [T, D])
    mk.ctx.alloc()
    mk.alloc_persist()
    mk.load_consts()
    with mk.ctx.scope():
        mk.alloc_l1_persist()
        mk.emit_l1_front(mk.inp["x1_in"])
        mk.emit_l1_prepass()
        mk.emit_l1_rows()
        mk.emit_l1_attn_loop()
    mk.emit_wout(zsrc=mk.yn_scr, w_key="wout1", nw_key="nw1",
                 res_buf=mk.inp["x1_in"], dest=mk.x2a_scr)
    mk.emit_moe(mk.x2a_scr, mk.out["y_out"])
    mk.ctx.barrier_all()
    mk.ctx.finalize()
    return nc, mk


_CACHE = {}


def _get_programs():
    if "l0" not in _CACHE:
        _CACHE["l0"] = build_l0_program()
    if "l1" not in _CACHE:
        _CACHE["l1"] = build_l1_program()
    return _CACHE["l0"][0], _CACHE["l1"][0]


def kernel_two_launch(**inputs):
    IN = inputs
    f = np.float32
    nc0, nc1 = _get_programs()
    sh0 = host_l0_inputs(IN)
    maps0 = [core_l0_inputs(IN, sh0, c) for c in range(8)]
    res0 = run_bass_kernel_spmd(nc0, maps0, core_ids=list(range(8)))
    x1s = [res0.results[c]["y_out"] for c in range(8)]
    sh1 = host_l1_inputs(IN)
    maps1 = []
    for c in range(8):
        m = dict(sh1)
        for k in ("ident_bf", "ident_f32", "masks3", "ones_row"):
            m[k] = sh0[k]
        m["maskcol"] = np.ones((128, 1), f)
        m["x1_in"] = np.ascontiguousarray(x1s[c], dtype=f)
        maps1.append(m)
    res1 = run_bass_kernel_spmd(nc1, maps1, core_ids=list(range(8)))
    out = np.zeros((4, 4096, 2048), f)
    for c in range(8):
        out[c // 2, (c % 2) * T:((c % 2) + 1) * T] = res1.results[c]["y_out"]
    return out


def build_full_program(dbg=()):
    nc = bass.Bass("TRN2", target_bir_lowering=False)
    mk = MK(nc, dbg=dbg)
    mk.declare_io()
    mk.declare_l1_io()
    x1_full = mk.dscr("x1_full", [T, D], F32)
    mk.ctx.alloc()
    mk.alloc_persist()
    mk.load_consts()
    mk.emit_gamma()
    with mk.ctx.scope():
        mk.alloc_l0_wide()
        mk.emit_l0_attn()
    mk.emit_ffn(x1_full)
    with mk.ctx.scope():
        mk.alloc_l1_persist()
        mk.emit_l1_front(x1_full)
        mk.emit_l1_prepass()
        mk.emit_l1_rows()
        mk.emit_l1_attn_loop()
    mk.emit_wout(zsrc=mk.yn_scr, w_key="wout1", nw_key="nw1",
                 res_buf=x1_full, dest=mk.x2a_scr)
    mk.emit_moe(mk.x2a_scr, mk.out["y_out"])
    mk.ctx.barrier_all()
    mk.ctx.finalize()
    return nc, mk


def kernel_fused(**inputs):
    IN = inputs
    f = np.float32
    cu = np.asarray(IN["L1_attn_cu_seqlens"]).ravel()
    seg = int(cu[1]) - int(cu[0])
    assert seg == T and all(int(cu[i + 1]) - int(cu[i]) == seg
                            for i in range(len(cu) - 1)), "unexpected cu_seqlens"
    if "full" not in _CACHE:
        _CACHE["full"] = build_full_program()
    nc = _CACHE["full"][0]
    sh0 = host_l0_inputs(IN)
    sh1 = host_l1_inputs(IN)
    maps = []
    for c in range(8):
        m = core_l0_inputs(IN, sh0, c)
        m.update(sh1)
        maps.append(m)
    res = run_bass_kernel_spmd(nc, maps, core_ids=list(range(8)))
    out = np.zeros((4, 4096, 2048), f)
    for c in range(8):
        out[c // 2, (c % 2) * T:((c % 2) + 1) * T] = res.results[c]["y_out"]
    return out


def kernel(**inputs):
    return kernel_fused(**inputs)


def _opt_methods():
    def mm_kouter(self, banks, kterms):
        nk = len(kterms)
        nb = len(banks)
        all_reads = []
        for ki, (l, rhss, rb) in enumerate(kterms):
            for b in rb:
                if b not in all_reads:
                    all_reads.append(b)
            for bi in range(nb):
                pbuf, oap = banks[bi]
                tok = self.ctx.op(
                    "pe",
                    lambda eng, l=l, r=rhss[bi], o=oap, ki=ki, nk=nk: eng.matmul(
                        o, l, r, start=(ki == 0), stop=(ki == nk - 1)),
                    reads=rb, writes=tuple(p for p, _ in banks) if ki == 0 and bi == 0 else (),
                    inc=(ki == nk - 1 and bi == nb - 1), commit=False)
        for pbuf, _ in banks:
            self.ctx.commit(tok, reads=all_reads, writes=(pbuf,))
        return tok
    return (mm_kouter,)


(MK.mm_kouter,) = _opt_methods()


# version 07
# speedup vs baseline: 1.3507x; 1.1830x over previous
import contextlib
import math
import numpy as np

import concourse.bass as bass
import concourse.mybir as mybir
from concourse.bass_utils import run_bass_kernel_spmd

F32 = mybir.dt.float32
BF16 = mybir.dt.bfloat16
AF = mybir.ActivationFunctionType
OP = mybir.AluOpType
AX = mybir.AxisListType

NEG = -1e30


ENGINES = ("pe", "dve", "act", "pool", "sp")
DMA_QUEUES = ("qsp", "qact", "qpool")
QUEUE_ENGINE = {"qsp": "sp", "qact": "act", "qpool": "pool"}
QUEUE_TYPE = {"qsp": "hw", "qact": "hw", "qpool": "sw"}
N_HW_DMA_SEMS = 40
N_SW_DMA_SEMS = 14
DMA_KEYS = tuple(f"dh{i}" for i in range(N_HW_DMA_SEMS)) + tuple(
    f"dsw{i}" for i in range(N_SW_DMA_SEMS))


class Buf:
    __slots__ = ("name", "ap_fn", "last_w", "readers", "dma_key", "dma_qt")

    def __init__(self, name, ap_fn):
        self.name = name
        self.ap_fn = ap_fn
        self.last_w = None
        self.readers = []
        self.dma_key = None
        self.dma_qt = None

    def ap(self):
        return self.ap_fn()


class Ctx:
    def __init__(self, nc):
        self.nc = nc
        self.streams = {e: [] for e in ENGINES}
        self.cnt = {}
        self.sems = {}
        self.waited = {e: {} for e in ENGINES}
        self._exit = contextlib.ExitStack()
        for e in ENGINES:
            self.cnt[e] = 0
        for q in DMA_QUEUES:
            self.cnt[q] = 0
        self._sem_names = list(ENGINES) + list(DMA_KEYS)
        for k in DMA_KEYS:
            self.cnt[k] = 0
        self._free_dma = {"hw": [f"dh{i}" for i in range(N_HW_DMA_SEMS)],
                          "sw": [f"dsw{i}" for i in range(N_SW_DMA_SEMS)]}
        self.bufs = []
        self._scopes = [self._exit]
        self._scope_bufs = [[]]

    def alloc(self):
        for s in self._sem_names:
            self.sems[s] = self._exit.enter_context(self.nc.semaphore("sem_" + s))

    @contextlib.contextmanager
    def scope(self):
        st = contextlib.ExitStack()
        self._scopes.append(st)
        self._scope_bufs.append([])
        try:
            yield
        finally:
            self.barrier_all()
            self._scopes.pop()
            for b in self._scope_bufs.pop():
                if b.dma_key is not None:
                    self._free_dma[b.dma_qt].append(b.dma_key)
                    b.dma_key = None
            st.close()

    def sbuf_root(self, name, shape, dtype):
        self._uid = getattr(self, "_uid", 0) + 1
        name = f"{name}_u{self._uid}"
        t = self._scopes[0].enter_context(self.nc.sbuf_tensor(name, shape, dtype))
        b = Buf(name, lambda t=t: t)
        self.bufs.append(b)
        return b

    def sbuf(self, name, shape, dtype):
        self._uid = getattr(self, "_uid", 0) + 1
        name = f"{name}_u{self._uid}"
        t = self._scopes[-1].enter_context(self.nc.sbuf_tensor(name, shape, dtype))
        b = Buf(name, lambda t=t: t)
        self.bufs.append(b)
        self._scope_bufs[-1].append(b)
        return b

    def psum(self, name, shape, dtype=F32):
        t = self._exit.enter_context(self.nc.psum_tensor(name, shape, dtype))
        b = Buf(name, lambda t=t: t)
        self.bufs.append(b)
        return b

    def dram(self, name, shape, dtype, kind="Internal"):
        t = self.nc.dram_tensor(name, shape, dtype, kind=kind)
        b = Buf(name, lambda t=t: t.ap())
        self.bufs.append(b)
        return b

    def _waits_for(self, reads, writes):
        toks = []
        for b in reads:
            if b.last_w is not None:
                toks.append(b.last_w)
        for b in writes:
            if b.last_w is not None:
                toks.append(b.last_w)
            toks.extend(b.readers)
        best = {}
        for (k, c) in toks:
            if best.get(k, -1) < c:
                best[k] = c
        return best

    def _commit(self, tok, reads, writes):
        for b in reads:
            b.readers.append(tok)
        for b in writes:
            b.last_w = tok
            b.readers = []

    def op(self, engine, fn, reads=(), writes=(), inc=True, commit=True):
        best = self._waits_for(reads, writes)
        waited = self.waited[engine]
        needed = [(k, c) for (k, c) in best.items() if waited.get(k, 0) < c]
        for (k, c) in needed:
            waited[k] = c
        if inc:
            self.cnt[engine] += 1
        tok = (engine, self.cnt[engine])
        sems = self.sems

        def emit(eng, needed=tuple(needed), fn=fn, inc=inc, engine=engine):
            for (k, c) in needed:
                eng.wait_ge(sems[k], c * 16 if k in DMA_KEYS else c)
            instr = fn(eng)
            if inc:
                instr.then_inc(sems[engine], 1)
        self.streams[engine].append(emit)
        if commit:
            assert inc, "op(commit=True) requires inc=True"
            self._commit(tok, reads, writes)
        return tok

    def commit(self, tok, reads=(), writes=()):
        self._commit(tok, reads, writes)

    def _dma_key_for(self, buf, qt):
        if buf.dma_key is None:
            assert self._free_dma[qt], f"out of {qt} dma sems"
            buf.dma_key = self._free_dma[qt].pop()
            buf.dma_qt = qt
        assert buf.dma_qt == qt, f"{buf.name} used from both queue types"
        return buf.dma_key

    def dma(self, queue, out_ap, in_ap, reads=(), writes=(), **kw):
        engine = QUEUE_ENGINE[queue]
        key = self._dma_key_for(writes[0], QUEUE_TYPE[queue])
        best = self._waits_for(reads, writes)
        waited = self.waited[engine]
        needed = [(k, c) for (k, c) in best.items() if waited.get(k, 0) < c]
        for (k, c) in needed:
            waited[k] = c
        self.cnt[key] += 1
        tok = (key, self.cnt[key])
        sems = self.sems

        def emit(eng, needed=tuple(needed), key=key):
            for (k, c) in needed:
                eng.wait_ge(sems[k], c * 16 if k in DMA_KEYS else c)
            eng.dma_start(out=out_ap, in_=in_ap, **kw).then_inc(sems[key], 16)
        self.streams[engine].append(emit)
        self._commit(tok, reads, writes)
        return tok

    def raw(self, engine, fn):
        self.streams[engine].append(fn)

    def barrier_all(self, engines=ENGINES):
        snapshot = dict(self.cnt)
        sems = self.sems
        for e in engines:
            waited = self.waited[e]
            needed = []
            for k, c in snapshot.items():
                if c > 0 and waited.get(k, 0) < c:
                    needed.append((k, c))
                    waited[k] = c

            def emit(eng, needed=tuple(needed)):
                for (k, c) in needed:
                    eng.wait_ge(sems[k], c * 16 if k in DMA_KEYS else c)
            self.streams[e].append(emit)

    def finalize(self):
        nc = self.nc
        with nc.Block() as block:
            for key, attr in (("sp", "sync"), ("pe", "tensor"), ("dve", "vector"),
                              ("act", "scalar"), ("pool", "gpsimd")):
                stream = self.streams[key]
                if not stream:
                    continue

                def body(eng, stream=stream):
                    for fn in stream:
                        fn(eng)
                getattr(block, attr)(body)
        self._exit.close()


T = 2048
TT = T // 128
D = 2048
DC = D // 128
INTER = 4096
GN = 128
NH, HD = 64, 64
L0C = 128
NCH = T // L0C
CONV_DIM = INTER + 2 * GN
DIN_PAD = 8576
XBC0, XBC1 = INTER, INTER + CONV_DIM
DT0 = INTER + CONV_DIM
F1 = 5632
FC1 = F1 // 128
HK, HV, DK, DV = 16, 32, 128, 128
QKVD = 8192
L1C = 128
NE, FMOE, FSH = 16, 1024, 2048


def ts(i, n):
    return slice(i * n, (i + 1) * n)


def ap_append0(ap, n):
    return bass.AP(ap.tensor, ap.offset, list(ap.ap) + [[0, n]])


def ap_mid0(ap, n):
    a = list(ap.ap)
    return bass.AP(ap.tensor, ap.offset, [a[0], [0, n]] + a[1:])


def tile_fm(w, cols_per):
    Dd, C = w.shape
    nc_ = C // cols_per
    t = w.reshape(Dd // 128, 128, nc_, cols_per).transpose(2, 1, 0, 3)
    return np.ascontiguousarray(t).reshape(nc_ * 128, (Dd // 128) * cols_per)




class MK:

    def __init__(self, nc, dbg=()):
        self.nc = nc
        self.ctx = Ctx(nc)
        self.inp = {}
        self.out = {}
        self.scr = {}
        self.dbg = set(dbg)

    def din(self, name, shape, dtype=F32):
        b = self.ctx.dram(name, shape, dtype, kind="ExternalInput")
        self.inp[name] = b
        return b

    def dout(self, name, shape, dtype=F32):
        b = self.ctx.dram(name, shape, dtype, kind="ExternalOutput")
        self.out[name] = b
        return b

    def dscr(self, name, shape, dtype):
        b = self.ctx.dram(name, shape, dtype, kind="Internal")
        self.scr[name] = b
        return b

    def dbg_dump(self, name, src_buf, ap, shape, dtype=F32):
        if name in self.dbg:
            o = self.dout("dbg_" + name, shape, dtype)
            self.ctx.dma("qact", o.ap(), ap, reads=(src_buf,), writes=(o,))

    def cb(self, val):
        cache = getattr(self, "_cbcache", None)
        if cache is None:
            cache = self._cbcache = {}
        assert val in cache, f"const {val} not preallocated"
        return cache[val]

    def prealloc_consts(self, vals):
        cache = getattr(self, "_cbcache", None)
        if cache is None:
            cache = self._cbcache = {}
        for val in vals:
            if val in cache:
                continue
            b = self.ctx.sbuf("cbv", [128, 1], F32)
            self.dve(lambda e, b=b, val=val: e.memset(b.ap()[:, :], float(val)),
                     writes=(b,))
            cache[val] = b

    def dve(self, fn, reads=(), writes=()):
        return self.ctx.op("dve", fn, reads=reads, writes=writes)

    def act(self, fn, reads=(), writes=()):
        return self.ctx.op("act", fn, reads=reads, writes=writes)

    def pool(self, fn, reads=(), writes=()):
        return self.ctx.op("pool", fn, reads=reads, writes=writes)

    def pe(self, fn, reads=(), writes=()):
        return self.ctx.op("pe", fn, reads=reads, writes=writes)

    def mm_acc(self, psum_buf, out_ap, terms):
        n = len(terms)
        all_reads = []
        for i, (l, r, rb) in enumerate(terms):
            for b in rb:
                if b not in all_reads:
                    all_reads.append(b)
            tok = self.ctx.op(
                "pe",
                lambda eng, l=l, r=r, i=i, n=n: eng.matmul(
                    out_ap, l, r, start=(i == 0), stop=(i == n - 1)),
                reads=rb, writes=(psum_buf,) if i == 0 else (),
                inc=(i == n - 1), commit=False)
        self.ctx.commit(tok, reads=all_reads, writes=(psum_buf,))
        return tok

    def transpose(self, psum_buf, out_ap, in_ap, ident_ap, reads):
        return self.ctx.op(
            "pe", lambda eng: eng.transpose(out_ap, in_ap, ident_ap),
            reads=reads, writes=(psum_buf,))

    def declare_io(self):
        self.din("x_own", [T, D])
        self.din("x_prev", [T, D])
        self.din("maskcol", [128, 1])
        self.din("ident_bf", [128, 128], BF16)
        self.din("ident_f32", [128, 128])
        self.din("masks3", [128, 512])
        self.din("ones_row", [1, 2048])
        self.din("w_in", [67 * 128, D])
        self.din("conv_w0", [CONV_DIM, 4])
        self.din("conv_b0", [CONV_DIM, 1])
        self.din("dt_bias0", [NH, 1])
        self.din("a0", [NH, 1])
        self.din("d0_row", [1, NH])
        self.din("nw0", [128, 32])
        self.din("w_out", [INTER, D])
        self.din("ffn_w1t", [F1, D])
        self.din("ffn_w2t", [F1, D])
        self.din("ffn_w3t", [F1, D])
        self.dout("y_out", [T, D])
        self.x_scr = self.dscr("x_scr", [INTER, T], BF16)
        self.gate_scr = self.dscr("gate_scr", [INTER, T], BF16)
        self.cA_scr = self.dscr("cA_scr", [NH, T], F32)
        self.rr_scr = self.dscr("rr_scr", [NH, T], F32)
        self.zn_scr = self.dscr("zn_scr", [128, NCH * 32 * 128], BF16)
        self.x1a_scr = self.dscr("x1a_scr", [T, D], F32)
        self.xn_scr = self.dscr("xn_scr", [128, DC * T], BF16)
        self.q3_scr = self.dscr("q3_scr", [F1, D], BF16)

    def alloc_persist(self):
        c = self.ctx
        self.consts = c.sbuf("consts", [128, 512], F32)
        self.identb = c.sbuf("identb", [128, 128], BF16)
        self.identf = c.sbuf("identf", [128, 128], F32)
        self.onesr = c.sbuf("onesr", [1, 512], F32)
        self.maskc = c.sbuf("maskc", [128, 1], F32)
        self.stats = c.sbuf("stats", [128, 16], F32)
        self.gthr = c.sbuf("gthr", [128, 8], F32)
        self.prealloc_consts((1e-5, 1.0, 1e-6, 1e-20, 0.5))
        self.pb = [c.psum(f"pb{i}", [128, 512], F32) for i in range(6)]
        self.pt = [c.psum(f"pt{i}", [128, 1024], BF16) for i in range(2)]

    def load_consts(self):
        c = self.ctx
        for dst, srcn in ((self.consts, "masks3"), (self.identb, "ident_bf"),
                          (self.identf, "ident_f32"), (self.maskc, "maskcol")):
            c.dma("qsp", dst.ap()[:, :], self.inp[srcn].ap(), reads=(self.inp[srcn],),
                  writes=(dst,))
        c.dma("qsp", self.onesr.ap()[:, :], self.inp["ones_row"].ap()[:, 0:512],
              reads=(self.inp["ones_row"],), writes=(self.onesr,))

    def m_low0(self, rep=1):
        a = self.consts.ap()[:, 0:128]
        return ap_mid0(a, rep) if rep > 1 else a

    def mask(self, which, rep=1):
        i = {"low0": 0, "up0": 1, "upS": 2, "lowS": 3}[which]
        a = self.consts.ap()[:, i * 128:(i + 1) * 128]
        return ap_mid0(a, rep) if rep > 1 else a

    def kone(self, row_ap, out_buf, out_ap, reads):
        self.mm_acc(out_buf, out_ap,
                    [(self.onesr.ap()[0:1, 0:128], row_ap, tuple(reads) + (self.onesr,))])

    def emit_ln(self, src_buf, xn_fm, xld, xnstage, dump):
        c = self.ctx
        st = self.stats
        xn3 = xn_fm.ap()[:, :].rearrange("p (dc t) -> p dc t", dc=DC)
        for i in range(TT):
            xb = xld
            c.dma("qsp", xb.ap()[:, :], src_buf.ap()[ts(i, 128), :],
                  reads=(src_buf,), writes=(xb,))
            self.act(lambda e, xb=xb: e.activation(dump.ap()[:, 0:D], xb.ap()[:, :],
                                                   AF.Square, accum_out=st.ap()[:, 0:1]),
                     reads=(xb,), writes=(dump, st))
            self.dve(lambda e, xb=xb: e.reduce_sum(st.ap()[:, 1:2], xb.ap()[:, :], axis=AX.X),
                     reads=(xb,), writes=(st,))
            self.dve(lambda e: e.tensor_scalar_mul(st.ap()[:, 2:3], st.ap()[:, 1:2], -1.0 / D),
                     reads=(st,), writes=(st,))
            self.dve(lambda e: e.tensor_mul(st.ap()[:, 3:4], st.ap()[:, 2:3], st.ap()[:, 2:3]),
                     reads=(st,), writes=(st,))
            self.dve(lambda e: e.scalar_tensor_tensor(st.ap()[:, 4:5], st.ap()[:, 0:1],
                                                      1.0 / D, st.ap()[:, 3:4],
                                                      op0=OP.mult, op1=OP.subtract),
                     reads=(st,), writes=(st,))
            self.act(lambda e: e.activation(st.ap()[:, 5:6], st.ap()[:, 4:5], AF.Ln, bias=self.cb(1e-5).ap()[:, :]),
                     reads=(st, self.cb(1e-5)), writes=(st,))
            self.act(lambda e: e.activation(st.ap()[:, 6:7], st.ap()[:, 5:6], AF.Exp, scale=-0.5),
                     reads=(st,), writes=(st,))
            self.dve(lambda e, xb=xb: e.tensor_scalar(xnstage.ap()[:, 0:D], xb.ap()[:, :],
                                                      st.ap()[:, 2:3], st.ap()[:, 6:7],
                                                      op0=OP.add, op1=OP.mult),
                     reads=(xb, st), writes=(xnstage,))
            for g in range(4):
                ptb = self.pt[g % 2]
                for j in range(4):
                    dc = g * 4 + j
                    self.transpose(ptb, ptb.ap()[:, ts(j, 128)],
                                   xnstage.ap()[:, ts(dc, 128)], self.identb.ap()[:, :],
                                   reads=(xnstage, self.identb))
                self.dve(lambda e, g=g, i=i, ptb=ptb:
                         e.tensor_copy(xn3[:, g * 4:(g + 1) * 4, ts(i, 128)],
                                       ptb.ap()[:, 0:512].rearrange("p (a b) -> p a b", a=4)),
                         reads=(ptb,), writes=(xn_fm,))

    def emit_front(self, is_prev, xn_fm, sc):
        c = self.ctx
        w_ap = self.inp["w_in"].ap()
        w_buf = self.inp["w_in"]
        xn3 = xn_fm.ap()[:, :].rearrange("p (dc t) -> p dc t", dc=DC)
        for idx, cc in enumerate(range(32, 67)):
            wc = sc["wcol"][idx % 2]
            c.dma("qpool", wc.ap()[:, :], w_ap[ts(cc, 128), :],
                  reads=(w_buf,), writes=(wc,))
            self.emit_gamma_step(2)
            self.mm_kouter(
                [(self.pb[tg], self.pb[tg].ap()[:, :]) for tg in range(4)],
                [(wc.ap()[:, ts(dc, 128)],
                  [xn3[:, dc, ts(tg, 512)] for tg in range(4)], (wc, xn_fm))
                 for dc in range(DC)])
            for tg in range(4):
                pa = self.pb[tg]
                if cc < 66:
                    self.act(lambda e, tg=tg, pa=pa: e.activation(
                        sc["conv_in"].ap()[:, 3 + tg * 512:3 + (tg + 1) * 512],
                        pa.ap()[:, :], AF.Copy),
                        reads=(pa,), writes=(sc["conv_in"],))
                else:
                    self.act(lambda e, tg=tg, pa=pa: e.activation(
                        sc["dtraw"].ap()[:, ts(tg, 512)], pa.ap()[0:64, :], AF.Copy),
                        reads=(pa,), writes=(sc["dtraw"],))
            if cc < 66:
                self.emit_conv_chunk(cc - 32, is_prev, sc)
        self.emit_dt_pipeline(is_prev, sc)

    def emit_conv_chunk(self, rel, is_prev, sc):
        c = self.ctx
        ci, cw, cb = sc["conv_in"], sc["cw_sb"], sc["cb_sb"]
        c.dma("qsp", cw.ap()[:, :], self.inp["conv_w0"].ap()[ts(rel, 128), :],
              reads=(self.inp["conv_w0"],), writes=(cw,))
        c.dma("qsp", cb.ap()[:, :], self.inp["conv_b0"].ap()[ts(rel, 128), :],
              reads=(self.inp["conv_b0"],), writes=(cb,))
        if is_prev:
            self.dve(lambda e: e.memset(ci.ap()[:, 0:3], 0.0), writes=(ci,))
        else:
            if rel < 34:
                self.dve(lambda e, rel=rel: e.tensor_copy(
                    ci.ap()[:, 0:3], self.halobuf.ap()[:, rel * 3:rel * 3 + 3]),
                    reads=(self.halobuf,), writes=(ci,))
        acc = sc["acc"]
        self.dve(lambda e: e.tensor_scalar(acc.ap()[:, 0:T], ci.ap()[:, 0:T],
                                           cw.ap()[:, 0:1], cb.ap()[:, 0:1],
                                           op0=OP.mult, op1=OP.add),
                 reads=(ci, cw, cb), writes=(acc,))
        for k in range(1, 4):
            self.dve(lambda e, k=k: e.scalar_tensor_tensor(
                acc.ap()[:, 0:T], ci.ap()[:, k:k + T], cw.ap()[:, k:k + 1],
                acc.ap()[:, 0:T], op0=OP.mult, op1=OP.add),
                reads=(ci, cw, acc), writes=(acc,))
        if is_prev and rel < 34:
            self.dve(lambda e, rel=rel: e.tensor_scalar_mul(
                self.halobuf.ap()[:, rel * 3:rel * 3 + 3], ci.ap()[:, T:T + 3],
                self.maskc.ap()[:, 0:1]),
                reads=(ci, self.maskc), writes=(self.halobuf,))
        if rel < 32:
            self.act(lambda e: e.activation(sc["xbcs"].ap()[:, :], acc.ap()[:, 0:T], AF.Silu),
                     reads=(acc,), writes=(sc["xbcs"],))
            c.dma("qact", self.x_scr.ap()[ts(rel, 128), :], sc["xbcs"].ap()[:, :],
                  reads=(sc["xbcs"],), writes=(self.x_scr,))
        elif rel == 32:
            self.act(lambda e: e.activation(self.Bfm.ap()[:, :], acc.ap()[:, 0:T], AF.Silu),
                     reads=(acc,), writes=(self.Bfm,))
        else:
            self.act(lambda e: e.activation(self.Cfm.ap()[:, :], acc.ap()[:, 0:T], AF.Silu),
                     reads=(acc,), writes=(self.Cfm,))

    def emit_dt_pipeline(self, is_prev, sc):
        c = self.ctx
        dtr, dtb = sc["dtraw"], sc["dtb_sb"]
        w0 = sc["dtw0"]; w1 = sc["dtw1"]
        self.dve(lambda e: e.tensor_scalar_add(w0.ap(), dtr.ap()[:, :], dtb.ap()[:, 0:1]),
                 reads=(dtr, dtb), writes=(w0,))
        self.act(lambda e: e.activation(w1.ap(), w0.ap(), AF.Abs),
                 reads=(w0,), writes=(w1,))
        self.act(lambda e: e.activation(w1.ap(), w1.ap(), AF.Exp, scale=-1.0),
                 reads=(w1,), writes=(w1,))
        self.act(lambda e: e.activation(w1.ap(), w1.ap(), AF.Ln,
                                        bias=self.cb(1.0).ap()[0:64, :]),
                 reads=(w1, self.cb(1.0)), writes=(w1,))
        self.dve(lambda e: e.tensor_scalar_max(w0.ap(), w0.ap(), 0.0),
                 reads=(w0,), writes=(w0,))
        self.dve(lambda e: e.tensor_add(self.dt_sb.ap()[:, :], w0.ap(), w1.ap()),
                 reads=(w0, w1), writes=(self.dt_sb,))
        self.act(lambda e: e.activation(w1.ap(), self.dt_sb.ap()[:, :], AF.Ln),
                 reads=(self.dt_sb,), writes=(w1,))
        self.dve(lambda e: e.tensor_scalar_mul(w0.ap(), self.dt_sb.ap()[:, :],
                                               sc["a0_sb"].ap()[:, 0:1]),
                 reads=(self.dt_sb, sc["a0_sb"]), writes=(w0,))
        for cch in range(NCH):
            self.dve(lambda e, cch=cch: e.tensor_tensor_scan(
                self.cA.ap()[:, ts(cch, 128)], w0.ap()[:, ts(cch, 128)],
                self.zeros64.ap()[:, :], 0.0, op0=OP.add, op1=OP.add),
                reads=(w0, self.zeros64), writes=(self.cA,))
        self.dve(lambda e: e.tensor_sub(w1.ap(), w1.ap(), self.cA.ap()[:, :]),
                 reads=(w1, self.cA), writes=(w1,))
        c.dma("qact", self.cA_scr.ap()[:, :], self.cA.ap()[:, :],
              reads=(self.cA,), writes=(self.cA_scr,))
        c.dma("qact", self.rr_scr.ap()[:, :], w1.ap(),
              reads=(w1,), writes=(self.rr_scr,))

    def flat_row(self, scr_buf, cch, part):
        t = scr_buf.ap().tensor
        return bass.AP(t, part * 16 * T + cch * 128, [[1, 1], [T, 16], [1, 128]])

    def emit_chunk_smalls(self, cch, need_decay, sb):
        w0 = sb["csc"]
        cAc = self.cA.ap()[:, ts(cch, 128)]
        last = self.cA.ap()[:, cch * 128 + 127:cch * 128 + 128]
        pa = self.pb[4]
        self.act(lambda e: e.activation(w0.ap()[:, 0:128], cAc, AF.Exp, scale=-1.0,
                                        bias=last),
                 reads=(self.cA,), writes=(w0,))
        self.dve(lambda e: e.tensor_mul(w0.ap()[:, 128:256], w0.ap()[:, 0:128],
                                        self.dt_sb.ap()[:, ts(cch, 128)]),
                 reads=(w0, self.dt_sb), writes=(w0,))
        self.transpose(pa, pa.ap()[:, 0:64], w0.ap()[:, 128:256],
                       self.identf.ap()[0:64, 0:64], reads=(w0, self.identf))
        self.dve(lambda e, pa=pa: e.tensor_copy(sb["w_tm"].ap()[:, :], pa.ap()[:, 0:64]),
                 reads=(pa,), writes=(sb["w_tm"],))
        if need_decay:
            self.act(lambda e: e.activation(w0.ap()[:, 256:384], cAc, AF.Exp),
                     reads=(self.cA,), writes=(w0,))
            self.transpose(pa, pa.ap()[:, 64:128], w0.ap()[:, 256:384],
                           self.identf.ap()[0:64, 0:64], reads=(w0, self.identf))
            self.dve(lambda e, pa=pa: e.tensor_copy(sb["expCA_tm"].ap()[:, :],
                                                    pa.ap()[:, 64:128]),
                     reads=(pa,), writes=(sb["expCA_tm"],))
        self.act(lambda e: e.activation(w0.ap()[:, 384:385], last, AF.Exp),
                 reads=(self.cA,), writes=(w0,))
        self.transpose(pa, pa.ap()[0:1, 128:192], w0.ap()[:, 384:385],
                       self.identf.ap()[0:64, 0:64], reads=(w0, self.identf))
        self.dve(lambda e, pa=pa: e.tensor_copy(sb["cdrow"].ap()[:, :], pa.ap()[0:1, 128:192]),
                 reads=(pa,), writes=(sb["cdrow"],))
        self.kone(sb["cdrow"].ap()[0:1, :], pa, pa.ap()[:, 192:256], (sb["cdrow"],))
        self.dve(lambda e, pa=pa: e.tensor_copy(sb["cdcols"].ap()[:, :], pa.ap()[:, 192:256]),
                 reads=(pa,), writes=(sb["cdcols"],))

    def emit_ssd_chunk(self, cch, full, sb):
        c = self.ctx
        x_tm = sb["x_tm"][cch % 2]
        c.dma("qsp", x_tm.ap()[:, :], self.x_scr.ap()[:, ts(cch, 128)],
              reads=(self.x_scr,), writes=(x_tm,), transpose=True)
        self.emit_chunk_smalls(cch, full, sb)
        ptb = self.pt[0]
        self.transpose(ptb, ptb.ap()[:, 0:128], self.Bfm.ap()[:, ts(cch, 128)],
                       self.identb.ap()[:, :], reads=(self.Bfm, self.identb))
        self.act(lambda e, ptb=ptb: e.activation(sb["B_tm"].ap()[:, :], ptb.ap()[:, 0:128],
                                                 AF.Copy),
                 reads=(ptb,), writes=(sb["B_tm"],))
        if full:
            pcb = self.pb[5]
            self.mm_acc(pcb, pcb.ap()[:, 0:128], [
                (self.Cfm.ap()[:, ts(cch, 128)], self.Bfm.ap()[:, ts(cch, 128)],
                 (self.Cfm, self.Bfm))])
            self.act(lambda e, pcb=pcb: e.activation(sb["CBt"].ap()[:, :],
                                                     pcb.ap()[:, 0:128], AF.Copy),
                     reads=(pcb,), writes=(sb["CBt"],))
            for part in range(2):
                self._load_rows(cch, part, sb)
        for g4 in range(16):
            if full and g4 in (4, 8):
                self._load_rows(cch, g4 // 4 + 1, sb)
            self._ssd_group(cch, g4, full, sb)

    def _load_rows(self, cch, part, sb):
        c = self.ctx
        l2 = sb["L2"][part % 2]
        r2 = sb["R2"][part % 2]
        c.dma("qsp", l2.ap()[0:1, :], self.flat_row(self.cA_scr, cch, part),
              reads=(self.cA_scr,), writes=(l2,))
        c.dma("qsp", r2.ap()[1:2, :], self.flat_row(self.rr_scr, cch, part),
              reads=(self.rr_scr,), writes=(r2,))

    def _ssd_group(self, cch, g4, full, sb):
        h0 = 4 * g4
        x_tm = sb["x_tm"][cch % 2]
        carry = self.carry
        xsl = x_tm.ap()[:, h0 * 64:(h0 + 4) * 64]
        csl = carry.ap()[:, h0 * 64:(h0 + 4) * 64]
        if full:
            l2 = sb["L2"][(h0 // 16) % 2]
            r2 = sb["R2"][(h0 // 16) % 2]
            pdiff = self.pb[g4 % 2]
            for j in range(4):
                hh = (h0 + j) % 16
                self.mm_acc(pdiff, pdiff.ap()[:, ts(j, 128)], [
                    (l2.ap()[0:2, ts(hh, 128)], r2.ap()[0:2, ts(hh, 128)], (l2, r2))])
            self.dve(lambda e, pdiff=pdiff: e.tensor_add(
                sb["D4"].ap()[:, :], pdiff.ap()[:, :], self.m_low0(rep=4)),
                reads=(pdiff, self.consts), writes=(sb["D4"],))
            self.act(lambda e: e.activation(sb["E4"].ap()[:, :], sb["D4"].ap()[:, :], AF.Exp),
                     reads=(sb["D4"],), writes=(sb["E4"],))
            self.dve(lambda e: e.tensor_mul(sb["M4"].ap()[:, :], sb["E4"].ap()[:, :],
                                            ap_mid0(sb["CBt"].ap()[:, :], 4)),
                     reads=(sb["E4"], sb["CBt"]), writes=(sb["M4"],))
        self.dve(lambda e: e.tensor_mul(
            sb["xw4"].ap()[:, :], xsl, ap_append0(sb["w_tm"].ap()[:, h0:h0 + 4], 64)),
            reads=(x_tm, sb["w_tm"]), writes=(sb["xw4"],))
        pstat = self.pb[3]
        self.mm_acc(pstat, pstat.ap()[:, 0:256], [
            (sb["B_tm"].ap()[:, :], sb["xw4"].ap()[:, 0:256],
             (sb["B_tm"], sb["xw4"]))])
        if full:
            pY = self.pb[2]
            for j in range(4):
                self.mm_acc(pY, pY.ap()[:, ts(j, 64)], [
                    (sb["M4"].ap()[:, ts(j, 128)],
                     x_tm.ap()[:, (h0 + j) * 64:(h0 + j + 1) * 64],
                     (sb["M4"], x_tm))])
            self.dve(lambda e: e.tensor_copy(sb["carry_bf"].ap()[:, :], csl),
                     reads=(carry,), writes=(sb["carry_bf"],))
            pI = self.pb[5]
            self.mm_acc(pI, pI.ap()[:, 0:256], [
                (self.Cfm.ap()[:, ts(cch, 128)],
                 sb["carry_bf"].ap()[:, 0:256],
                 (self.Cfm, sb["carry_bf"]))])
            self.dve(lambda e, pI=pI: e.tensor_mul(
                sb["t1"].ap()[:, :], pI.ap()[:, 0:256],
                ap_append0(sb["expCA_tm"].ap()[:, h0:h0 + 4], 64)),
                reads=(pI, sb["expCA_tm"]), writes=(sb["t1"],))
            self.dve(lambda e, pY=pY: e.tensor_add(sb["t2"].ap()[:, :], sb["t1"].ap()[:, :],
                                                   pY.ap()[:, 0:256]),
                     reads=(sb["t1"], pY), writes=(sb["t2"],))
            self.dve(lambda e: e.tensor_mul(
                sb["xD4"].ap()[:, :], xsl, ap_append0(sb["dcols"].ap()[:, h0:h0 + 4], 64)),
                reads=(x_tm, sb["dcols"]), writes=(sb["xD4"],))
            self.dve(lambda e: e.tensor_add(
                sb["Y_tm"].ap()[:, h0 * 64:(h0 + 4) * 64], sb["t2"].ap()[:, :],
                sb["xD4"].ap()[:, :]),
                reads=(sb["t2"], sb["xD4"]), writes=(sb["Y_tm"],))
        self.dve(lambda e: e.tensor_mul(
            sb["tscan"].ap()[:, :], csl, ap_append0(sb["cdcols"].ap()[:, h0:h0 + 4], 64)),
            reads=(carry, sb["cdcols"]), writes=(sb["tscan"],))
        self.dve(lambda e, pstat=pstat: e.tensor_add(csl, sb["tscan"].ap()[:, :],
                                                     pstat.ap()[:, 0:256]),
                 reads=(sb["tscan"], pstat), writes=(carry,))

    def emit_z_chunk(self, cch, sb):
        c = self.ctx
        gt = sb["gate_tile"][0]
        c.dma("qsp", gt.ap()[:, :],
              self.gate_scr.ap()[:, ts(cch, 128)], reads=(self.gate_scr,), writes=(gt,),
              transpose=True)
        self.act(lambda e, gt=gt: e.activation(gt.ap()[:, :], gt.ap()[:, :], AF.Silu),
                 reads=(gt,), writes=(gt,))
        self.dve(lambda e, gt=gt: e.tensor_mul(sb["Y_tm"].ap()[:, :], sb["Y_tm"].ap()[:, :],
                                               gt.ap()[:, :]),
                 reads=(sb["Y_tm"], gt), writes=(sb["Y_tm"],))
        st = self.stats
        self.act(lambda e: e.activation(sb["dump"].ap()[:, :], sb["Y_tm"].ap()[:, :],
                                        AF.Square, accum_out=st.ap()[:, 8:9]),
                 reads=(sb["Y_tm"],), writes=(sb["dump"], st))
        self.act(lambda e: e.activation(st.ap()[:, 9:10], st.ap()[:, 8:9], AF.Ln,
                                        scale=1.0 / INTER, bias=self.cb(1e-5).ap()[:, :]),
                 reads=(st, self.cb(1e-5)), writes=(st,))
        self.act(lambda e: e.activation(st.ap()[:, 10:11], st.ap()[:, 9:10], AF.Exp,
                                        scale=-0.5),
                 reads=(st,), writes=(st,))
        self.dve(lambda e: e.tensor_scalar_mul(sb["dump"].ap()[:, :], sb["Y_tm"].ap()[:, :],
                                               st.ap()[:, 10:11]),
                 reads=(sb["Y_tm"], st), writes=(sb["dump"],))
        zf3 = sb["zn_fm"].ap()[:, :].rearrange("p (ic t) -> p ic t", ic=32)
        for g in range(8):
            ptb = self.pt[g % 2]
            for j in range(4):
                ic = g * 4 + j
                self.transpose(ptb, ptb.ap()[:, ts(j, 128)], sb["dump"].ap()[:, ts(ic, 128)],
                               self.identb.ap()[:, :], reads=(sb["dump"], self.identb))
            if g % 2 == 0:
                self.dve(lambda e, g=g, ptb=ptb: e.tensor_copy(
                    zf3[:, g * 4:(g + 1) * 4, :], ptb.ap()[:, 0:512].rearrange(
                        "p (a b) -> p a b", a=4)),
                    reads=(ptb,), writes=(sb["zn_fm"],))
            else:
                self.act(lambda e, g=g, ptb=ptb: e.activation(
                    zf3[:, g * 4:(g + 1) * 4, :], ptb.ap()[:, 0:512].rearrange(
                        "p (a b) -> p a b", a=4), AF.Copy),
                    reads=(ptb,), writes=(sb["zn_fm"],))
        c.dma("qact", self.zn_scr.ap()[:, ts(cch, 4096)], sb["zn_fm"].ap()[:, :],
              reads=(sb["zn_fm"],), writes=(self.zn_scr,))

    def emit_gate(self, xn_fm, sc):
        c = self.ctx
        w_ap = self.inp["w_in"].ap()
        xn3 = xn_fm.ap()[:, :].rearrange("p (dc t) -> p dc t", dc=DC)
        for cc in range(32):
            wc = sc["wcol"][cc % 2]
            c.dma("qpool", wc.ap()[:, :], w_ap[ts(cc, 128), :],
                  reads=(self.inp["w_in"],), writes=(wc,))
            self.mm_kouter(
                [(self.pb[tg], self.pb[tg].ap()[:, :]) for tg in range(4)],
                [(wc.ap()[:, ts(dc, 128)],
                  [xn3[:, dc, ts(tg, 512)] for tg in range(4)], (wc, xn_fm))
                 for dc in range(DC)])
            for tg in range(4):
                pa = self.pb[tg]
                self.act(lambda e, pa=pa, tg=tg: e.activation(
                    sc["gstage"].ap()[:, ts(tg, 512)], pa.ap()[:, :], AF.Copy),
                    reads=(pa,), writes=(sc["gstage"],))
            c.dma("qact", self.gate_scr.ap()[ts(cc, 128), :],
                  sc["gstage"].ap()[:, :], reads=(sc["gstage"],),
                  writes=(self.gate_scr,))

    def emit_wout(self, zsrc=None, w_key="w_out", nw_key="nw0", res_buf=None,
                  dest=None):
        c = self.ctx
        if zsrc is None:
            zsrc = self.zn_scr
        if res_buf is None:
            res_buf = self.inp["x_own"]
        if dest is None:
            dest = self.x1a_scr
        with c.scope():
            nw_sb = c.sbuf("nw_sb", [128, 32], F32)
            c.dma("qsp", nw_sb.ap()[:, :], self.inp[nw_key].ap(),
                  reads=(self.inp[nw_key],), writes=(nw_sb,))
            wof = [c.sbuf(f"wof{i}", [128, 512], F32) for i in range(2)]
            wos = [c.sbuf(f"wos{i}", [128, 32 * 512], BF16) for i in range(2)]
            zld = [c.sbuf(f"zld{i}", [128, 4096], BF16) for i in range(2)]
            xres = [c.sbuf(f"xres{i}", [128, 512], F32) for i in range(2)]
            x1t = [c.sbuf(f"x1t{i}", [128, 512], F32) for i in range(2)]
            for ds in range(4):
                wo = wos[ds % 2]
                for ic in range(32):
                    wf = wof[ic % 2]
                    c.dma("qsp", wf.ap()[:, :],
                          self.inp[w_key].ap()[ts(ic, 128), ts(ds, 512)],
                          reads=(self.inp[w_key],), writes=(wf,))
                    self.dve(lambda e, wf=wf, ic=ic, wo=wo: e.tensor_scalar_mul(
                        wo.ap()[:, ts(ic, 512)], wf.ap()[:, :], nw_sb.ap()[:, ic:ic + 1]),
                        reads=(wf, nw_sb), writes=(wo,))
                for tt in range(TT):
                    zl = zld[tt % 2]
                    c.dma("qsp", zl.ap()[:, :], zsrc.ap()[:, ts(tt, 4096)],
                          reads=(zsrc,), writes=(zl,))
                    xr = xres[tt % 2]
                    c.dma("qsp", xr.ap()[:, :],
                          res_buf.ap()[ts(tt, 128), ts(ds, 512)],
                          reads=(res_buf,), writes=(xr,))
                    pa = self.pb[tt % 2]
                    self.mm_acc(pa, pa.ap()[:, :], [
                        (zl.ap()[:, ts(ic, 128)], wo.ap()[:, ts(ic, 512)], (zl, wo))
                        for ic in range(32)])
                    xt = x1t[tt % 2]
                    self.dve(lambda e, pa=pa, xr=xr, xt=xt: e.tensor_add(
                        xt.ap()[:, :], pa.ap()[:, :], xr.ap()[:, :]),
                        reads=(pa, xr), writes=(xt,))
                    c.dma("qact", dest.ap()[ts(tt, 128), ts(ds, 512)],
                          xt.ap()[:, :], reads=(xt,), writes=(dest,))

    def alloc_l0_wide(self):
        c = self.ctx
        self.emit_gamma_setup()
        self.carry = c.sbuf("carry", [128, NH * HD], F32)
        self.halobuf = c.sbuf("halobuf", [128, 3 * 34], F32)
        self.Bfm = c.sbuf("Bfm", [128, T], BF16)
        self.Cfm = c.sbuf("Cfm", [128, T], BF16)
        self.dt_sb = c.sbuf("dt_sb", [64, T], F32)
        self.cA = c.sbuf("cA", [64, T], F32)
        self.zeros64 = c.sbuf("zeros64", [64, 128], F32)
        self.dve(lambda e: e.memset(self.zeros64.ap()[:, :], 0.0), writes=(self.zeros64,))

    def emit_l0_attn(self):
        c = self.ctx
        for phase in ("prev", "own"):
            is_prev = phase == "prev"
            with c.scope():
                sc = {}
                xn_fm = c.sbuf("xn_fm", [128, DC * T], BF16)
                sc["wcol"] = [c.sbuf(f"wcol{i}", [128, D], BF16) for i in range(2)]
                sc["conv_in"] = c.sbuf("conv_in", [128, T + 3], F32)
                sc["xbcs"] = c.sbuf("xbcs", [128, T], BF16)
                sc["acc"] = c.sbuf("acc", [128, T], F32)
                sc["dtraw"] = c.sbuf("dtraw", [64, T], F32)
                sc["dtw0"] = DTView(sc["conv_in"], 0)
                sc["dtw1"] = DTView(sc["acc"], 0)
                sc["dump"] = c.sbuf("dumpA", [128, D], BF16)
                sc["cw_sb"] = c.sbuf("cw_sb", [128, 4], F32)
                sc["cb_sb"] = c.sbuf("cb_sb", [128, 1], F32)
                sc["dtb_sb"] = c.sbuf("dtb_sb", [64, 1], F32)
                sc["a0_sb"] = c.sbuf("a0_sb", [64, 1], F32)
                xld = c.sbuf("xld", [128, D], F32)
                xnstage = c.sbuf("xnstage", [128, D], BF16)
                c.dma("qsp", sc["dtb_sb"].ap()[:, :], self.inp["dt_bias0"].ap(),
                      reads=(self.inp["dt_bias0"],), writes=(sc["dtb_sb"],))
                c.dma("qsp", sc["a0_sb"].ap()[:, :], self.inp["a0"].ap(),
                      reads=(self.inp["a0"],), writes=(sc["a0_sb"],))
                src = self.inp["x_prev" if is_prev else "x_own"]
                self.emit_ln(src, Buf("xn_fm_view", xn_fm.ap_fn) if False else xn_fm,
                             xld, xnstage, sc["dump"])
                if not is_prev:
                    self.dbg_dump("xn_fm", xn_fm, xn_fm.ap()[:, :], [128, DC * T], BF16)
                self.emit_front(is_prev, xn_fm, sc)
                if not is_prev:
                    sc["gstage"] = c.sbuf("gstage", [128, 2048], BF16)
                    self.emit_gate(xn_fm, sc)
                    c.dma("qact", self.xn_scr.ap(), xn_fm.ap()[:, :],
                          reads=(xn_fm,), writes=(self.xn_scr,))
            with c.scope():
                sb = {}
                sb["x_tm"] = [c.sbuf(f"x_tm{i}", [128, INTER], BF16) for i in range(2)]
                sb["B_tm"] = c.sbuf("B_tm", [128, 128], BF16)
                sb["w_tm"] = c.sbuf("w_tm", [128, NH], F32)
                sb["expCA_tm"] = c.sbuf("expCA_tm", [128, NH], F32)
                sb["cdrow"] = c.sbuf("cdrow", [1, NH], F32)
                sb["cdcols"] = c.sbuf("cdcols", [128, NH], F32)
                sb["csc"] = c.sbuf("csc", [64, 512], F32)
                sb["L2"] = [c.sbuf(f"L2_{i}", [2, 2048], F32) for i in range(2)]
                sb["R2"] = [c.sbuf(f"R2_{i}", [2, 2048], F32) for i in range(2)]
                sb["CBt"] = c.sbuf("CBt", [128, 128], F32)
                sb["D4"] = c.sbuf("D4", [128, 512], F32)
                sb["E4"] = c.sbuf("E4", [128, 512], F32)
                sb["M4"] = c.sbuf("M4", [128, 512], BF16)
                sb["xw4"] = c.sbuf("xw4", [128, 256], BF16)
                sb["xD4"] = c.sbuf("xD4", [128, 256], F32)
                sb["t1"] = c.sbuf("t1", [128, 256], F32)
                sb["t2"] = c.sbuf("t2", [128, 256], F32)
                sb["tscan"] = c.sbuf("tscan", [128, 256], F32)
                for i in range(2):
                    c.dma("qsp", sb["L2"][i].ap()[1:2, :], self.inp["ones_row"].ap(),
                          reads=(self.inp["ones_row"],), writes=(sb["L2"][i],))
                    self.dve(lambda e, i=i: e.memset(sb["R2"][i].ap()[0:1, :], 1.0),
                             writes=(sb["R2"][i],))
                if is_prev:
                    self.dve(lambda e: e.memset(self.carry.ap()[:, :], 0.0),
                             writes=(self.carry,))
                else:
                    sb["Y_tm"] = c.sbuf("Y_tm", [128, INTER], F32)
                    sb["carry_bf"] = c.sbuf("carry_bf", [128, 256], BF16)
                    sb["gate_tile"] = [c.sbuf("gt0", [128, INTER], BF16)]
                    sb["dump"] = c.sbuf("dumpB", [128, INTER], BF16)
                    sb["zn_fm"] = c.sbuf("zn_fm", [128, INTER], BF16)
                    sb["drow"] = c.sbuf("drow", [1, NH], F32)
                    sb["dcols"] = c.sbuf("dcols", [128, NH], F32)
                    c.dma("qsp", sb["drow"].ap()[:, :], self.inp["d0_row"].ap(),
                          reads=(self.inp["d0_row"],), writes=(sb["drow"],))
                    pa = self.pb[4]
                    self.kone(sb["drow"].ap()[0:1, :], pa, pa.ap()[:, 0:64], (sb["drow"],))
                    self.dve(lambda e, pa=pa: e.tensor_copy(sb["dcols"].ap()[:, :],
                                                            pa.ap()[:, 0:64]),
                             reads=(pa,), writes=(sb["dcols"],))
                    self.dve(lambda e: e.tensor_scalar_mul(self.carry.ap()[:, :],
                                                           self.carry.ap()[:, :],
                                                           self.maskc.ap()[:, 0:1]),
                             reads=(self.carry, self.maskc), writes=(self.carry,))
                for cch in range(NCH):
                    self.emit_ssd_chunk(cch, not is_prev, sb)
                    if not is_prev:
                        self.emit_z_chunk(cch, sb)
                if not is_prev:
                    self.dbg_dump("Y_l0", sb["Y_tm"], sb["Y_tm"].ap()[:, :],
                                  [128, INTER], F32)
        self.emit_gamma_tail()
        self.emit_wout()


class DTView:
    def __init__(self, buf, p0):
        self.buf = buf
        self.p0 = p0
        self.name = buf.name + f"@{p0}"

    def ap(self):
        return self.buf.ap()[self.p0:self.p0 + 64, 0:T]

    @property
    def last_w(self):
        return self.buf.last_w

    @last_w.setter
    def last_w(self, v):
        self.buf.last_w = v

    @property
    def readers(self):
        return self.buf.readers

    @readers.setter
    def readers(self, v):
        self.buf.readers = v


class MKFFN:
    pass


def _ffn_methods():
    def emit_gamma_setup(self):
        c = self.ctx
        self.g_gld = [c.sbuf(f"gld{i}", [128, 2048], F32) for i in range(2)]
        self.g_gs = c.sbuf("gs", [128, 8], F32)
        self.g_gtmp = c.sbuf("gtmp", [128, 2048], BF16)
        self.g_ones = c.sbuf("onescol", [128, 1], F32)
        self.g_grow = c.sbuf("grow", [1, 8], F32)
        self.dve(lambda e: e.memset(self.g_ones.ap()[:, :], 1.0), writes=(self.g_ones,))
        self.dve(lambda e: e.memset(self.g_gs.ap()[:, :], 0.0), writes=(self.g_gs,))
        self.g_units = [(wi, wname, rs)
                        for wi, wname in enumerate(("ffn_w1t", "ffn_w2t", "ffn_w3t"))
                        for rs in range(F1 // 128)]
        self.g_idx = 0

    def emit_gamma_step(self, k=1):
        c = self.ctx
        for _ in range(k):
            if self.g_idx >= len(self.g_units):
                return
            wi, wname, rs = self.g_units[self.g_idx]
            self.g_idx += 1
            g = self.g_gld[self.g_idx % 2]
            w = self.inp[wname]
            c.dma("qact", g.ap()[:, :], w.ap()[ts(rs, 128), :], reads=(w,), writes=(g,))
            self.act(lambda e, g=g: e.activation(
                self.g_gtmp.ap()[:, :], g.ap()[:, :], AF.Abs,
                accum_out=self.g_gs.ap()[:, 4:5]),
                reads=(g,), writes=(self.g_gtmp, self.g_gs))
            self.dve(lambda e, wi=wi: e.tensor_add(
                self.g_gs.ap()[:, wi:wi + 1], self.g_gs.ap()[:, wi:wi + 1],
                self.g_gs.ap()[:, 4:5]),
                reads=(self.g_gs,), writes=(self.g_gs,))

    def emit_gamma_tail(self):
        self.emit_gamma_step(len(self.g_units) - self.g_idx)
        gs, grow = self.g_gs, self.g_grow
        pa = self.pb[4]
        self.mm_acc(pa, pa.ap()[0:1, 0:3], [
            (self.g_ones.ap()[:, :], gs.ap()[:, 0:3], (self.g_ones, gs))])
        self.dve(lambda e, pa=pa: e.tensor_copy(grow.ap()[0:1, 0:3], pa.ap()[0:1, 0:3]),
                 reads=(pa,), writes=(grow,))
        self.kone(grow.ap()[0:1, 0:3], pa, pa.ap()[:, 8:11], (grow,))
        n1 = float(D * F1)
        for wi, n in enumerate((n1, n1, n1)):
            self.dve(lambda e, wi=wi, n=n, pa=pa: e.tensor_scalar(
                self.gthr.ap()[:, wi:wi + 1], pa.ap()[:, 8 + wi:9 + wi],
                0.5 / n, 0.5e-8, op0=OP.mult, op1=OP.add),
                reads=(pa,), writes=(self.gthr,))
            self.dve(lambda e, wi=wi: e.tensor_scalar_mul(
                self.gthr.ap()[:, 3 + wi:4 + wi], self.gthr.ap()[:, wi:wi + 1], -1.0),
                reads=(self.gthr,), writes=(self.gthr,))

    def _quant(self, dst_buf, src_buf, wi, ta, tb):
        self.dve(lambda e: e.tensor_single_scalar(ta.ap()[:, :], src_buf.ap()[:, :],
                                                  self.gthr.ap()[:, wi:wi + 1], op=OP.is_gt),
                 reads=(src_buf, self.gthr), writes=(ta,))
        self.dve(lambda e: e.tensor_single_scalar(tb.ap()[:, :], src_buf.ap()[:, :],
                                                  self.gthr.ap()[:, 3 + wi:4 + wi], op=OP.is_lt),
                 reads=(src_buf, self.gthr), writes=(tb,))
        self.dve(lambda e: e.tensor_sub(dst_buf.ap()[:, :], ta.ap()[:, :], tb.ap()[:, :]),
                 reads=(ta, tb), writes=(dst_buf,))

    def emit_ffn(self, dest_scr):
        c = self.ctx
        q12 = [self.dscr("q1_scr", [F1, D], BF16), self.dscr("q2_scr", [F1, D], BF16)]
        for th in range(2):
            with c.scope():
                xn_h = c.sbuf("xn_ffn", [128, DC * 1024], BF16)
                xa = self.xn_scr.ap()[:, :].rearrange("p (dc t) -> p dc t", dc=DC)
                c.dma("qsp", xn_h.ap()[:, :].rearrange("p (dc t) -> p dc t", dc=DC),
                      xa[:, :, th * 1024:(th + 1) * 1024],
                      reads=(self.xn_scr,), writes=(xn_h,))
                xn3 = xn_h.ap()[:, :].rearrange("p (dc t) -> p dc t", dc=DC)
                h_sb = c.sbuf("h_sb", [128, FC1 * 1024], BF16)
                sgm = c.sbuf("sgm", [128, 512], F32)

                def pass1_fc(fc, q1t, q2t):
                    self.mm_kouter(
                        [(self.pb[tg], self.pb[tg].ap()[:, :]) for tg in range(2)],
                        [(q1t.ap()[:, ts(dc, 128)],
                          [xn3[:, dc, ts(tg, 512)] for tg in range(2)], (q1t, xn_h))
                         for dc in range(DC)])
                    self.mm_kouter(
                        [(self.pb[2 + tg], self.pb[2 + tg].ap()[:, :]) for tg in range(2)],
                        [(q2t.ap()[:, ts(dc, 128)],
                          [xn3[:, dc, ts(tg, 512)] for tg in range(2)], (q2t, xn_h))
                         for dc in range(DC)])
                    for tg in range(2):
                        p1 = self.pb[tg]
                        p2 = self.pb[2 + tg]
                        self.act(lambda e, p1=p1: e.activation(sgm.ap()[:, :],
                                                               p1.ap()[:, :], AF.Sigmoid),
                                 reads=(p1,), writes=(sgm,))
                        self.dve(lambda e, p2=p2, fc=fc, tg=tg: e.tensor_mul(
                            h_sb.ap()[:, fc * 1024 + tg * 512:fc * 1024 + (tg + 1) * 512],
                            sgm.ap()[:, :], p2.ap()[:, :]),
                            reads=(sgm, p2), writes=(h_sb,))
                if th == 0:
                    with c.scope():
                        wld = [c.sbuf(f"wld{i}", [128, 2048], F32) for i in range(2)]
                        w3ld = [c.sbuf(f"w3ld{i}", [128, 2048], F32) for i in range(2)]
                        qt = c.sbuf("qt", [128, 2048], BF16)
                        qt2 = c.sbuf("qt2", [128, 2048], BF16)
                        q3t = c.sbuf("q3t", [128, 2048], BF16)
                        q12t = [c.sbuf(f"q12t{i}", [128, 2048], BF16) for i in range(2)]
                        for fc in range(FC1):
                            tiles = []
                            for wi, wn in enumerate(("ffn_w1t", "ffn_w2t")):
                                wl = wld[wi]
                                c.dma("qsp", wl.ap()[:, :],
                                      self.inp[wn].ap()[ts(fc, 128), :],
                                      reads=(self.inp[wn],), writes=(wl,))
                                qd = q12t[wi]
                                self._quant(qd, wl, wi, qt, qt2)
                                c.dma("qact", q12[wi].ap()[ts(fc, 128), :], qd.ap()[:, :],
                                      reads=(qd,), writes=(q12[wi],))
                                tiles.append(qd)
                            wl3 = w3ld[fc % 2]
                            c.dma("qsp", wl3.ap()[:, :],
                                  self.inp["ffn_w3t"].ap()[ts(fc, 128), :],
                                  reads=(self.inp["ffn_w3t"],), writes=(wl3,))
                            self._quant(q3t, wl3, 2, qt, qt2)
                            c.dma("qact", self.q3_scr.ap()[ts(fc, 128), :],
                                  q3t.ap()[:, :],
                                  reads=(q3t,), writes=(self.q3_scr,))
                            pass1_fc(fc, *tiles)
                else:
                    q12l = [c.sbuf(f"q12l{i}", [128, 2048], BF16) for i in range(4)]
                    for fc in range(FC1):
                        tiles = []
                        for wi in range(2):
                            ql = q12l[(fc * 2 + wi) % 4]
                            c.dma("qsp", ql.ap()[:, :], q12[wi].ap()[ts(fc, 128), :],
                                  reads=(q12[wi],), writes=(ql,))
                            tiles.append(ql)
                        pass1_fc(fc, *tiles)
                q3sb = c.sbuf("q3sb", [128, FC1 * 512], BF16)
                xres = [c.sbuf(f"xrf{i}", [128, 512], F32) for i in range(2)]
                outt = [c.sbuf(f"otf{i}", [128, 512], F32) for i in range(2)]
                h3 = h_sb.ap()[:, :].rearrange("p (a t) -> p a t", a=FC1)
                for dsl in range(4):
                    for fcc in range(FC1):
                        c.dma("qpool", q3sb.ap()[:, ts(fcc, 512)],
                              self.q3_scr.ap()[ts(fcc, 128), ts(dsl, 512)],
                              reads=(self.q3_scr,), writes=(q3sb,))
                    for tt8 in range(8):
                        gt = th * 8 + tt8
                        xr = xres[tt8 % 2]
                        c.dma("qsp", xr.ap()[:, :],
                              self.x1a_scr.ap()[ts(gt, 128), ts(dsl, 512)],
                              reads=(self.x1a_scr,), writes=(xr,))
                        pa = self.pb[4 + tt8 % 2]
                        self.mm_acc(pa, pa.ap()[:, :], [
                            (h3[:, fcc, ts(tt8, 128)], q3sb.ap()[:, ts(fcc, 512)],
                             (h_sb, q3sb))
                            for fcc in range(FC1)])
                        ot = outt[tt8 % 2]
                        self.dve(lambda e, pa=pa, xr=xr, ot=ot: e.tensor_add(
                            ot.ap()[:, :], pa.ap()[:, :], xr.ap()[:, :]),
                            reads=(pa, xr), writes=(ot,))
                        c.dma("qact", dest_scr.ap()[ts(gt, 128), ts(dsl, 512)],
                              ot.ap()[:, :], reads=(ot,), writes=(dest_scr,))

    return emit_gamma_setup, emit_gamma_step, emit_gamma_tail, _quant, emit_ffn


(MK.emit_gamma_setup, MK.emit_gamma_step, MK.emit_gamma_tail, MK._quant,
 MK.emit_ffn) = _ffn_methods()



def build_l0_program(dbg=()):
    nc = bass.Bass("TRN2", target_bir_lowering=False)
    mk = MK(nc, dbg=dbg)
    mk.declare_io()
    mk.ctx.alloc()
    mk.alloc_persist()
    mk.load_consts()
    with mk.ctx.scope():
        mk.alloc_l0_wide()
        mk.emit_l0_attn()
    mk.emit_ffn(mk.out["y_out"])
    mk.ctx.barrier_all()
    mk.ctx.finalize()
    return nc, mk


def host_l0_inputs(IN):
    f = np.float32
    import ml_dtypes
    bf = ml_dtypes.bfloat16
    r = np.arange(128)
    low0 = np.where(r[:, None] >= r[None, :], 0.0, NEG).astype(f)
    up0 = np.where(r[None, :] >= r[:, None], 0.0, NEG).astype(f)
    upS = np.where(r[None, :] > r[:, None], 0.0, NEG).astype(f)
    lowS = np.where(r[:, None] > r[None, :], 0.0, NEG).astype(f)
    shared = {
        "ident_bf": np.eye(128).astype(bf),
        "ident_f32": np.eye(128).astype(f),
        "masks3": np.concatenate([low0, up0, upS, lowS], axis=1),
        "ones_row": np.ones((1, 2048), f),
        "w_in": tile_fm(np.pad(np.asarray(IN["L0_attn_W_in"], f), ((0, 0), (0, 64))), 128),
        "conv_w0": np.ascontiguousarray(IN["L0_attn_conv_w"]).astype(f),
        "conv_b0": np.ascontiguousarray(IN["L0_attn_conv_b"]).reshape(-1, 1).astype(f),
        "dt_bias0": IN["L0_attn_dt_bias"].reshape(-1, 1).astype(f),
        "a0": IN["L0_attn_A"].reshape(-1, 1).astype(f),
        "d0_row": IN["L0_attn_D"].reshape(1, -1).astype(f),
        "nw0": np.ascontiguousarray(IN["L0_attn_norm_weight"].reshape(32, 128).T).astype(f),
        "w_out": np.ascontiguousarray(IN["L0_attn_W_out"]).astype(f),
        "ffn_w1t": tile_fm(np.asarray(IN["L0_ffn_w1"], f).T, 128),
        "ffn_w2t": tile_fm(np.asarray(IN["L0_ffn_w2"], f).T, 128),
        "ffn_w3t": np.ascontiguousarray(IN["L0_ffn_w3"].T).astype(f),
    }
    return shared


def core_l0_inputs(IN, shared, core):
    b, h = core // 2, core % 2
    x = IN["L0_attn_hidden_states"]
    d = dict(shared)
    d["x_own"] = np.ascontiguousarray(x[b, h * T:(h + 1) * T]).astype(np.float32)
    d["x_prev"] = np.ascontiguousarray(x[b, 0:T]).astype(np.float32)
    d["maskcol"] = np.full((128, 1), 1.0 if h == 1 else 0.0, np.float32)
    return d


def _l1_methods():
    def declare_l1_io(self):
        self.din("wqkv", [QKVD, D])
        self.din("wz", [4096, D])
        self.din("wab", [D, 64])
        self.din("conv_w1", [QKVD, 4])
        self.din("alog", [HV, 1])
        self.din("dtb1", [HV, 1])
        self.din("nw1", [128, 32])
        self.din("wout1", [4096, D])
        self.din("gwT", [D, 16])
        self.din("ebias", [1, 16])
        self.din("gate_wT", [NE * 2 * 128, DC * 512])
        self.din("up_wT", [NE * 2 * 128, DC * 512])
        self.din("down_wT", [NE * 4 * 128, 8 * 512])
        self.din("shgT", [4 * 128, DC * 512])
        self.din("shuT", [4 * 128, DC * 512])
        self.din("shdT", [4 * 128, DC * 512])
        self.qkv_scr = self.dscr("qkv_scr", [QKVD, T], BF16)
        self.z_scr = self.dscr("z_scr", [4096, T], BF16)
        self.yn_scr = self.dscr("yn_scr", [128, NCH * 32 * 128], BF16)
        self.lr_scr = self.dscr("lr_scr", [3 * HV, T], F32)
        self.x2a_scr = self.dscr("x2a_scr", [T, D], F32)
        self.xn1_scr = self.dscr("xn1_scr", [128, DC * T], BF16)

    def alloc_l1_persist(self):
        c = self.ctx
        self.cA1 = c.sbuf("cA1", [32, T], F32)
        self.beta_hm = c.sbuf("beta_hm", [32, T], F32)
        self.La_hm = c.sbuf("La_hm", [32, T], F32)
        self.Ra_hm = c.sbuf("Ra_hm", [32, T], F32)
        self.Rq_hm = c.sbuf("Rq_hm", [32, T], F32)
        self.g_hm = c.sbuf("g_hm", [32, T], F32)
        self.lnb_hm = c.sbuf("lnb_hm", [32, T], F32)
        self.lnik_hm = c.sbuf("lnik_hm", [16, T], F32)
        self.lniq_hm = c.sbuf("lniq_hm", [16, T], F32)
        self.zeros32 = c.sbuf("zeros32", [32, 128], F32)

    def emit_l1_front(self, x1_buf):
        c = self.ctx
        self.dve(lambda e: e.memset(self.zeros32.ap()[:, :], 0.0), writes=(self.zeros32,))
        with c.scope():
            sc = {}
            xn_fm = c.sbuf("xn1_fm", [128, DC * T], BF16)
            with c.scope():
                xld = c.sbuf("xld1", [128, D], F32)
                xnstage = c.sbuf("xnstage1", [128, D], BF16)
                dump = c.sbuf("dumpL1", [128, D], BF16)
                self.emit_ln(x1_buf, xn_fm, xld, xnstage, dump)
            c.dma("qact", self.xn1_scr.ap(), xn_fm.ap()[:, :],
                  reads=(xn_fm,), writes=(self.xn1_scr,))
            xn3 = xn_fm.ap()[:, :].rearrange("p (dc t) -> p dc t", dc=DC)
            self._l1_qkv_conv(xn3, xn_fm)
            self._l1_zab(xn3, xn_fm)

    def _l1_qkv_conv(self, xn3, xn_fm):
        c = self.ctx
        with c.scope():
            wcol = [c.sbuf(f"wcol1{i}", [128, D], BF16) for i in range(2)]
            conv_in = c.sbuf("conv_in1", [128, T + 3], F32)
            acc = c.sbuf("acc1", [128, T], F32)
            xbcs = c.sbuf("xbcs1", [128, T], BF16)
            cw = c.sbuf("cw1", [128, 4], F32)
            for cc in range(64):
                wc = wcol[cc % 2]
                c.dma("qpool", wc.ap()[:, :], self.inp["wqkv"].ap()[ts(cc, 128), :],
                      reads=(self.inp["wqkv"],), writes=(wc,))
                self.mm_kouter(
                    [(self.pb[tg], self.pb[tg].ap()[:, :]) for tg in range(4)],
                    [(wc.ap()[:, ts(dc, 128)],
                      [xn3[:, dc, ts(tg, 512)] for tg in range(4)], (wc, xn_fm))
                     for dc in range(DC)])
                for tg in range(4):
                    pa = self.pb[tg]
                    self.act(lambda e, tg=tg, pa=pa: e.activation(
                        conv_in.ap()[:, 3 + tg * 512:3 + (tg + 1) * 512],
                        pa.ap()[:, :], AF.Copy),
                        reads=(pa,), writes=(conv_in,))
                c.dma("qsp", cw.ap()[:, :], self.inp["conv_w1"].ap()[ts(cc, 128), :],
                      reads=(self.inp["conv_w1"],), writes=(cw,))
                self.dve(lambda e: e.memset(conv_in.ap()[:, 0:3], 0.0), writes=(conv_in,))
                self.dve(lambda e: e.tensor_scalar(acc.ap()[:, 0:T], conv_in.ap()[:, 0:T],
                                                   cw.ap()[:, 0:1], None, op0=OP.mult),
                         reads=(conv_in, cw), writes=(acc,))
                for k in range(1, 4):
                    self.dve(lambda e, k=k: e.scalar_tensor_tensor(
                        acc.ap()[:, 0:T], conv_in.ap()[:, k:k + T], cw.ap()[:, k:k + 1],
                        acc.ap()[:, 0:T], op0=OP.mult, op1=OP.add),
                        reads=(conv_in, cw, acc), writes=(acc,))
                self.act(lambda e: e.activation(xbcs.ap()[:, :], acc.ap()[:, 0:T], AF.Silu),
                         reads=(acc,), writes=(xbcs,))
                c.dma("qact", self.qkv_scr.ap()[ts(cc, 128), :], xbcs.ap()[:, :],
                      reads=(xbcs,), writes=(self.qkv_scr,))

    def _l1_zab(self, xn3, xn_fm):
        c = self.ctx
        with c.scope():
            wcol = [c.sbuf(f"wzc{i}", [128, D], BF16) for i in range(2)]
            gstage = c.sbuf("gstage1", [128, 2048], BF16)
            for cc in range(32):
                wc = wcol[cc % 2]
                c.dma("qpool", wc.ap()[:, :], self.inp["wz"].ap()[ts(cc, 128), :],
                      reads=(self.inp["wz"],), writes=(wc,))
                self.mm_kouter(
                    [(self.pb[tg], self.pb[tg].ap()[:, :]) for tg in range(4)],
                    [(wc.ap()[:, ts(dc, 128)],
                      [xn3[:, dc, ts(tg, 512)] for tg in range(4)], (wc, xn_fm))
                     for dc in range(DC)])
                for tg in range(4):
                    pa = self.pb[tg]
                    self.act(lambda e, pa=pa, tg=tg: e.activation(
                        gstage.ap()[:, ts(tg, 512)], pa.ap()[:, :], AF.Copy),
                        reads=(pa,), writes=(gstage,))
                c.dma("qact", self.z_scr.ap()[ts(cc, 128), :],
                      gstage.ap()[:, :], reads=(gstage,), writes=(self.z_scr,))
            wabs = c.sbuf("wabs", [128, DC * 64], BF16)
            wab3 = wabs.ap()[:, :].rearrange("p (dc g) -> p dc g", dc=DC)
            ab_hm = c.sbuf("ab_hm", [64, T], F32)
            abst = c.sbuf("abst", [128, 64], F32)
            for q in range(4):
                src = self.inp["wab"].ap()[ts(q, 512), :].rearrange(
                    "(a p) c -> p a c", p=128)
                c.dma("qpool", wab3[:, q * 4:(q + 1) * 4, :], src,
                      reads=(self.inp["wab"],), writes=(wabs,))
            for tt in range(TT):
                pa = self.pb[tt % 2]
                self.mm_acc(pa, pa.ap()[:, 0:64], [
                    (xn3[:, dc, ts(tt, 128)], wab3[:, dc, :], (xn_fm, wabs))
                    for dc in range(DC)])
                self.dve(lambda e, pa=pa: e.tensor_copy(abst.ap()[:, :], pa.ap()[:, 0:64]),
                         reads=(pa,), writes=(abst,))
                pt = self.pb[2 + tt % 2]
                self.transpose(pt, pt.ap()[0:64, 0:128], abst.ap()[:, :],
                               self.identf.ap()[:, :], reads=(abst, self.identf))
                self.dve(lambda e, pt=pt, tt=tt: e.tensor_copy(
                    ab_hm.ap()[:, ts(tt, 128)], pt.ap()[0:64, 0:128]),
                    reads=(pt,), writes=(ab_hm,))
            al = c.sbuf("al", [32, 2], F32)
            c.dma("qsp", al.ap()[:, 0:1], self.inp["alog"].ap(),
                  reads=(self.inp["alog"],), writes=(al,))
            c.dma("qsp", al.ap()[1:2, 1:2] if False else al.ap()[:, 1:2],
                  self.inp["dtb1"].ap(), reads=(self.inp["dtb1"],), writes=(al,))
            self.act(lambda e: e.activation(al.ap()[:, 0:1], al.ap()[:, 0:1], AF.Exp),
                     reads=(al,), writes=(al,))
            w0 = c.sbuf("spw0", [32, T], F32)
            w1 = c.sbuf("spw1", [32, T], F32)
            self.dve(lambda e: e.tensor_scalar_add(w0.ap()[:, :], ab_hm.ap()[0:32, :],
                                                   al.ap()[:, 1:2]),
                     reads=(ab_hm, al), writes=(w0,))
            self.act(lambda e: e.activation(w1.ap()[:, :], w0.ap()[:, :], AF.Abs),
                     reads=(w0,), writes=(w1,))
            self.act(lambda e: e.activation(w1.ap()[:, :], w1.ap()[:, :], AF.Exp, scale=-1.0),
                     reads=(w1,), writes=(w1,))
            self.act(lambda e: e.activation(w1.ap()[:, :], w1.ap()[:, :], AF.Ln,
                                            bias=self.cb(1.0).ap()[0:32, :]),
                     reads=(w1, self.cb(1.0)), writes=(w1,))
            self.dve(lambda e: e.tensor_scalar_max(w0.ap()[:, :], w0.ap()[:, :], 0.0),
                     reads=(w0,), writes=(w0,))
            self.dve(lambda e: e.tensor_add(w0.ap()[:, :], w0.ap()[:, :], w1.ap()[:, :]),
                     reads=(w0, w1), writes=(w0,))
            self.dve(lambda e: e.tensor_scalar(self.g_hm.ap()[:, :], w0.ap()[:, :],
                                               al.ap()[:, 0:1], -1.0,
                                               op0=OP.mult, op1=OP.mult),
                     reads=(w0, al), writes=(self.g_hm,))
            for cch in range(NCH):
                self.dve(lambda e, cch=cch: e.tensor_tensor_scan(
                    self.cA1.ap()[:, ts(cch, 128)], self.g_hm.ap()[:, ts(cch, 128)],
                    self.zeros32.ap()[:, :], 0.0, op0=OP.add, op1=OP.add),
                    reads=(self.g_hm, self.zeros32), writes=(self.cA1,))
            self.dve(lambda e: e.tensor_copy(w0.ap()[:, :], ab_hm.ap()[32:64, :]),
                     reads=(ab_hm,), writes=(w0,))
            self.act(lambda e: e.activation(self.beta_hm.ap()[:, :], w0.ap()[:, :],
                                            AF.Sigmoid),
                     reads=(w0,), writes=(self.beta_hm,))
            self.act(lambda e: e.activation(w1.ap()[:, :], w0.ap()[:, :], AF.Abs),
                     reads=(w0,), writes=(w1,))
            self.act(lambda e: e.activation(w1.ap()[:, :], w1.ap()[:, :], AF.Exp, scale=-1.0),
                     reads=(w1,), writes=(w1,))
            self.act(lambda e: e.activation(w1.ap()[:, :], w1.ap()[:, :], AF.Ln,
                                            bias=self.cb(1.0).ap()[0:32, :]),
                     reads=(w1, self.cb(1.0)), writes=(w1,))
            self.dve(lambda e: e.tensor_scalar(w0.ap()[:, :], w0.ap()[:, :],
                                               -1.0, 0.0, op0=OP.mult, op1=OP.max),
                     reads=(w0,), writes=(w0,))
            self.dve(lambda e: e.scalar_tensor_tensor(self.lnb_hm.ap()[:, :],
                                                      w0.ap()[:, :], -1.0, w1.ap()[:, :],
                                                      op0=OP.bypass, op1=OP.add),
                     reads=(w0, w1), writes=(self.lnb_hm,))
            self.dve(lambda e: e.tensor_scalar_mul(self.lnb_hm.ap()[:, :],
                                                   self.lnb_hm.ap()[:, :], -1.0),
                     reads=(self.lnb_hm,), writes=(self.lnb_hm,))
    return declare_l1_io, alloc_l1_persist, emit_l1_front, _l1_qkv_conv, _l1_zab


(MK.declare_l1_io, MK.alloc_l1_persist, MK.emit_l1_front, MK._l1_qkv_conv,
 MK._l1_zab) = _l1_methods()


def _l1_chunk_methods():
    def emit_l1_prepass(self):
        c = self.ctx
        with c.scope():
            qk_tm = [c.sbuf(f"qkt{i}", [128, 2048], BF16) for i in range(2)]
            sqf = c.sbuf("sqf", [128, 2048], F32)
            ss = c.sbuf("ssn", [128, 16], F32)
            for cch in range(NCH):
                for qi, dst in ((0, self.lniq_hm), (1, self.lnik_hm)):
                    qt = qk_tm[qi]
                    c.dma("qsp", qt.ap()[:, :],
                          self.qkv_scr.ap()[qi * 2048:(qi + 1) * 2048, ts(cch, 128)],
                          reads=(self.qkv_scr,), writes=(qt,), transpose=True)
                    self.act(lambda e, qt=qt: e.activation(sqf.ap()[:, :], qt.ap()[:, :],
                                                           AF.Square),
                             reads=(qt,), writes=(sqf,))
                    self.dve(lambda e: e.tensor_reduce(
                        ss.ap()[:, :].rearrange("p (g o) -> p g o", g=16),
                        sqf.ap()[:, :].rearrange("p (g n) -> p g n", g=16),
                        axis=AX.X, op=OP.add),
                        reads=(sqf,), writes=(ss,))
                    self.act(lambda e: e.activation(ss.ap()[:, :], ss.ap()[:, :], AF.Ln,
                                                    bias=self.cb(1e-6).ap()[:, :]),
                             reads=(ss, self.cb(1e-6)), writes=(ss,))
                    if qi == 0:
                        self.dve(lambda e: e.tensor_scalar(
                            ss.ap()[:, :], ss.ap()[:, :], float(np.log(128.0)), -0.5,
                            op0=OP.add, op1=OP.mult),
                            reads=(ss,), writes=(ss,))
                    else:
                        self.dve(lambda e: e.tensor_scalar_mul(ss.ap()[:, :],
                                                               ss.ap()[:, :], -0.5),
                                 reads=(ss,), writes=(ss,))
                    pt = self.pb[4]
                    self.transpose(pt, pt.ap()[0:16, 0:128], ss.ap()[:, :],
                                   self.identf.ap()[:, :], reads=(ss, self.identf))
                    self.dve(lambda e, pt=pt, dst=dst, cch=cch: e.tensor_copy(
                        dst.ap()[:, ts(cch, 128)], pt.ap()[0:16, 0:128]),
                        reads=(pt,), writes=(dst,))

    def emit_l1_rows(self):
        c = self.ctx
        ln_scr = self.dscr(f"ln_scr_{self.ctx.cnt['dve']}", [32, T], F32)
        c.dma("qact", ln_scr.ap()[0:16, :], self.lnik_hm.ap()[:, :],
              reads=(self.lnik_hm,), writes=(ln_scr,))
        c.dma("qact", ln_scr.ap()[16:32, :], self.lniq_hm.ap()[:, :],
              reads=(self.lniq_hm,), writes=(ln_scr,))
        with c.scope():
            ikrep = c.sbuf("ikrep", [32, T], F32)
            iqrep = c.sbuf("iqrep", [32, T], F32)
            t = ln_scr.ap().tensor
            c.dma("qsp", ikrep.ap()[:, :], bass.AP(t, 0, [[T, 16], [0, 2], [1, T]]),
                  reads=(ln_scr,), writes=(ikrep,))
            c.dma("qsp", iqrep.ap()[:, :], bass.AP(t, 16 * T, [[T, 16], [0, 2], [1, T]]),
                  reads=(ln_scr,), writes=(iqrep,))
            self.dve(lambda e: e.tensor_sub(self.La_hm.ap()[:, :], ikrep.ap()[:, :],
                                            self.cA1.ap()[:, :]),
                     reads=(ikrep, self.cA1), writes=(self.La_hm,))
            self.dve(lambda e: e.tensor_add(self.Ra_hm.ap()[:, :], ikrep.ap()[:, :],
                                            self.cA1.ap()[:, :]),
                     reads=(ikrep, self.cA1), writes=(self.Ra_hm,))
            self.dve(lambda e: e.tensor_add(self.Ra_hm.ap()[:, :], self.Ra_hm.ap()[:, :],
                                            self.lnb_hm.ap()[:, :]),
                     reads=(self.Ra_hm, self.lnb_hm), writes=(self.Ra_hm,))
            self.dve(lambda e: e.tensor_add(self.Rq_hm.ap()[:, :], iqrep.ap()[:, :],
                                            self.cA1.ap()[:, :]),
                     reads=(iqrep, self.cA1), writes=(self.Rq_hm,))
        for i, b in enumerate((self.La_hm, self.Ra_hm, self.Rq_hm)):
            c.dma("qact", self.lr_scr.ap()[i * HV:(i + 1) * HV, :], b.ap()[:, :],
                  reads=(b,), writes=(self.lr_scr,))

    def lr_row(self, which, cch, part):
        t = self.lr_scr.ap().tensor
        base = {"La": 0, "Ra": HV, "Rq": 2 * HV}[which] * T
        return bass.AP(t, base + part * 8 * T + cch * 128, [[1, 1], [T, 8], [1, 128]])

    return emit_l1_prepass, emit_l1_rows, lr_row


MK.emit_l1_prepass, MK.emit_l1_rows, MK.lr_row = _l1_chunk_methods()


def _l1_loop_methods():
    def _l1_smalls(self, cch, sb):
        hmw = sb["hmw"]
        pt = self.pb[4]
        last = self.cA1.ap()[:, cch * 128 + 127:cch * 128 + 128]
        specs = [(self.La_hm, 0, "wk_tm", last), (self.Ra_hm, 128, "bl_tm", None),
                 (self.Rq_hm, 256, "liq_tm", None), (self.beta_hm, 384, "beta_tm", None)]
        for (srcb, off, dst, bias) in specs:
            if dst == "beta_tm":
                self.dve(lambda e, srcb=srcb, off=off: e.tensor_copy(
                    hmw.ap()[:, off:off + 128], srcb.ap()[:, ts(cch, 128)]),
                    reads=(srcb,), writes=(hmw,))
            else:
                bb = 0.0 if bias is None else bias
                self.act(lambda e, srcb=srcb, off=off, bb=bb: e.activation(
                    hmw.ap()[:, off:off + 128], srcb.ap()[:, ts(cch, 128)], AF.Exp,
                    bias=bb),
                    reads=(srcb, self.cA1), writes=(hmw,))
            self.transpose(pt, pt.ap()[:, off // 4:off // 4 + 32],
                           hmw.ap()[:, off:off + 128], self.identf.ap()[0:32, 0:32],
                           reads=(hmw, self.identf))
            self.dve(lambda e, dst=dst, off=off, pt=pt: e.tensor_copy(
                sb[dst].ap()[:, :], pt.ap()[:, off // 4:off // 4 + 32]),
                reads=(pt,), writes=(sb[dst],))
        self.act(lambda e: e.activation(hmw.ap()[:, 384:385] if False else hmw.ap()[:, 440:441],
                                        last, AF.Exp),
                 reads=(self.cA1,), writes=(hmw,))
        self.transpose(pt, pt.ap()[0:1, 160:192], hmw.ap()[:, 440:441],
                       self.identf.ap()[0:32, 0:32], reads=(hmw, self.identf))
        self.dve(lambda e, pt=pt: e.tensor_copy(sb["lamrow"].ap()[:, :],
                                                pt.ap()[0:1, 160:192]),
                 reads=(pt,), writes=(sb["lamrow"],))
        self.kone(sb["lamrow"].ap()[0:1, :], pt, pt.ap()[:, 192:224], (sb["lamrow"],))
        self.dve(lambda e, pt=pt: e.tensor_copy(sb["lamL"].ap()[:, :], pt.ap()[:, 192:224]),
                 reads=(pt,), writes=(sb["lamL"],))

    def _l1_load_rows(self, cch, part, sb):
        c = self.ctx
        p = part % 2
        for name, which, row in (("Lp", "La", 0), ("Rap", "Ra", 1), ("Rqp", "Rq", 1)):
            tile = sb[name][p]
            c.dma("qsp", tile.ap()[row:row + 1, :], self.lr_row(which, cch, part),
                  reads=(self.lr_scr,), writes=(tile,))

    def emit_l1_chunk(self, cch, sb):
        c = self.ctx
        qf, kf, k_tm, v_tm = sb["qf"], sb["kf"], sb["k_tm"], sb["v_tm"]
        for dst, r0, r1, tr in ((qf, 0, 2048, False), (kf, 2048, 4096, False)):
            for q in range(2):
                src = self.qkv_scr.ap()[r0 + q * 1024:r0 + (q + 1) * 1024,
                                        ts(cch, 128)].rearrange("(a p) t -> p a t", p=128)
                c.dma("qsp", dst.ap()[:, :].rearrange("p (a t) -> p a t", a=16)
                      [:, q * 8:(q + 1) * 8, :], src,
                      reads=(self.qkv_scr,), writes=(dst,))
        c.dma("qact", k_tm.ap()[:, :], self.qkv_scr.ap()[2048:4096, ts(cch, 128)],
              reads=(self.qkv_scr,), writes=(k_tm,), transpose=True)
        c.dma("qact", v_tm.ap()[:, :], self.qkv_scr.ap()[4096:8192, ts(cch, 128)],
              reads=(self.qkv_scr,), writes=(v_tm,), transpose=True)
        self._l1_smalls(cch, sb)
        self.dve(lambda e: e.tensor_copy(self.S_bf.ap()[:, :], self.S_f32.ap()[:, :]),
                 reads=(self.S_f32,), writes=(self.S_bf,))
        self._l1_load_rows(cch, 0, sb)
        self._l1_load_rows(cch, 1, sb)
        import itertools
        for pr in range(4):
            if pr in (1, 2):
                self._l1_load_rows(cch, pr + 1, sb)
            it0 = self._l1_group_stages(cch, 2 * pr, sb, sb["PS"][0])
            it1 = self._l1_group_stages(cch, 2 * pr + 1, sb, sb["PS"][1])
            for _ in itertools.zip_longest(it0, it1):
                pass
        self._l1_ychunk(cch, sb)

    def _l1_group_stages(self, cch, g, sb, P):
        h0 = 4 * g
        p = (g // 2) % 2
        Lp, Rap, Rqp = sb["Lp"][p], sb["Rap"][p], sb["Rqp"][p]
        qf, kf, k_tm, v_tm = sb["qf"], sb["kf"], sb["k_tm"], sb["v_tm"]
        ph0 = (h0 % 8)
        bA, bB, bC = P["banks"]
        for j, kk in enumerate((g * 2, g * 2 + 1)):
            self.mm_acc(bA, bA.ap()[:, ts(j, 128)], [
                (kf.ap()[:, ts(kk, 128)], kf.ap()[:, ts(kk, 128)], (kf,))])
            self.mm_acc(bA, bA.ap()[:, 256 + j * 128:256 + (j + 1) * 128], [
                (kf.ap()[:, ts(kk, 128)], qf.ap()[:, ts(kk, 128)], (kf, qf))])
        self.act(lambda e: e.activation(P["KKq"].ap()[:, :], bA.ap()[:, :], AF.Copy),
                 reads=(bA,), writes=(P["KKq"],))
        yield

        def kk_pat(off):
            a = P["KKq"].ap()[:, off:off + 256]
            return bass.AP(a.tensor, a.offset, [list(a.ap[0]), [128, 2], [0, 2], [1, 128]])

        def build(dst, lhs, rhs, mask, pat, bank):
            for j in range(4):
                hh = ph0 + j
                self.mm_acc(bank, bank.ap()[:, ts(j, 128)], [
                    (lhs.ap()[0:2, ts(hh, 128)], rhs.ap()[0:2, ts(hh, 128)], (lhs, rhs))])
            self.dve(lambda e: e.tensor_add(P["dA"].ap()[:, :], bank.ap()[:, :],
                                            self.mask(mask, rep=4)),
                     reads=(bank, self.consts), writes=(P["dA"],))
            self.act(lambda e: e.activation(P["dA"].ap()[:, :], P["dA"].ap()[:, :],
                                            AF.Exp),
                     reads=(P["dA"],), writes=(P["dA"],))
            self.dve(lambda e, pat=pat: e.tensor_mul(
                dst.ap()[:, :].rearrange("p (a b c) -> p a b c", a=2, b=2),
                P["dA"].ap()[:, :].rearrange("p (a b c) -> p a b c", a=2, b=2), pat),
                reads=(P["dA"], P["KKq"]), writes=(dst,))
        build(P["At4"], Lp, Rap, "upS", kk_pat(0), bB)
        yield
        build(P["A4"], Rap, Lp, "lowS", kk_pat(0), bC)
        yield
        build(P["Mqt4"], Lp, Rqp, "up0", kk_pat(256), bB)
        yield
        for j in range(4):
            kk = g * 2 + j // 2
            self.mm_acc(bA, bA.ap()[:, ts(j, 128)], [
                (kf.ap()[:, ts(kk, 128)], self.S_bf.ap()[:, ts(h0 + j, 128)],
                 (kf, self.S_bf))])
        self.dve(lambda e: e.tensor_mul(P["tf1"].ap()[:, :], bA.ap()[:, :],
                                        ap_append0(sb["bl_tm"].ap()[:, h0:h0 + 4], 128)),
                 reads=(bA, sb["bl_tm"]), writes=(P["tf1"],))
        self.dve(lambda e: e.tensor_mul(P["tf2"].ap()[:, :],
                                        v_tm.ap()[:, h0 * 128:(h0 + 4) * 128],
                                        ap_append0(sb["beta_tm"].ap()[:, h0:h0 + 4], 128)),
                 reads=(v_tm, sb["beta_tm"]), writes=(P["tf2"],))
        V = P["V4"][0]
        self.dve(lambda e, V=V: e.tensor_sub(V.ap()[:, :], P["tf2"].ap()[:, :],
                                             P["tf1"].ap()[:, :]),
                 reads=(P["tf1"], P["tf2"]), writes=(V,))
        yield
        Pc, Ptc = P["A4"], P["At4"]
        vcur = 0
        for j in range(4):
            self.mm_acc(bC, bC.ap()[:, ts(j, 128)], [
                (Ptc.ap()[:, ts(j, 128)], V.ap()[:, ts(j, 128)], (Ptc, V))])
        Vn = P["V4"][1]
        self.dve(lambda e, V=V, Vn=Vn: e.tensor_sub(Vn.ap()[:, :], V.ap()[:, :],
                                                    bC.ap()[:, :]),
                 reads=(V, bC), writes=(Vn,))
        vcur = 1
        yield
        for k in range(1, 7):
            Pn = P["P"][k % 2]
            Ptn = P["Pt"][k % 2]
            if k < 6:
                for j in range(4):
                    self.mm_acc(bB, bB.ap()[:, ts(j, 128)], [
                        (Ptc.ap()[:, ts(j, 128)], Pc.ap()[:, ts(j, 128)], (Ptc, Pc))])
                self.act(lambda e, Pn=Pn: e.activation(Pn.ap()[:, :], bB.ap()[:, :],
                                                       AF.Copy),
                         reads=(bB,), writes=(Pn,))
            for j in range(4):
                self.mm_acc(bC, bC.ap()[:, ts(j, 128)], [
                    (Pc.ap()[:, ts(j, 128)], Ptc.ap()[:, ts(j, 128)], (Pc, Ptc))])
            self.act(lambda e, Ptn=Ptn: e.activation(Ptn.ap()[:, :], bC.ap()[:, :],
                                                     AF.Copy),
                     reads=(bC,), writes=(Ptn,))
            yield
            Pc, Ptc = Pn, Ptn
            V, Vn = P["V4"][vcur], P["V4"][1 - vcur]
            for j in range(4):
                self.mm_acc(bA, bA.ap()[:, ts(j, 128)], [
                    (Ptc.ap()[:, ts(j, 128)], V.ap()[:, ts(j, 128)], (Ptc, V))])
            self.dve(lambda e, V=V, Vn=Vn: e.tensor_add(Vn.ap()[:, :], V.ap()[:, :],
                                                        bA.ap()[:, :]),
                     reads=(V, bA), writes=(Vn,))
            vcur = 1 - vcur
            yield
        U = P["V4"][vcur]
        for j in range(4):
            kk = g * 2 + j // 2
            self.mm_acc(bB, bB.ap()[:, ts(j, 128)], [
                (qf.ap()[:, ts(kk, 128)], self.S_bf.ap()[:, ts(h0 + j, 128)],
                 (qf, self.S_bf))])
        self.dve(lambda e: e.tensor_mul(P["tf1"].ap()[:, :], bB.ap()[:, :],
                                        ap_append0(sb["liq_tm"].ap()[:, h0:h0 + 4], 128)),
                 reads=(bB, sb["liq_tm"]), writes=(P["tf1"],))
        for j in range(4):
            self.mm_acc(bC, bC.ap()[:, ts(j, 128)], [
                (P["Mqt4"].ap()[:, ts(j, 128)], U.ap()[:, ts(j, 128)],
                 (P["Mqt4"], U))])
        self.dve(lambda e: e.tensor_add(sb["o_tm"].ap()[:, h0 * 128:(h0 + 4) * 128],
                                        P["tf1"].ap()[:, :], bC.ap()[:, :]),
                 reads=(P["tf1"], bC), writes=(sb["o_tm"],))
        yield
        ka = k_tm.ap()[:, :]
        kpat = bass.AP(ka.tensor, ka.offset + (g * 2) * 128,
                       [list(ka.ap[0]), [128, 2], [0, 2], [1, 128]])
        self.dve(lambda e, kpat=kpat: e.tensor_mul(
            P["kw4"].ap()[:, :].rearrange("p (a b c) -> p a b c", a=2, b=2), kpat,
            ap_append0(sb["wk_tm"].ap()[:, h0:h0 + 4], 128).rearrange(
                "p (a b) c -> p a b c", a=2)),
            reads=(k_tm, sb["wk_tm"]), writes=(P["kw4"],))
        for j in range(4):
            self.mm_acc(bA, bA.ap()[:, ts(j, 128)], [
                (P["kw4"].ap()[:, ts(j, 128)], U.ap()[:, ts(j, 128)],
                 (P["kw4"], U))])
        ssl = self.S_f32.ap()[:, h0 * 128:(h0 + 4) * 128]
        self.dve(lambda e: e.tensor_mul(P["tf1"].ap()[:, :], ssl,
                                        ap_append0(sb["lamL"].ap()[:, h0:h0 + 4], 128)),
                 reads=(self.S_f32, sb["lamL"]), writes=(P["tf1"],))
        self.dve(lambda e: e.tensor_add(ssl, P["tf1"].ap()[:, :], bA.ap()[:, :]),
                 reads=(P["tf1"], bA), writes=(self.S_f32,))

    def _l1_ychunk(self, cch, sb):
        c = self.ctx
        zt = sb["zt"]
        c.dma("qsp", zt.ap()[:, :], self.z_scr.ap()[:, ts(cch, 128)],
              reads=(self.z_scr,), writes=(zt,), transpose=True)
        self.act(lambda e: e.activation(zt.ap()[:, :], zt.ap()[:, :], AF.Silu),
                 reads=(zt,), writes=(zt,))
        self.dve(lambda e: e.tensor_mul(sb["o_tm"].ap()[:, :], sb["o_tm"].ap()[:, :],
                                        zt.ap()[:, :]),
                 reads=(sb["o_tm"], zt), writes=(sb["o_tm"],))
        st = self.stats
        self.act(lambda e: e.activation(zt.ap()[:, :], sb["o_tm"].ap()[:, :],
                                        AF.Square),
                 reads=(sb["o_tm"],), writes=(zt,))
        self.dve(lambda e: e.tensor_reduce(
            sb["ssy"].ap()[:, :].rearrange("p (g o) -> p g o", g=32),
            zt.ap()[:, :].rearrange("p (g n) -> p g n", g=32),
            axis=AX.X, op=OP.add),
            reads=(zt,), writes=(sb["ssy"],))
        self.act(lambda e: e.activation(sb["ssy"].ap()[:, :], sb["ssy"].ap()[:, :],
                                        AF.Ln, scale=1.0 / 128,
                                        bias=self.cb(1e-6).ap()[:, :]),
                 reads=(sb["ssy"], self.cb(1e-6)), writes=(sb["ssy"],))
        self.act(lambda e: e.activation(sb["ssy"].ap()[:, :], sb["ssy"].ap()[:, :],
                                        AF.Exp, scale=-0.5),
                 reads=(sb["ssy"],), writes=(sb["ssy"],))
        self.dve(lambda e: e.tensor_mul(
            zt.ap()[:, :].rearrange("p (g n) -> p g n", g=32),
            sb["o_tm"].ap()[:, :].rearrange("p (g n) -> p g n", g=32),
            ap_append0(sb["ssy"].ap()[:, :], 128)),
            reads=(sb["o_tm"], sb["ssy"]), writes=(zt,))
        yf3 = sb["yn_fm"].ap()[:, :].rearrange("p (ic t) -> p ic t", ic=32)
        for gg in range(8):
            ptb = self.pt[gg % 2]
            for j in range(4):
                ic = gg * 4 + j
                self.transpose(ptb, ptb.ap()[:, ts(j, 128)], zt.ap()[:, ts(ic, 128)],
                               self.identb.ap()[:, :], reads=(zt, self.identb))
            if gg % 2 == 0:
                self.dve(lambda e, gg=gg, ptb=ptb: e.tensor_copy(
                    yf3[:, gg * 4:(gg + 1) * 4, :],
                    ptb.ap()[:, 0:512].rearrange("p (a b) -> p a b", a=4)),
                    reads=(ptb,), writes=(sb["yn_fm"],))
            else:
                self.act(lambda e, gg=gg, ptb=ptb: e.activation(
                    yf3[:, gg * 4:(gg + 1) * 4, :],
                    ptb.ap()[:, 0:512].rearrange("p (a b) -> p a b", a=4), AF.Copy),
                    reads=(ptb,), writes=(sb["yn_fm"],))
        c.dma("qact", self.yn_scr.ap()[:, ts(cch, 4096)], sb["yn_fm"].ap()[:, :],
              reads=(sb["yn_fm"],), writes=(self.yn_scr,))

    def emit_l1_attn_loop(self):
        c = self.ctx
        with c.scope():
            sb = {}
            self.S_f32 = c.sbuf("S_f32", [128, HV * 128], F32)
            self.S_bf = c.sbuf("S_bf", [128, HV * 128], BF16)
            sb["qf"] = c.sbuf("qf", [128, 2048], BF16)
            sb["kf"] = c.sbuf("kf", [128, 2048], BF16)
            sb["k_tm"] = c.sbuf("k_tm1", [128, 2048], BF16)
            sb["v_tm"] = c.sbuf("v_tm1", [128, 4096], BF16)
            sb["hmw"] = c.sbuf("hmw", [32, 512], F32)
            for nm in ("wk_tm", "bl_tm", "liq_tm", "beta_tm", "lamL"):
                sb[nm] = c.sbuf(nm, [128, 32], F32)
            sb["lamrow"] = c.sbuf("lamrow", [1, 32], F32)
            sb["Lp"] = [c.sbuf(f"Lp{i}", [2, 1024], F32) for i in range(2)]
            sb["Rap"] = [c.sbuf(f"Rap{i}", [2, 1024], F32) for i in range(2)]
            sb["Rqp"] = [c.sbuf(f"Rqp{i}", [2, 1024], F32) for i in range(2)]
            for i in range(2):
                c.dma("qsp", sb["Lp"][i].ap()[1:2, :],
                      self.inp["ones_row"].ap()[:, 0:1024],
                      reads=(self.inp["ones_row"],), writes=(sb["Lp"][i],))
                self.dve(lambda e, i=i: e.memset(sb["Rap"][i].ap()[0:1, :], 1.0),
                         writes=(sb["Rap"][i],))
                self.dve(lambda e, i=i: e.memset(sb["Rqp"][i].ap()[0:1, :], 1.0),
                         writes=(sb["Rqp"][i],))
            sb["PS"] = []
            for sfx in ("a", "b"):
                P = {}
                P["KKq"] = c.sbuf("KKq" + sfx, [128, 512], F32)
                P["dA"] = c.sbuf("dAx" + sfx, [128, 512], F32)
                P["At4"] = c.sbuf("At4" + sfx, [128, 512], BF16)
                P["A4"] = c.sbuf("A4" + sfx, [128, 512], BF16)
                P["Mqt4"] = c.sbuf("Mqt4" + sfx, [128, 512], BF16)
                P["V4"] = [c.sbuf(f"V4_{sfx}{i}", [128, 512], BF16) for i in range(2)]
                P["P"] = [c.sbuf(f"Pp{sfx}", [128, 512], BF16), P["A4"]]
                P["Pt"] = [c.sbuf(f"Ppt{sfx}", [128, 512], BF16), P["At4"]]
                P["tf1"] = c.sbuf("tf1" + sfx, [128, 512], F32)
                sb["PS"].append(P)
            for nm, shp, dt in (("tf2", [128, 512], F32), ("kw4", [128, 512], BF16)):
                shared = c.sbuf(nm + "s", shp, dt)
                sb["PS"][0][nm] = shared
                sb["PS"][1][nm] = shared
            sb["PS"][0]["banks"] = (self.pb[0], self.pb[1], self.pb[2])
            sb["PS"][1]["banks"] = (self.pb[3], self.pb[4], self.pb[5])
            sb["o_tm"] = c.sbuf("o_tm", [128, 4096], F32)
            sb["zt"] = c.sbuf("ztl", [128, 4096], BF16)
            sb["ssy"] = c.sbuf("ssy", [128, 32], F32)
            sb["yn_fm"] = sb["v_tm"]
            self.dve(lambda e: e.memset(self.S_f32.ap()[:, :], 0.0), writes=(self.S_f32,))
            for cch in range(getattr(self, "l1_nch", NCH)):
                self.emit_l1_chunk(cch, sb)

    return (_l1_smalls, _l1_load_rows, emit_l1_chunk, _l1_group_stages, _l1_ychunk,
            emit_l1_attn_loop)


(MK._l1_smalls, MK._l1_load_rows, MK.emit_l1_chunk, MK._l1_group_stages, MK._l1_ychunk,
 MK.emit_l1_attn_loop) = _l1_loop_methods()


def build_l1_attn_program(dbg=()):
    nc = bass.Bass("TRN2", target_bir_lowering=False)
    mk = MK(nc, dbg=dbg)
    mk.ctx.dram("x1_in", [T, D], F32, kind="ExternalInput")
    mk.inp["x1_in"] = mk.ctx.bufs[-1]
    mk.din("maskcol", [128, 1])
    mk.din("ident_bf", [128, 128], BF16)
    mk.din("ident_f32", [128, 128])
    mk.din("masks3", [128, 512])
    mk.din("ones_row", [1, 2048])
    mk.declare_l1_io()
    mk.dout("y_out", [T, D])
    mk.ctx.alloc()
    mk.alloc_persist()
    mk.load_consts()
    with mk.ctx.scope():
        mk.alloc_l1_persist()
        mk.emit_l1_front(mk.inp["x1_in"])
        mk.emit_l1_prepass()
        mk.emit_l1_rows()
        mk.emit_l1_attn_loop()
    mk.emit_wout(zsrc=mk.yn_scr, w_key="wout1", nw_key="nw1",
                 res_buf=mk.inp["x1_in"], dest=mk.out["y_out"])
    mk.ctx.barrier_all()
    mk.ctx.finalize()
    return nc, mk


def host_l1_inputs(IN):
    f = np.float32
    shared = {
        "wqkv": tile_fm(np.asarray(IN["L1_attn_Wqkv"], f), 128),
        "wz": tile_fm(np.asarray(IN["L1_attn_Wz"], f), 128),
        "wab": np.concatenate([IN["L1_attn_Wa"], IN["L1_attn_Wb"]], axis=1).astype(f),
        "conv_w1": np.ascontiguousarray(IN["L1_attn_conv_w"]).astype(f),
        "alog": IN["L1_attn_A_log"].reshape(-1, 1).astype(f),
        "dtb1": IN["L1_attn_dt_bias"].reshape(-1, 1).astype(f),
        "nw1": np.ascontiguousarray(np.tile(IN["L1_attn_norm_w"], 32).reshape(32, 128).T).astype(f),
        "wout1": np.ascontiguousarray(IN["L1_attn_Wout"]).astype(f),
        "gwT": np.ascontiguousarray(IN["L1_ffn_gate_weight"].T).astype(f),
        "ebias": IN["L1_ffn_e_score_correction_bias"].reshape(1, -1).astype(f),
        "gate_wT": np.concatenate(
            [tile_fm(np.asarray(IN["L1_ffn_gate_w"][n], f).T, 512) for n in range(NE)]),
        "up_wT": np.concatenate(
            [tile_fm(np.asarray(IN["L1_ffn_up_w"][n], f).T, 512) for n in range(NE)]),
        "down_wT": np.concatenate(
            [tile_fm(np.asarray(IN["L1_ffn_down_w"][n], f).T, 512) for n in range(NE)]),
        "shgT": tile_fm(np.asarray(IN["L1_ffn_sh_gate_w"], f).T, 512),
        "shuT": tile_fm(np.asarray(IN["L1_ffn_sh_up_w"], f).T, 512),
        "shdT": tile_fm(np.asarray(IN["L1_ffn_sh_down_w"], f).T, 512),
    }
    return shared


def _moe_methods():
    def emit_moe_routing(self, xn_fm, xn3, sb, th):
        c = self.ctx
        gw = sb["gw_sb"]
        eb = sb["ebias_b"]
        for tt in range(8):
            gt = th * 8 + tt
            pa = self.pb[tt % 2]
            self.mm_acc(pa, pa.ap()[:, 0:16], [
                (xn3[:, dc, ts(tt, 128)], gw.ap()[:, dc * 16:(dc + 1) * 16],
                 (xn_fm, gw)) for dc in range(DC)])
            r = sb["rt"]
            self.act(lambda e, pa=pa: e.activation(r.ap()[:, 0:16], pa.ap()[:, 0:16],
                                                   AF.Sigmoid),
                     reads=(pa,), writes=(r,))
            self.dve(lambda e: e.tensor_add(r.ap()[:, 16:32], r.ap()[:, 0:16],
                                            eb.ap()[:, :]),
                     reads=(r, eb), writes=(r,))
            for gg in range(4):
                gsl = r.ap()[:, 16 + gg * 4:16 + (gg + 1) * 4]
                self.dve(lambda e, gsl=gsl, gg=gg: e.reduce_max(
                    r.ap()[:, 32 + gg:33 + gg], gsl, axis=AX.X),
                    reads=(r,), writes=(r,))
                self.dve(lambda e, gsl=gsl, gg=gg: e.tensor_single_scalar(
                    r.ap()[:, 48 + gg * 4:48 + (gg + 1) * 4], gsl,
                    r.ap()[:, 32 + gg:33 + gg], op=OP.is_ge),
                    reads=(r,), writes=(r,))
                self.dve(lambda e, gsl=gsl, gg=gg: e.scalar_tensor_tensor(
                    r.ap()[:, 48 + gg * 4:48 + (gg + 1) * 4],
                    r.ap()[:, 48 + gg * 4:48 + (gg + 1) * 4], NEG, gsl,
                    op0=OP.mult, op1=OP.add),
                    reads=(r,), writes=(r,))
                self.dve(lambda e, gg=gg: e.reduce_max(
                    r.ap()[:, 36 + gg:37 + gg],
                    r.ap()[:, 48 + gg * 4:48 + (gg + 1) * 4], axis=AX.X),
                    reads=(r,), writes=(r,))
            self.dve(lambda e: e.tensor_add(r.ap()[:, 40:44], r.ap()[:, 32:36],
                                            r.ap()[:, 36:40]),
                     reads=(r,), writes=(r,))
            self.dve(lambda e: e.reduce_max(r.ap()[:, 44:45], r.ap()[:, 40:44],
                                            axis=AX.X),
                     reads=(r,), writes=(r,))
            self.dve(lambda e: e.tensor_single_scalar(r.ap()[:, 48:52], r.ap()[:, 40:44],
                                                      r.ap()[:, 44:45], op=OP.is_ge),
                     reads=(r,), writes=(r,))
            self.dve(lambda e: e.scalar_tensor_tensor(r.ap()[:, 48:52], r.ap()[:, 48:52],
                                                      NEG, r.ap()[:, 40:44],
                                                      op0=OP.mult, op1=OP.add),
                     reads=(r,), writes=(r,))
            self.dve(lambda e: e.reduce_max(r.ap()[:, 44:45], r.ap()[:, 48:52],
                                            axis=AX.X),
                     reads=(r,), writes=(r,))
            self.dve(lambda e: e.tensor_single_scalar(r.ap()[:, 45:49], r.ap()[:, 40:44],
                                                      r.ap()[:, 44:45], op=OP.is_ge),
                     reads=(r,), writes=(r,))
            self.dve(lambda e: e.tensor_scalar(r.ap()[:, 52:56], r.ap()[:, 45:49],
                                               -NEG, NEG, op0=OP.mult, op1=OP.add),
                     reads=(r,), writes=(r,))
            self.dve(lambda e: e.tensor_add(
                r.ap()[:, 64:80].rearrange("p (g e) -> p g e", g=4),
                r.ap()[:, 16:32].rearrange("p (g e) -> p g e", g=4),
                ap_append0(r.ap()[:, 52:56], 4)),
                reads=(r,), writes=(r,))
            self.dve(lambda e: e.tensor_copy(r.ap()[:, 80:96], r.ap()[:, 64:80]),
                     reads=(r,), writes=(r,))
            for _ in range(3):
                self.dve(lambda e: e.reduce_max(r.ap()[:, 44:45], r.ap()[:, 64:80],
                                                axis=AX.X),
                         reads=(r,), writes=(r,))
                self.dve(lambda e: e.tensor_single_scalar(
                    r.ap()[:, 96:112], r.ap()[:, 64:80], r.ap()[:, 44:45], op=OP.is_ge),
                    reads=(r,), writes=(r,))
                self.dve(lambda e: e.scalar_tensor_tensor(
                    r.ap()[:, 64:80], r.ap()[:, 96:112], NEG, r.ap()[:, 64:80],
                    op0=OP.mult, op1=OP.add),
                    reads=(r,), writes=(r,))
            self.dve(lambda e: e.reduce_max(r.ap()[:, 44:45], r.ap()[:, 64:80],
                                            axis=AX.X),
                     reads=(r,), writes=(r,))
            self.dve(lambda e: e.tensor_single_scalar(r.ap()[:, 96:112], r.ap()[:, 80:96],
                                                      r.ap()[:, 44:45], op=OP.is_ge),
                     reads=(r,), writes=(r,))
            self.dve(lambda e: e.tensor_mul(r.ap()[:, 96:112], r.ap()[:, 96:112],
                                            r.ap()[:, 0:16]),
                     reads=(r,), writes=(r,))
            self.dve(lambda e: e.reduce_sum(r.ap()[:, 44:45], r.ap()[:, 96:112],
                                            axis=AX.X),
                     reads=(r,), writes=(r,))
            self.dve(lambda e: e.tensor_scalar_add(r.ap()[:, 44:45], r.ap()[:, 44:45],
                                                   1e-20),
                     reads=(r,), writes=(r,))
            self.dve(lambda e: e.reciprocal(r.ap()[:, 45:46], r.ap()[:, 44:45]),
                     reads=(r,), writes=(r,))
            self.dve(lambda e: e.tensor_scalar(r.ap()[:, 96:112], r.ap()[:, 96:112],
                                               r.ap()[:, 45:46], 2.5,
                                               op0=OP.mult, op1=OP.mult),
                     reads=(r,), writes=(r,))
            pt = self.pb[4]
            self.transpose(pt, pt.ap()[0:16, 0:128], r.ap()[:, 96:112],
                           self.identf.ap()[:, :], reads=(r, self.identf))
            self.dve(lambda e, pt=pt: e.tensor_copy(sb["wrow"].ap()[:, :],
                                                    pt.ap()[0:16, 0:128]),
                     reads=(pt,), writes=(sb["wrow"],))
            c.dma("qact", self.w_scr.ap()[:, ts(gt, 128)], sb["wrow"].ap()[:, :],
                  reads=(sb["wrow"],), writes=(self.w_scr,))

    def emit_moe(self, x2a_buf, dest):
        c = self.ctx
        self.w_scr = self.dscr("w_scr", [16, T], F32)
        with c.scope():
            sb = {}
            sb["gw_sb"] = c.sbuf("gw_sb", [128, DC * 16], BF16)
            sb["ebias_b"] = c.sbuf("ebias_b", [128, 16], F32)
            sb["rt"] = c.sbuf("rt", [128, 112], F32)
            sb["wrow"] = c.sbuf("wrow", [16, 128], F32)
            src = self.inp["gwT"].ap()[:, :].rearrange("(a p) c -> p a c", p=128)
            c.dma("qpool", sb["gw_sb"].ap()[:, :].rearrange("p (a c) -> p a c", a=DC),
                  src, reads=(self.inp["gwT"],), writes=(sb["gw_sb"],))
            ea = self.inp["ebias"].ap()
            c.dma("qsp", sb["ebias_b"].ap()[:, :],
                  bass.AP(ea.tensor, 0, [[0, 128], [1, 16]]),
                  reads=(self.inp["ebias"],), writes=(sb["ebias_b"],))
            xn_th = c.sbuf("xn_moe", [128, DC * 1024], BF16)
            acc = c.sbuf("acc_moe", [128, 8 * D], F32)
            wgu = [c.sbuf(f"wgu{i}", [128, DC * 512], BF16) for i in range(3)]
            wdn = [c.sbuf(f"wdn{i}", [128, 8 * 512], BF16) for i in range(2)]
            actn = c.sbuf("actn", [128, 8 * 1024], BF16)
            sgt = c.sbuf("sgt", [128, 512], F32)
            wut = c.sbuf("wut", [128, 512], F32)
            wbt = [c.sbuf(f"wbt{i}", [128, 1024], F32) for i in range(2)]
            outt = [c.sbuf(f"outt_m{i}", [128, 512], F32) for i in range(2)]
            x2t = [c.sbuf(f"x2t{i}", [128, 512], F32) for i in range(2)]
            ring = [0]

            def slab():
                r = wgu[ring[0] % 3]
                ring[0] += 1
                return r
            for th in range(2):
                xa = self.xn1_scr.ap()[:, :].rearrange("p (dc t) -> p dc t", dc=DC)
                c.dma("qsp", xn_th.ap()[:, :].rearrange("p (dc t) -> p dc t", dc=DC),
                      xa[:, :, th * 1024:(th + 1) * 1024],
                      reads=(self.xn1_scr,), writes=(xn_th,))
                xn3 = xn_th.ap()[:, :].rearrange("p (dc t) -> p dc t", dc=DC)
                self.emit_moe_routing(xn_th, xn3, sb, th)
                for n in range(NE):
                    wb = wbt[n % 2]
                    wa = self.w_scr.ap()
                    c.dma("qsp", wb.ap()[:, :],
                          bass.AP(wa.tensor, n * T + th * 1024, [[0, 128], [1, 1024]]),
                          reads=(self.w_scr,), writes=(wb,))
                    for fh2 in range(2):
                        pair = []
                        for wn in ("gate_wT", "up_wT"):
                            w = slab()
                            c.dma("qpool", w.ap()[:, :],
                                  self.inp[wn].ap()[ts(n * 2 + fh2, 128), :],
                                  reads=(self.inp[wn],), writes=(w,))
                            pair.append(w)
                        wg, wu = pair
                        for fcl in range(4):
                            fc = fh2 * 4 + fcl
                            w0v = wg.ap()[:, :].rearrange("p (a c) -> p a c", a=DC)
                            w1v = wu.ap()[:, :].rearrange("p (a c) -> p a c", a=DC)
                            self.mm_kouter(
                                [(self.pb[tgg], self.pb[tgg].ap()[:, :])
                                 for tgg in range(2)],
                                [(w0v[:, dc, ts(fcl, 128)],
                                  [xn3[:, dc, ts(tgg, 512)] for tgg in range(2)],
                                  (wg, xn_th)) for dc in range(DC)])
                            self.mm_kouter(
                                [(self.pb[2 + tgg], self.pb[2 + tgg].ap()[:, :])
                                 for tgg in range(2)],
                                [(w1v[:, dc, ts(fcl, 128)],
                                  [xn3[:, dc, ts(tgg, 512)] for tgg in range(2)],
                                  (wu, xn_th)) for dc in range(DC)])
                            for tgg in range(2):
                                pg = self.pb[tgg]
                                pu = self.pb[2 + tgg]
                                self.act(lambda e, pg=pg: e.activation(
                                    sgt.ap()[:, :], pg.ap()[:, :], AF.Silu),
                                    reads=(pg,), writes=(sgt,))
                                self.dve(lambda e, pu=pu, tgg=tgg, wb=wb: e.tensor_mul(
                                    wut.ap()[:, :], pu.ap()[:, :],
                                    wb.ap()[:, ts(tgg, 512)]),
                                    reads=(pu, wb), writes=(wut,))
                                self.dve(lambda e, fc=fc, tgg=tgg: e.tensor_mul(
                                    actn.ap()[:, fc * 1024 + tgg * 512:
                                              fc * 1024 + (tgg + 1) * 512],
                                    sgt.ap()[:, :], wut.ap()[:, :]),
                                    reads=(sgt, wut), writes=(actn,))
                    for ds in range(4):
                        wd = wdn[ds % 2]
                        c.dma("qpool", wd.ap()[:, :],
                              self.inp["down_wT"].ap()[ts(n * 4 + ds, 128), :],
                              reads=(self.inp["down_wT"],), writes=(wd,))
                        for tt in range(8):
                            pa = self.pb[4 + tt % 2]
                            self.mm_acc(pa, pa.ap()[:, :], [
                                (actn.ap()[:, fcc * 1024:(fcc + 1) * 1024]
                                 .rearrange("p (g t) -> p g t", g=2)
                                 [:, tt // 4, ts(tt % 4, 128)],
                                 wd.ap()[:, ts(fcc, 512)], (actn, wd))
                                for fcc in range(8)])
                            asl = acc.ap()[:, :].rearrange("p (tt d) -> p tt d", tt=8)
                            if n == 0:
                                self.dve(lambda e, pa=pa, tt=tt, ds=ds, asl=asl:
                                         e.tensor_copy(asl[:, tt, ts(ds, 512)],
                                                       pa.ap()[:, :]),
                                         reads=(pa,), writes=(acc,))
                            else:
                                self.dve(lambda e, pa=pa, tt=tt, ds=ds, asl=asl:
                                         e.tensor_add(asl[:, tt, ts(ds, 512)],
                                                      asl[:, tt, ts(ds, 512)],
                                                      pa.ap()[:, :]),
                                         reads=(pa, acc), writes=(acc,))
                self._moe_shared(th, xn_th, xn3, acc, slab, wdn, sgt, actn, c)
                for tt in range(8):
                    gt = th * 8 + tt
                    for ds in range(4):
                        asl = acc.ap()[:, :].rearrange("p (tt d) -> p tt d", tt=8)
                        xt = x2t[ds % 2]
                        ot = outt[ds % 2]
                        c.dma("qsp", xt.ap()[:, :],
                              x2a_buf.ap()[ts(gt, 128), ts(ds, 512)],
                              reads=(x2a_buf,), writes=(xt,))
                        self.dve(lambda e, tt=tt, ds=ds, asl=asl, xt=xt, ot=ot:
                                 e.tensor_add(ot.ap()[:, :], asl[:, tt, ts(ds, 512)],
                                              xt.ap()[:, :]),
                                 reads=(acc, xt), writes=(ot,))
                        c.dma("qact", dest.ap()[ts(gt, 128), ts(ds, 512)],
                              ot.ap()[:, :], reads=(ot,), writes=(dest,))

    def _moe_shared(self, th, xn_fm, xn3, acc, slab, wdn, sgt, actn, c):
        for fh in range(2):
            for fh2 in range(2):
                pair = []
                for wn in ("shgT", "shuT"):
                    w = slab()
                    c.dma("qpool", w.ap()[:, :],
                          self.inp[wn].ap()[ts(fh * 2 + fh2, 128), :],
                          reads=(self.inp[wn],), writes=(w,))
                    pair.append(w)
                wg, wu = pair
                for fcl in range(4):
                    fc = fh2 * 4 + fcl
                    w0v = wg.ap()[:, :].rearrange("p (a c) -> p a c", a=DC)
                    w1v = wu.ap()[:, :].rearrange("p (a c) -> p a c", a=DC)
                    self.mm_kouter(
                        [(self.pb[tgg], self.pb[tgg].ap()[:, :]) for tgg in range(2)],
                        [(w0v[:, dc, ts(fcl, 128)],
                          [xn3[:, dc, ts(tgg, 512)] for tgg in range(2)],
                          (wg, xn_fm)) for dc in range(DC)])
                    self.mm_kouter(
                        [(self.pb[2 + tgg], self.pb[2 + tgg].ap()[:, :])
                         for tgg in range(2)],
                        [(w1v[:, dc, ts(fcl, 128)],
                          [xn3[:, dc, ts(tgg, 512)] for tgg in range(2)],
                          (wu, xn_fm)) for dc in range(DC)])
                    for tgg in range(2):
                        pg = self.pb[tgg]
                        pu = self.pb[2 + tgg]
                        self.act(lambda e, pg=pg: e.activation(sgt.ap()[:, :],
                                                               pg.ap()[:, :], AF.Silu),
                                 reads=(pg,), writes=(sgt,))
                        self.dve(lambda e, pu=pu, fc=fc, tgg=tgg: e.tensor_mul(
                            actn.ap()[:, fc * 1024 + tgg * 512:
                                      fc * 1024 + (tgg + 1) * 512],
                            sgt.ap()[:, :], pu.ap()[:, :]),
                            reads=(sgt, pu), writes=(actn,))
            for ds in range(4):
                wd = wdn[ds % 2]
                c.dma("qpool", wd.ap()[:, :],
                      self.inp["shdT"].ap()[ts(ds, 128), ts(fh, 4096)],
                      reads=(self.inp["shdT"],), writes=(wd,))
                for tt in range(8):
                    pa = self.pb[4 + tt % 2]
                    self.mm_acc(pa, pa.ap()[:, :], [
                        (actn.ap()[:, fcc * 1024:(fcc + 1) * 1024]
                         .rearrange("p (g t) -> p g t", g=2)
                         [:, tt // 4, ts(tt % 4, 128)],
                         wd.ap()[:, ts(fcc, 512)], (actn, wd))
                        for fcc in range(8)])
                    asl = acc.ap()[:, :].rearrange("p (tt d) -> p tt d", tt=8)
                    self.dve(lambda e, pa=pa, tt=tt, ds=ds, asl=asl: e.tensor_add(
                        asl[:, tt, ts(ds, 512)], asl[:, tt, ts(ds, 512)], pa.ap()[:, :]),
                        reads=(pa, acc), writes=(acc,))

    return emit_moe_routing, emit_moe, _moe_shared


MK.emit_moe_routing, MK.emit_moe, MK._moe_shared = _moe_methods()


def build_l1_program(dbg=()):
    nc = bass.Bass("TRN2", target_bir_lowering=False)
    mk = MK(nc, dbg=dbg)
    mk.ctx.dram("x1_in", [T, D], F32, kind="ExternalInput")
    mk.inp["x1_in"] = mk.ctx.bufs[-1]
    mk.din("maskcol", [128, 1])
    mk.din("ident_bf", [128, 128], BF16)
    mk.din("ident_f32", [128, 128])
    mk.din("masks3", [128, 512])
    mk.din("ones_row", [1, 2048])
    mk.declare_l1_io()
    mk.dout("y_out", [T, D])
    mk.ctx.alloc()
    mk.alloc_persist()
    mk.load_consts()
    with mk.ctx.scope():
        mk.alloc_l1_persist()
        mk.emit_l1_front(mk.inp["x1_in"])
        mk.emit_l1_prepass()
        mk.emit_l1_rows()
        mk.emit_l1_attn_loop()
    mk.emit_wout(zsrc=mk.yn_scr, w_key="wout1", nw_key="nw1",
                 res_buf=mk.inp["x1_in"], dest=mk.x2a_scr)
    mk.emit_moe(mk.x2a_scr, mk.out["y_out"])
    mk.ctx.barrier_all()
    mk.ctx.finalize()
    return nc, mk


_CACHE = {}


def _get_programs():
    if "l0" not in _CACHE:
        _CACHE["l0"] = build_l0_program()
    if "l1" not in _CACHE:
        _CACHE["l1"] = build_l1_program()
    return _CACHE["l0"][0], _CACHE["l1"][0]


def kernel_two_launch(**inputs):
    IN = inputs
    f = np.float32
    nc0, nc1 = _get_programs()
    sh0 = host_l0_inputs(IN)
    maps0 = [core_l0_inputs(IN, sh0, c) for c in range(8)]
    res0 = run_bass_kernel_spmd(nc0, maps0, core_ids=list(range(8)))
    x1s = [res0.results[c]["y_out"] for c in range(8)]
    sh1 = host_l1_inputs(IN)
    maps1 = []
    for c in range(8):
        m = dict(sh1)
        for k in ("ident_bf", "ident_f32", "masks3", "ones_row"):
            m[k] = sh0[k]
        m["maskcol"] = np.ones((128, 1), f)
        m["x1_in"] = np.ascontiguousarray(x1s[c], dtype=f)
        maps1.append(m)
    res1 = run_bass_kernel_spmd(nc1, maps1, core_ids=list(range(8)))
    out = np.zeros((4, 4096, 2048), f)
    for c in range(8):
        out[c // 2, (c % 2) * T:((c % 2) + 1) * T] = res1.results[c]["y_out"]
    return out


def build_full_program(dbg=()):
    nc = bass.Bass("TRN2", target_bir_lowering=False)
    mk = MK(nc, dbg=dbg)
    mk.declare_io()
    mk.declare_l1_io()
    x1_full = mk.dscr("x1_full", [T, D], F32)
    mk.ctx.alloc()
    mk.alloc_persist()
    mk.load_consts()
    with mk.ctx.scope():
        mk.alloc_l0_wide()
        mk.emit_l0_attn()
    mk.emit_ffn(x1_full)
    with mk.ctx.scope():
        mk.alloc_l1_persist()
        mk.emit_l1_front(x1_full)
        mk.emit_l1_prepass()
        mk.emit_l1_rows()
        mk.emit_l1_attn_loop()
    mk.emit_wout(zsrc=mk.yn_scr, w_key="wout1", nw_key="nw1",
                 res_buf=x1_full, dest=mk.x2a_scr)
    mk.emit_moe(mk.x2a_scr, mk.out["y_out"])
    mk.ctx.barrier_all()
    mk.ctx.finalize()
    return nc, mk


def kernel_fused(**inputs):
    IN = inputs
    f = np.float32
    cu = np.asarray(IN["L1_attn_cu_seqlens"]).ravel()
    seg = int(cu[1]) - int(cu[0])
    assert seg == T and all(int(cu[i + 1]) - int(cu[i]) == seg
                            for i in range(len(cu) - 1)), "unexpected cu_seqlens"
    if "full" not in _CACHE:
        _CACHE["full"] = build_full_program()
    nc = _CACHE["full"][0]
    sh0 = host_l0_inputs(IN)
    sh1 = host_l1_inputs(IN)
    maps = []
    for c in range(8):
        m = core_l0_inputs(IN, sh0, c)
        m.update(sh1)
        maps.append(m)
    res = run_bass_kernel_spmd(nc, maps, core_ids=list(range(8)))
    out = np.zeros((4, 4096, 2048), f)
    for c in range(8):
        out[c // 2, (c % 2) * T:((c % 2) + 1) * T] = res.results[c]["y_out"]
    return out


def kernel(**inputs):
    return kernel_fused(**inputs)


def _opt_methods():
    def mm_kouter(self, banks, kterms):
        nk = len(kterms)
        nb = len(banks)
        all_reads = []
        for ki, (l, rhss, rb) in enumerate(kterms):
            for b in rb:
                if b not in all_reads:
                    all_reads.append(b)
            for bi in range(nb):
                pbuf, oap = banks[bi]
                tok = self.ctx.op(
                    "pe",
                    lambda eng, l=l, r=rhss[bi], o=oap, ki=ki, nk=nk: eng.matmul(
                        o, l, r, start=(ki == 0), stop=(ki == nk - 1)),
                    reads=rb, writes=tuple(p for p, _ in banks) if ki == 0 and bi == 0 else (),
                    inc=(ki == nk - 1 and bi == nb - 1), commit=False)
        for pbuf, _ in banks:
            self.ctx.commit(tok, reads=all_reads, writes=(pbuf,))
        return tok
    return (mm_kouter,)


(MK.mm_kouter,) = _opt_methods()
